# version 01
# baseline (speedup 1.0000x reference)
import numpy as np
from contextlib import ExitStack
import concourse.bass as bass
import concourse.mybir as mybir
from concourse.bass_utils import run_bass_kernel_spmd

F32 = mybir.dt.float32
BF16 = mybir.dt.bfloat16
AF = mybir.ActivationFunctionType
ALU = mybir.AluOpType
AX = mybir.AxisListType

D, S = 1024, 2048
NT = D // 128
H, HD = 16, 64
NST = S // 128
CH = 256
NCH = S // CH
STATE, CHUNK = 64, 64
GRP = 512
NGRP = S // GRP
MG = GRP // CHUNK
NEXP, MLP = 8, 4096
NMT = MLP // 128
JB = 8
NJB = NMT // JB
TC = 256
NTC = S // TC
EPS0, EPS1 = 1e-5, 1e-6
SMSHIFT = -12.0
VROW = H * 68


class Sched:
    FIFO_SEM = {"t": "pe", "v": "sv", "s": "ss"}

    def __init__(self):
        self.ops = []
        self.count = {}
        self.waited = {}

    def n(self, sem):
        return self.count.get(sem, 0)

    def op(self, eng, fn, wait=(), inc=()):
        waits = []
        for w in wait:
            waits.append(w if isinstance(w, tuple) else (w, self.count.get(w, 0)))
        incs = []
        for s_, amt in inc:
            self.count[s_] = self.count.get(s_, 0) + amt
            incs.append((s_, amt))
        self.ops.append((eng, fn, waits, incs))

    def barrier(self, eng, sems):
        self.op(eng, None, wait=sems)

    def flush(self, blk, semh):
        ops, self.ops = self.ops, []
        for name in ("sy", "g", "t", "v", "s"):
            mine = [o for o in ops if o[0] == name]
            if not mine:
                continue
            waited = self.waited.setdefault(name, {})

            def body(eng, mine=mine, waited=waited):
                for _, fn, waits, incs in mine:
                    for s_, v_ in waits:
                        if v_ <= 0 or waited.get(s_, -1) >= v_:
                            continue
                        eng.wait_ge(semh[s_], v_)
                        waited[s_] = v_
                    if fn is None:
                        continue
                    instr = fn(eng)
                    for s_, amt in incs:
                        instr.then_inc(semh[s_], amt)

            getattr(blk, {"sy": "sync", "g": "gpsimd", "t": "tensor",
                          "v": "vector", "s": "scalar"}[name])(body)


def build(taps=(), phases=99, racecheck=False):
    taps = set(taps)
    nc = bass.Bass("TRN2", target_bir_lowering=False,
                   detect_race_conditions=racecheck)

    h_in = {}
    def din(name, shape, dt=F32):
        h_in[name] = nc.dram_tensor(name, list(shape), dt, kind="ExternalInput").ap()
        return h_in[name]

    x_h = din("x", [S, D])
    wqT_h = din("wqT", [D, D]); wkT_h = din("wkT", [D, D])
    wvT_h = din("wvT", [D, D]); woT_h = din("woT", [D, D])
    bq_h = din("bq", [128, NT]); bk_h = din("bk", [128, NT])
    bv_h = din("bv", [128, NT]); bo_h = din("bo", [128, NT])
    wrT_h = din("wrT", [D, D]); wkkT_h = din("wkkT", [D, D]); wvvT_h = din("wvvT", [D, D])
    mur_h = din("mur", [128, NT]); muk_h = din("muk", [128, NT])
    normw_h = din("normw", [128, NT])
    xprojT_h = din("xprojT", [D, 192])
    dtprojT_h = din("dtprojT", [64, D])
    dtprojb_h = din("dtprojb", [128, NT])
    alog_h = din("alog", [128, NT * STATE])
    dvec_h = din("dvec", [128, NT])
    outprojT_h = din("outprojT", [D, D])
    wgate_h = din("wgate", [128, NT * NEXP])
    bgate_h = din("bgate", [1, NEXP])
    w1_h = din("w1", [NEXP, D, MLP])
    b1_h = din("b1", [128, NEXP * NMT])
    w2_h = din("w2", [NEXP, MLP, D])
    b2_h = din("b2", [NEXP, D])

    y_h = nc.dram_tensor("y", [S, D], F32, kind="ExternalOutput").ap()
    tap_h = {}
    def TAP(name, shape, dt=F32):
        if name in taps and name not in tap_h:
            tap_h[name] = nc.dram_tensor("tap_" + name, list(shape), dt,
                                         kind="ExternalOutput").ap()
        return tap_h.get(name)

    sch = Sched()
    with ExitStack() as CTX:
        blk = CTX.enter_context(nc.Block())
        SEMH = {}
        for nm in ("ld", "ldx", "lset", "lffn", "lwv", "lqk", "lwo", "lop",
                   "lma", "lmb", "gc", "pe", "sv", "ss", "st", "sta", "stb"):
            SEMH[nm] = CTX.enter_context(nc.semaphore(nm))

        def sb(stack, name, shape, dt):
            return stack.enter_context(nc.sbuf_tensor(name, shape, dt))
        def ps(stack, name, shape, dt=F32):
            return stack.enter_context(nc.psum_tensor(name, shape, dt))
        def flush():
            sch.flush(blk, SEMH)

        xT = sb(CTX, "xT", [128, NT * S], F32)
        xnbf = sb(CTX, "xnbf", [128, NT * S], BF16)
        ones1 = sb(CTX, "ones1", [1, 128], F32)
        onesc = sb(CTX, "onesc", [128, 1], F32)
        ident = sb(CTX, "ident", [128, 128], F32)
        bq_s = sb(CTX, "bq_s", [128, NT], F32); bk_s = sb(CTX, "bk_s", [128, NT], F32)
        bv_s = sb(CTX, "bv_s", [128, NT], F32); bo_s = sb(CTX, "bo_s", [128, NT], F32)
        bv_bf = sb(CTX, "bv_bf", [128, NT], BF16)
        bo2_s = sb(CTX, "bo2_s", [128, NT], F32)
        mur_s = sb(CTX, "mur_s", [128, NT], F32); muk_s = sb(CTX, "muk_s", [128, NT], F32)
        murc_s = sb(CTX, "murc_s", [128, NT], F32); mukc_s = sb(CTX, "mukc_s", [128, NT], F32)
        normw_s = sb(CTX, "normw_s", [128, NT], F32)
        dtprojb_s = sb(CTX, "dtprojb_s", [128, NT], F32)
        aneg_s = sb(CTX, "aneg_s", [128, NT * STATE], F32)
        dvec_s = sb(CTX, "dvec_s", [128, NT], F32)
        wgate_s = sb(CTX, "wgate_s", [128, NT * NEXP], F32)
        bgate_s = sb(CTX, "bgate_s", [1, NEXP], F32)
        bgb_s = sb(CTX, "bgb_s", [128, NEXP], F32)
        b1_s = sb(CTX, "b1_s", [128, NEXP * NMT], F32)
        b2_s = sb(CTX, "b2_s", [NEXP, D], BF16)
        xprojT_s = sb(CTX, "xprojT_s", [128, NT * 192], BF16)
        dtprojT_s = sb(CTX, "dtprojT_s", [64, D], BF16)
        ones8 = sb(CTX, "ones8", [128, NEXP], F32)
        prevcol = sb(CTX, "prevcol", [128, NT], F32)
        CONSTS = [float(EPS0), float(EPS1), 1.0, float(SMSHIFT)]
        cst = sb(CTX, "cst", [128, len(CONSTS)], F32)
        for ci, cv in enumerate(CONSTS):
            nc.const_aps.register(F32, cv, cst[:, ci:ci + 1])

        su = ExitStack()
        x_nat = sb(su, "x_nat", [128, (NST // 2) * D], F32)
        alog_s = sb(su, "alog_s", [128, NT * STATE], F32)
        psT = [ps(su, "psT0", [128, 512]), ps(su, "psT1", [128, 512])]
        psBG = ps(su, "psBG", [128, NEXP])

        for dst, src in [(bq_s, bq_h), (bk_s, bk_h), (bv_s, bv_h), (bo_s, bo_h),
                         (mur_s, mur_h), (muk_s, muk_h), (normw_s, normw_h),
                         (dtprojb_s, dtprojb_h), (alog_s, alog_h), (dvec_s, dvec_h),
                         (wgate_s, wgate_h), (bgate_s, bgate_h), (b1_s, b1_h)]:
            sch.op("sy", lambda sy, d=dst, s_=src: sy.dma_start(out=d[:], in_=s_[:]),
                   inc=[("ld", 16)])
        LD_SM = sch.n("ld")

        sch.op("g", lambda g: g.memset(ident[:], 1.0), inc=[("gc", 1)])
        sch.op("g", lambda g: g.affine_select(
            out=ident[:], in_=ident[:], pattern=[[1, 128]], compare_op=ALU.is_equal,
            fill=0.0, base=0, channel_multiplier=-1), wait=["gc"], inc=[("gc", 1)])
        GC_ID = sch.n("gc")
        sch.op("g", lambda g: g.dma_start(out=b2_s[:], in_=b2_h[:]), inc=[("lset", 16)])
        sch.op("g", lambda g: g.dma_start(
            out=xprojT_s[:].rearrange("p (dt r) -> p dt r", dt=NT),
            in_=xprojT_h[:].rearrange("(dt p) r -> p dt r", p=128)),
            inc=[("lset", 16)])
        sch.op("g", lambda g: g.dma_start(out=dtprojT_s[:], in_=dtprojT_h[:]),
               inc=[("lset", 16)])
        LDG_SET = sch.n("lset")

        sch.op("v", lambda v: v.memset(ones1[:], 1.0))
        sch.op("v", lambda v: v.memset(onesc[:], 1.0))
        sch.op("v", lambda v: v.memset(ones8[:], 1.0))
        for ci, cv in enumerate(CONSTS):
            sch.op("v", lambda v, ci=ci, cv=cv: v.memset(cst[:, ci:ci + 1], cv))
        sch.op("v", lambda v: v.memset(prevcol[:], 0.0), inc=[("sv", 1)])
        SV_ONES = sch.n("sv")
        sch.barrier("v", [("ld", LD_SM)])
        sch.op("v", lambda v: v.tensor_scalar(
            out=murc_s[:], in0=mur_s[:], scalar1=-1.0, scalar2=1.0,
            op0=ALU.mult, op1=ALU.add))
        sch.op("v", lambda v: v.tensor_scalar(
            out=mukc_s[:], in0=muk_s[:], scalar1=-1.0, scalar2=1.0,
            op0=ALU.mult, op1=ALU.add), inc=[("sv", 1)])
        sch.op("v", lambda v: v.tensor_copy(bv_bf[:], bv_s[:]),
               wait=[("ld", LD_SM)], inc=[("sv", 1)])
        sch.op("s", lambda s: s.activation(out=aneg_s[:], in_=alog_s[:], func=AF.Exp),
               wait=[("ld", LD_SM)], inc=[("ss", 1)])
        SS_ANEG = sch.n("ss")
        sch.op("v", lambda v: v.tensor_scalar_mul(aneg_s[:], aneg_s[:], -1.0),
               wait=[("ss", SS_ANEG)], inc=[("sv", 1)])
        sch.op("t", lambda t: t.matmul(psBG[:], ones1[:], bgate_s[:], start=True,
                                       stop=True),
               wait=[("sv", SV_ONES), ("ld", LD_SM)], inc=[("pe", 1)])
        sch.op("v", lambda v: v.tensor_copy(bgb_s[:], psBG[:]), wait=["pe"],
               inc=[("sv", 1)])

        SV_T0 = sch.n("sv")
        gi = 0
        for half in range(2):
            for i in range(NST // 2):
                sti = half * 8 + i
                sch.op("sy", lambda sy, i=i, sti=sti: sy.dma_start(
                    out=x_nat[:, i * D:(i + 1) * D],
                    in_=x_h[sti * 128:(sti + 1) * 128, :]), inc=[("ldx", 16)])
            LD_H = sch.n("ldx")
            for dt_j in range(NT):
                for sg in range(2):
                    bank = gi % 2
                    for q in range(4):
                        i = sg * 4 + q
                        w = [("gc", GC_ID), ("ldx", LD_H)]
                        if gi >= 2 and q == 0:
                            w.append(("sv", SV_T0 + gi - 1))
                        sch.op("t", lambda t, b=bank, q=q, i=i, dj=dt_j: t.transpose(
                            psT[b][:, q * 128:(q + 1) * 128],
                            x_nat[:, i * D + dj * 128: i * D + dj * 128 + 128],
                            ident[:]),
                            wait=w, inc=([("pe", 1)] if q == 3 else []))
                    so = half * 1024 + sg * 512
                    sch.op("v", lambda v, b=bank, dj=dt_j, so=so: v.tensor_copy(
                        xT[:, dj * S + so: dj * S + so + 512], psT[b][:]),
                        wait=["pe"], inc=[("sv", 1)])
                    gi += 1
            if half == 0:
                sch.barrier("sy", [("pe", sch.n("pe"))])
        flush()
        su.close()

        def emit_output():
            ou = ExitStack()
            ynat = [sb(ou, "ynat0", [128, D], F32), sb(ou, "ynat1", [128, D], F32)]
            psY = [ps(ou, "psY0", [128, 512]), ps(ou, "psY1", [128, 512]),
                   ps(ou, "psY2", [128, 512]), ps(ou, "psY3", [128, 512])]
            SV_ALL = sch.n("sv")
            SS_ALL = sch.n("ss")
            for sti in range(NST):
                yb = sti % 2
                for dt in range(NT):
                    bank = yb * 2 + dt // 4
                    w = []
                    if sti == 0 and dt == 0:
                        w = [("sv", SV_ALL), ("ss", SS_ALL)]
                    if sti >= 2 and dt == 0:
                        w = [("sta" if sti % 2 == 0 else "stb",
                              sch.n("sta" if sti % 2 == 0 else "stb"))]
                    sch.op("t", lambda t, dt=dt, sti=sti, bank=bank: t.transpose(
                        psY[bank][:, (dt % 4) * 128:(dt % 4 + 1) * 128],
                        xT[:, dt * S + sti * 128: dt * S + sti * 128 + 128],
                        ident[:]),
                        wait=w, inc=([("pe", 1)] if dt == NT - 1 else []))
                sch.op("v", lambda v, yb=yb: v.tensor_copy(
                    ynat[yb][:, 0:512], psY[yb * 2][:]), wait=["pe"])
                sch.op("v", lambda v, yb=yb: v.tensor_copy(
                    ynat[yb][:, 512:1024], psY[yb * 2 + 1][:]), inc=[("sv", 1)])
                sch.op("sy", lambda sy, sti=sti, yb=yb: sy.dma_start(
                    out=y_h[sti * 128:(sti + 1) * 128, :], in_=ynat[yb][:]),
                    wait=[("sv", sch.n("sv"))],
                    inc=[("sta" if sti % 2 == 0 else "stb", 16)])
            sch.barrier("sy", [("sta", sch.n("sta")), ("stb", sch.n("stb")),
                               ("st", sch.n("st"))])
            flush()
            ou.close()

        if phases < 1:
            emit_output()
            return nc, h_in, tap_h

        l0 = ExitStack()
        mrow = sb(l0, "mrow", [1, GRP], F32)
        rrow = sb(l0, "rrow", [1, GRP], F32)
        rta = sb(l0, "rta", [1, GRP], F32)
        rtb = sb(l0, "rtb", [1, GRP], F32)
        wr_sb = sb(l0, "wr_sb", [128, NT * D], BF16)
        wkk_sb = sb(l0, "wkk_sb", [128, NT * D], BF16)
        wvv_sb = sb(l0, "wvv_sb", [128, NT * D], BF16)
        xnf = sb(l0, "xnf", [128, NT * CH], F32)
        xr_bf = sb(l0, "xr_bf", [128, NT * CH], BF16)
        xk_bf = sb(l0, "xk_bf", [128, NT * CH], BF16)
        ktmp = sb(l0, "ktmp", [128, CH], F32)
        kact = sb(l0, "kact", [128, NT * CH], BF16)
        rsig = sb(l0, "rsig", [128, CH], F32)
        mixt = sb(l0, "mixt", [128, CH], F32)
        xsq = sb(l0, "xsq", [128, CH], F32)
        psRow = ps(l0, "psRow", [1, 2 * CH])
        psBC = ps(l0, "psBC", [128, 2 * CH])
        psMM = [ps(l0, "psMM0", [128, CH]), ps(l0, "psMM1", [128, CH])]

        PE_SETUP = sch.n("pe")
        for wdst, wsrc in ((wr_sb, wrT_h), (wkk_sb, wkkT_h), (wvv_sb, wvvT_h)):
            sch.op("g", lambda g, d=wdst, s_=wsrc: g.dma_start(
                out=d[:].rearrange("p (dt e) -> p dt e", dt=NT),
                in_=s_[:].rearrange("(dt p) e -> p dt e", p=128)),
                wait=[("pe", PE_SETUP)], inc=[("lffn", 16)])
        LDG_FFN = sch.n("lffn")

        def ln_stats(W, xsrc, xoff, mdst, rdst, eps, prow_a, prow_b, sqbuf, rta, rtb):
            sch.barrier("t", [("sv", sch.n("sv")), ("ss", sch.n("ss"))])
            for dt in range(NT):
                sch.op("t", lambda t, dt=dt: t.matmul(
                    prow_a[0:1, 0:W], onesc[:], xsrc[:, xoff(dt): xoff(dt) + W],
                    start=(dt == 0), stop=(dt == NT - 1)),
                    inc=([("pe", 1)] if dt == NT - 1 else []))
            PE_SQ = sch.n("pe")
            for dt in range(NT):
                wsq = [("sv", sch.n("sv"))] if dt == 0 else \
                      [("pe", PE_SQ + dt)]
                sch.op("s", lambda s, dt=dt: s.activation(
                    out=sqbuf[:, 0:W], in_=xsrc[:, xoff(dt): xoff(dt) + W],
                    func=AF.Square), wait=wsq, inc=[("ss", 1)])
                sch.op("t", lambda t, dt=dt: t.matmul(
                    prow_b[0:1, 0:W], onesc[:], sqbuf[:, 0:W],
                    start=(dt == 0), stop=(dt == NT - 1)), wait=["ss"],
                    inc=[("pe", 1)])
            if mdst is not None:
                sch.op("v", lambda v: v.tensor_scalar_mul(
                    mdst[0:1, 0:W], prow_a[0:1, 0:W], 1.0 / D),
                    wait=["pe", ("ss", sch.n("ss"))])
                sch.op("v", lambda v: v.tensor_scalar_mul(
                    rta[0:1, 0:W], prow_b[0:1, 0:W], 1.0 / D))
                sch.op("v", lambda v: v.tensor_mul(
                    rtb[0:1, 0:W], mdst[0:1, 0:W], mdst[0:1, 0:W]))
                sch.op("v", lambda v: v.tensor_sub(
                    rta[0:1, 0:W], rta[0:1, 0:W], rtb[0:1, 0:W]), inc=[("sv", 1)])
            else:
                sch.op("v", lambda v: v.tensor_scalar_mul(
                    rta[0:1, 0:W], prow_b[0:1, 0:W], 1.0 / D),
                    wait=["pe", ("ss", sch.n("ss"))], inc=[("sv", 1)])
            sch.op("s", lambda s: s.activation(
                out=rtb[0:1, 0:W], in_=rta[0:1, 0:W], func=AF.Ln,
                bias=float(eps)), wait=["sv"])
            sch.op("s", lambda s, rdst=rdst: s.activation(
                out=rdst[0:1, 0:W], in_=rtb[0:1, 0:W], func=AF.Exp,
                scale=-0.5), inc=[("ss", 1)])

        for c in range(NCH):
            co = c * CH
            ln_stats(CH, xT, lambda dt, co=co: dt * S + co, mrow, rrow, EPS0,
                     psRow[0:1, 0:CH], psRow[0:1, CH:2 * CH], xsq, rta, rtb)
            SS_R = sch.n("ss")
            sch.op("t", lambda t: t.matmul(
                psBC[:, 0:CH], ones1[:], mrow[0:1, 0:CH], start=True, stop=True),
                wait=[("ss", SS_R), ("sv", sch.n("sv"))])
            sch.op("t", lambda t: t.matmul(
                psBC[:, CH:2 * CH], ones1[:], rrow[0:1, 0:CH], start=True,
                stop=True), inc=[("pe", 1)])
            PE_BC = sch.n("pe")
            for dt in range(NT):
                sch.op("v", lambda v, dt=dt, co=co: v.scalar_tensor_tensor(
                    out=xnf[:, dt * CH:(dt + 1) * CH],
                    in0=xT[:, dt * S + co: dt * S + co + CH], scalar=1.0,
                    in1=psBC[:, 0:CH], op0=ALU.mult, op1=ALU.subtract),
                    wait=([("pe", PE_BC)] if dt == 0 else ()))
                sch.op("v", lambda v, dt=dt: v.tensor_mul(
                    xnf[:, dt * CH:(dt + 1) * CH], xnf[:, dt * CH:(dt + 1) * CH],
                    psBC[:, CH:2 * CH]))
                sch.op("v", lambda v, dt=dt, co=co: v.tensor_copy(
                    xnbf[:, dt * S + co: dt * S + co + CH],
                    xnf[:, dt * CH:(dt + 1) * CH]), inc=[("sv", 1)])
            for (dst, mu, muc) in ((xr_bf, mur_s, murc_s), (xk_bf, muk_s, mukc_s)):
                for dt in range(NT):
                    sch.op("v", lambda v, dt=dt, mu=mu: v.tensor_scalar(
                        out=mixt[:, 0:CH], in0=xnf[:, dt * CH:(dt + 1) * CH],
                        scalar1=mu[:, dt:dt + 1], scalar2=None, op0=ALU.mult))
                    sch.op("v", lambda v, dt=dt, dst=dst, muc=muc:
                        v.scalar_tensor_tensor(
                            out=dst[:, dt * CH + 1:(dt + 1) * CH],
                            in0=xnf[:, dt * CH: dt * CH + CH - 1],
                            scalar=muc[:, dt:dt + 1], in1=mixt[:, 1:CH],
                            op0=ALU.mult, op1=ALU.add))
                    sch.op("v", lambda v, dt=dt, dst=dst, muc=muc:
                        v.scalar_tensor_tensor(
                            out=dst[:, dt * CH: dt * CH + 1],
                            in0=prevcol[:, dt:dt + 1], scalar=muc[:, dt:dt + 1],
                            in1=mixt[:, 0:1], op0=ALU.mult, op1=ALU.add))
            for dt in range(NT):
                sch.op("v", lambda v, dt=dt: v.tensor_copy(
                    prevcol[:, dt:dt + 1], xnf[:, dt * CH + CH - 1: (dt + 1) * CH]),
                    inc=([("sv", 1)] if dt == NT - 1 else []))
            SV_MIX = sch.n("sv")
            for do in range(NT):
                b = do % 2
                for di in range(NT):
                    w = []
                    if di == 0:
                        w = [("sv", SV_MIX), ("ss", sch.n("ss")), ("lffn", LDG_FFN)]
                    sch.op("t", lambda t, do=do, di=di, b=b: t.matmul(
                        psMM[b][:], wkk_sb[:, di * D + do * 128: di * D + (do + 1) * 128],
                        xk_bf[:, di * CH:(di + 1) * CH],
                        start=(di == 0), stop=(di == NT - 1)), wait=w,
                        inc=([("pe", 1)] if di == NT - 1 else []))
                sch.op("s", lambda s, b=b: s.activation(
                    out=ktmp[:], in_=psMM[b][:], func=AF.Relu),
                    wait=["pe", ("sv", sch.n("sv"))], inc=[("ss", 1)])
                sch.op("v", lambda v, do=do: v.tensor_mul(
                    kact[:, do * CH:(do + 1) * CH], ktmp[:], ktmp[:]),
                    wait=["ss"], inc=[("sv", 1)])
            for do in range(NT):
                b = do % 2
                for di in range(NT):
                    w = [("ss", sch.n("ss")), ("sv", sch.n("sv"))] if di == 0 else ()
                    sch.op("t", lambda t, do=do, di=di, b=b: t.matmul(
                        psMM[b][:], wr_sb[:, di * D + do * 128: di * D + (do + 1) * 128],
                        xr_bf[:, di * CH:(di + 1) * CH],
                        start=(di == 0), stop=(di == NT - 1)), wait=w,
                        inc=([("pe", 1)] if di == NT - 1 else []))
                sch.op("s", lambda s, b=b: s.activation(
                    out=rsig[:], in_=psMM[b][:], func=AF.Sigmoid),
                    wait=["pe", ("sv", sch.n("sv"))], inc=[("ss", 1)])
                SS_RS = sch.n("ss")
                b2_ = (do + 1) % 2
                for di in range(NT):
                    w = [("ss", SS_RS), ("sv", sch.n("sv"))] if di == 0 else ()
                    sch.op("t", lambda t, do=do, di=di, b=b2_: t.matmul(
                        psMM[b][:], wvv_sb[:, di * D + do * 128: di * D + (do + 1) * 128],
                        kact[:, di * CH:(di + 1) * CH],
                        start=(di == 0), stop=(di == NT - 1)), wait=w,
                        inc=([("pe", 1)] if di == NT - 1 else []))
                sch.op("v", lambda v, b=b2_: v.tensor_mul(
                    mixt[:], psMM[b][:], rsig[:]), wait=["pe"])
                sch.op("v", lambda v, do=do, co=co: v.tensor_add(
                    xT[:, do * S + co: do * S + co + CH],
                    xT[:, do * S + co: do * S + co + CH], mixt[:]), inc=[("sv", 1)])
        flush()
        l0.close()

        if TAP("xnbf", [128, NT * S], BF16) is not None:
            sch.op("sy", lambda sy: sy.dma_start(out=tap_h["xnbf"][:], in_=xnbf[:]),
                   wait=[("sv", sch.n("sv"))], inc=[("st", 16)])
        if TAP("xl0f", [128, NT * S]) is not None:
            sch.op("sy", lambda sy: sy.dma_start(out=tap_h["xl0f"][:], in_=xT[:]),
                   wait=[("sv", sch.n("sv"))], inc=[("st", 16)])
        flush()
        if phases < 2:
            emit_output()
            return nc, h_in, tap_h

        vp = ExitStack()
        VB = sb(vp, "VB", [128, NST * VROW], BF16)
        vpw = ExitStack()
        wvslc = [sb(vpw, "wvslc0", [128, NT * 256], BF16),
                 sb(vpw, "wvslc1", [128, NT * 256], BF16)]
        psV = [ps(vpw, "psV0", [128, 256]), ps(vpw, "psV1", [128, 256])]

        SV_L0 = sch.n("sv")
        sch.op("v", lambda v: v.memset(VB[:], 0.0), wait=[("sv", SV_L0)],
               inc=[("sv", 1)])
        sch.op("v", lambda v: v.memset(
            VB[:].rearrange("p (t h c) -> p t h c", t=NST, h=H, c=68)[:, :, :, 64:65],
            1.0), inc=[("sv", 1)])
        SV_VONES = sch.n("sv")

        for ec in range(4):
            buf = ec % 2
            wv_w = [("gc", sch.n("gc"))]
            if ec == 0:
                wv_w += [("sv", SV_L0), ("ss", sch.n("ss")), ("pe", sch.n("pe"))]
            sch.op("g", lambda g, ec=ec, b=buf: g.dma_start(
                out=wvslc[b][:].rearrange("p (dt e) -> p dt e", dt=NT),
                in_=wvT_h[:, ec * 256:(ec + 1) * 256].rearrange(
                    "(dt p) e -> p dt e", p=128)),
                wait=wv_w, inc=[("lwv", 16)])
            LDW = sch.n("lwv")
            for sti in range(NST):
                b2_ = sti % 2
                for di in range(NT):
                    w = []
                    if di == 0:
                        w = [("lwv", LDW), ("sv", sch.n("sv"))]
                        if sti == 0 and ec == 0:
                            w.append(("sv", SV_VONES))
                    sch.op("t", lambda t, b=buf, b2_=b2_, di=di, sti=sti: t.matmul(
                        psV[b2_][:], xnbf[:, di * S + sti * 128: di * S + sti * 128 + 128],
                        wvslc[b][:, di * 256:(di + 1) * 256],
                        start=(di == 0), stop=(di == NT - 1)),
                        wait=w, inc=([("pe", 1)] if di == NT - 1 else []))
                sch.op("v", lambda v, sti=sti, ec=ec, b2_=b2_: v.tensor_copy(
                    VB[:, sti * VROW + (ec * 4) * 68: sti * VROW + (ec * 4 + 4) * 68]
                      .rearrange("p (h c) -> p h c", h=4, c=68)[:, :, 0:64],
                    psV[b2_][:].rearrange("p (h c) -> p h c", h=4, c=64)),
                    wait=["pe"], inc=[("sv", 1)])
            sch.op("g", lambda g: g.engine_nop(), wait=[("sv", sch.n("sv"))],
                   inc=[("gc", 1)])
        flush()
        vpw.close()
        if TAP("VB", [128, NST * VROW], BF16) is not None:
            sch.op("sy", lambda sy: sy.dma_start(out=tap_h["VB"][:], in_=VB[:]),
                   wait=[("sv", sch.n("sv"))], inc=[("st", 16)])
            flush()

        at = ExitStack()
        QTh = sb(at, "QTh", [64, S], BF16)
        KTh = sb(at, "KTh", [64, S], BF16)
        expS = sb(at, "expS", [128, NST * 512], BF16)
        rb_sb = sb(at, "rb_sb", [128, 512], F32)
        ctxf = sb(at, "ctxf", [128, NT * S], BF16)
        recipr = sb(at, "recipr", [1, 512], F32)
        wqkslc = [sb(at, "wqks0", [128, NT * 128], BF16)]
        woslc = [sb(at, "woslc0", [128, NT * 128], BF16)]
        psS = [ps(at, "psS0", [128, 512]), ps(at, "psS1", [128, 512]),
               ps(at, "psS2", [128, 512])]
        psU = ps(at, "psU", [128, 512])
        psR = ps(at, "psR", [128, 512])
        psQK = ps(at, "psQK", [128, 512])

        for hh in range(H):
            et, sub = divmod(hh, 2)
            buf = 0
            sch.op("g", lambda g, hh=hh, b=buf: g.dma_start(
                out=wqkslc[b][:, 0:NT * 64].rearrange("p (dt e) -> p dt e", dt=NT),
                in_=wqT_h[:, hh * 64:(hh + 1) * 64].rearrange(
                    "(dt p) e -> p dt e", p=128)),
                wait=[("gc", sch.n("gc"))], inc=[("lqk", 16)])
            sch.op("g", lambda g, hh=hh, b=buf: g.dma_start(
                out=wqkslc[b][:, NT * 64:NT * 128].rearrange(
                    "p (dt e) -> p dt e", dt=NT),
                in_=wkT_h[:, hh * 64:(hh + 1) * 64].rearrange(
                    "(dt p) e -> p dt e", p=128)),
                inc=[("lqk", 16)])
            LDW = sch.n("lqk")
            for (dst, wof, bias) in ((QTh, 0, bq_s), (KTh, NT * 64, bk_s)):
                for sc in range(4):
                    for di in range(NT):
                        w = [("lqk", LDW), ("ss", sch.n("ss"))] if di == 0 else ()
                        sch.op("t", lambda t, b=buf, wof=wof, di=di, sc=sc: t.matmul(
                            psQK[0:64, :],
                            wqkslc[b][:, wof + di * 64: wof + (di + 1) * 64],
                            xnbf[:, di * S + sc * 512: di * S + (sc + 1) * 512],
                            start=(di == 0), stop=(di == NT - 1)),
                            wait=w, inc=([("pe", 1)] if di == NT - 1 else []))
                    sch.op("s", lambda s, dst=dst, sc=sc, bias=bias, et=et, sub=sub:
                        s.activation(
                            out=dst[:, sc * 512:(sc + 1) * 512], in_=psQK[0:64, :],
                            func=AF.Identity,
                            bias=bias[sub * 64:sub * 64 + 64, et:et + 1]),
                        wait=["pe"], inc=[("ss", 1)])
            SS_QK = sch.n("ss")
            sch.op("g", lambda g: g.engine_nop(), wait=[("ss", SS_QK)],
                   inc=[("gc", 1)])
            for qc in range(4):
                qo = qc * 512
                for kt in range(NST):
                    b = kt % 3
                    w = []
                    if hh == 0 and qc == 0 and kt == 0:
                        w = [("ss", SS_QK), ("sv", sch.n("sv"))]
                    elif kt >= 3 or not (hh == 0 and qc == 0):
                        w = [("ss", sch.n("ss") - 2)]
                    sch.op("t", lambda t, b=b, kt=kt, qo=qo: t.matmul(
                        psS[b][:], KTh[:, kt * 128:(kt + 1) * 128],
                        QTh[:, qo:qo + 512], start=True, stop=True),
                        wait=(w + [("ss", SS_QK)] if kt == 0 else w),
                        inc=[("pe", 1)])
                    sch.op("s", lambda s, b=b, kt=kt: s.activation(
                        out=expS[:, kt * 512:(kt + 1) * 512], in_=psS[b][:],
                        func=AF.Exp, scale=0.125, bias=SMSHIFT),
                        wait=["pe"], inc=[("ss", 1)])
                SS_EXP = sch.n("ss")
                if sub == 0:
                    for kt in range(NST):
                        w = [("ss", SS_EXP), ("sv", sch.n("sv"))] if kt == 0 else ()
                        sch.op("t", lambda t, kt=kt, hh=hh: t.matmul(
                            psU[0:65, :],
                            VB[:, kt * VROW + hh * 68: kt * VROW + hh * 68 + 65],
                            expS[:, kt * 512:(kt + 1) * 512],
                            start=(kt == 0), stop=(kt == NST - 1)), wait=w,
                            inc=([("pe", 1)] if kt == NST - 1 else []))
                    sch.op("v", lambda v: v.reciprocal(recipr[:], psU[64:65, :]),
                           wait=["pe"], inc=[("sv", 1)])
                else:
                    for kt in range(NST):
                        w = [("ss", SS_EXP), ("sv", sch.n("sv"))] if kt == 0 else ()
                        sch.op("t", lambda t, kt=kt, hh=hh: t.matmul(
                            psU[64:128, :],
                            VB[:, kt * VROW + hh * 68: kt * VROW + hh * 68 + 64],
                            expS[:, kt * 512:(kt + 1) * 512],
                            start=(kt == 0), stop=(kt == NST - 1),
                            tile_position=(0, 64)), wait=w,
                            inc=([("pe", 1)] if kt == NST - 1 else []))
                    for kt in range(NST):
                        sch.op("t", lambda t, kt=kt, hh=hh: t.matmul(
                            psU[0:1, :],
                            VB[:, kt * VROW + hh * 68 + 64: kt * VROW + hh * 68 + 65],
                            expS[:, kt * 512:(kt + 1) * 512],
                            start=(kt == 0), stop=(kt == NST - 1)),
                            inc=([("pe", 1)] if kt == NST - 1 else []))
                    sch.op("v", lambda v: v.reciprocal(recipr[:], psU[0:1, :]),
                           wait=["pe"], inc=[("sv", 1)])
                sch.op("t", lambda t: t.matmul(
                    psR[:], ones1[:], recipr[:], start=True, stop=True),
                    wait=["sv"], inc=[("pe", 1)])
                sch.op("s", lambda s: s.activation(
                    out=rb_sb[:], in_=psR[:], func=AF.Copy),
                    wait=["pe", ("sv", sch.n("sv"))],
                    inc=[("ss", 1)])
                sch.op("v", lambda v, et=et, sub=sub, qo=qo: v.tensor_mul(
                    ctxf[sub * 64:sub * 64 + 64, et * S + qo: et * S + qo + 512],
                    psU[sub * 64:sub * 64 + 64, :],
                    rb_sb[sub * 64:sub * 64 + 64, :]),
                    wait=[("ss", sch.n("ss"))], inc=[("sv", 1)])
        SV_CTX = sch.n("sv")
        for et in range(NT):
            buf = 0
            sch.op("g", lambda g, et=et, b=buf: g.dma_start(
                out=woslc[b][:].rearrange("p (dt e) -> p dt e", dt=NT),
                in_=woT_h[:, et * 128:(et + 1) * 128].rearrange(
                    "(dt p) e -> p dt e", p=128)),
                wait=[("gc", sch.n("gc"))], inc=[("lwo", 16)])
            LDW = sch.n("lwo")
            for di in range(NT):
                sch.op("t", lambda t, b=buf, di=di, et=et: t.matmul(
                    psS[2][:, et:et + 1],
                    woslc[b][:, di * 128:(di + 1) * 128],
                    bv_bf[:, di:di + 1], start=(di == 0), stop=(di == NT - 1)),
                    wait=([("lwo", LDW), ("sv", SV_CTX)] if di == 0 else ()),
                    inc=([("pe", 1)] if di == NT - 1 else []))
            sch.op("v", lambda v, et=et: v.tensor_add(
                bo2_s[:, et:et + 1], psS[2][:, et:et + 1], bo_s[:, et:et + 1]),
                wait=["pe"], inc=[("sv", 1)])
            SV_BO = sch.n("sv")
            for qc in range(4):
                b2_ = qc % 2
                for di in range(NT):
                    w = [("sv", SV_BO)] if di == 0 else ()
                    sch.op("t", lambda t, b=buf, b2_=b2_, di=di, qc=qc: t.matmul(
                        psS[b2_][:], woslc[b][:, di * 128:(di + 1) * 128],
                        ctxf[:, di * S + qc * 512: di * S + (qc + 1) * 512],
                        start=(di == 0), stop=(di == NT - 1)),
                        wait=w, inc=([("pe", 1)] if di == NT - 1 else []))
                sch.op("v", lambda v, et=et, qc=qc, b2_=b2_:
                    v.scalar_tensor_tensor(
                        out=xT[:, et * S + qc * 512: et * S + (qc + 1) * 512],
                        in0=psS[b2_][:], scalar=bo2_s[:, et:et + 1],
                        in1=xT[:, et * S + qc * 512: et * S + (qc + 1) * 512],
                        op0=ALU.add, op1=ALU.add), wait=["pe"], inc=[("sv", 1)])
            sch.op("g", lambda g: g.engine_nop(), wait=[("sv", sch.n("sv"))],
                   inc=[("gc", 1)])
        if TAP("ctxf", [128, NT * S], BF16) is not None:
            sch.op("sy", lambda sy: sy.dma_start(out=tap_h["ctxf"][:], in_=ctxf[:]),
                   wait=[("sv", sch.n("sv"))], inc=[("st", 16)])
        if TAP("QTh", [64, S], BF16) is not None:
            sch.op("sy", lambda sy: sy.dma_start(out=tap_h["QTh"][:], in_=QTh[:]),
                   wait=[("sv", sch.n("sv"))], inc=[("st", 16)])
        if TAP("KTh", [64, S], BF16) is not None:
            sch.op("sy", lambda sy: sy.dma_start(out=tap_h["KTh"][:], in_=KTh[:]),
                   wait=[("sv", sch.n("sv"))], inc=[("st", 16)])
        if TAP("expS", [128, NST * 512], BF16) is not None:
            sch.op("sy", lambda sy: sy.dma_start(out=tap_h["expS"][:], in_=expS[:]),
                   wait=[("sv", sch.n("sv"))], inc=[("st", 16)])
        flush()
        at.close()
        vp.close()
        if TAP("xl0", [128, NT * S]) is not None:
            sch.op("sy", lambda sy: sy.dma_start(out=tap_h["xl0"][:], in_=xT[:]),
                   wait=[("sv", sch.n("sv"))], inc=[("st", 16)])
            flush()
        if phases < 3:
            emit_output()
            return nc, h_in, tap_h

        l1c = ExitStack()
        wselT_s = sb(l1c, "wselT_s", [NEXP, S], F32)
        wselT_bf = sb(l1c, "wselT_bf", [NEXP, S], BF16)
        l1 = ExitStack()
        mrow = sb(l1, "mrow1", [1, GRP], F32)
        rrow = sb(l1, "rrow1", [1, GRP], F32)
        rta = sb(l1, "rta1", [1, GRP], F32)
        rtb = sb(l1, "rtb1", [1, GRP], F32)
        xnf1 = sb(l1, "xnf1", [128, NT * GRP], F32)
        deltaf = xnf1
        Uf = sb(l1, "Uf", [128, NT * GRP], F32)
        xn2f = sb(l1, "xn2f", [128, NT * GRP], F32)
        Yf = xn2f
        xn2bf = sb(l1, "xn2bf", [128, NT * GRP], BF16)
        dlo_bf = sb(l1, "dlo_bf", [64, GRP], BF16)
        BCrow = sb(l1, "BCrow", [128, GRP], F32)
        Bbig = sb(l1, "Bbig", [128, GRP], F32)
        Cbig = sb(l1, "Cbig", [128, GRP], F32)
        dAb = [sb(l1, "dA0", [128, GRP], F32), sb(l1, "dA1", [128, GRP], F32)]
        wSb = sb(l1, "wS0", [128, GRP], F32)
        hSb = sb(l1, "hS0", [128, GRP], F32)
        tmpS = wSb
        oin_bf = xn2bf
        opslc = [sb(l1, "opslc0", [128, NT * 128], BF16)]
        sp1 = sb(l1, "sp1", [128, GRP], F32)
        lgt = sb(l1, "lgt", [128, NEXP], F32)
        gex = sb(l1, "gex", [128, NEXP], F32)
        gex2 = sb(l1, "gex2", [128, NEXP], F32)
        gmx = sb(l1, "gmx", [128, 4], F32)
        geq = sb(l1, "geq", [128, NEXP], F32)
        ginc = sb(l1, "ginc", [128, NEXP], F32)
        gq = sb(l1, "gq", [128, 12], F32)
        wsel = sb(l1, "wsel", [128, NEXP], F32)
        psBC1 = ps(l1, "psBC1", [128, 2 * GRP])
        psG = ps(l1, "psG", [128, NEXP])
        psWT = ps(l1, "psWT", [NEXP, 128])
        psP = [ps(l1, "psP0", [128, GRP]), ps(l1, "psP1", [128, GRP])]
        psBb = ps(l1, "psBb", [128, GRP])
        psCb = ps(l1, "psCb", [128, GRP])

        for gi in range(NGRP):
            co = gi * GRP
            ln_stats(GRP, xT, lambda dt, co=co: dt * S + co, mrow, rrow, EPS0,
                     psP[0][0:1, :], psP[1][0:1, :], sp1, rta, rtb)
            SS_R = sch.n("ss")
            sch.op("t", lambda t: t.matmul(
                psBC1[:, 0:GRP], ones1[:], mrow[0:1, 0:GRP], start=True,
                stop=True), wait=[("ss", SS_R), ("sv", sch.n("sv"))])
            sch.op("t", lambda t: t.matmul(
                psBC1[:, GRP:2 * GRP], ones1[:], rrow[0:1, 0:GRP], start=True,
                stop=True), inc=[("pe", 1)])
            PE_BC1 = sch.n("pe")
            for dt in range(NT):
                sch.op("v", lambda v, dt=dt, co=co: v.scalar_tensor_tensor(
                    out=xnf1[:, dt * GRP:(dt + 1) * GRP],
                    in0=xT[:, dt * S + co: dt * S + co + GRP], scalar=1.0,
                    in1=psBC1[:, 0:GRP], op0=ALU.mult, op1=ALU.subtract),
                    wait=([("pe", PE_BC1)] if dt == 0 else ()))
                sch.op("v", lambda v, dt=dt: v.tensor_mul(
                    xnf1[:, dt * GRP:(dt + 1) * GRP], xnf1[:, dt * GRP:(dt + 1) * GRP],
                    psBC1[:, GRP:2 * GRP]))
                sch.op("v", lambda v, dt=dt, co=co: v.tensor_copy(
                    xnbf[:, dt * S + co: dt * S + co + GRP],
                    xnf1[:, dt * GRP:(dt + 1) * GRP]), inc=[("sv", 1)])
            SV_XN1 = sch.n("sv")
            for sti in range(GRP // 128):
                for dt in range(NT):
                    w = [("sv", SV_XN1 if sti == 0 else sch.n("sv"))] \
                        if dt == 0 else ()
                    sch.op("t", lambda t, sti=sti, dt=dt: t.matmul(
                        psG[:], xnf1[:, dt * GRP + sti * 128: dt * GRP + sti * 128 + 128],
                        wgate_s[:, dt * NEXP:(dt + 1) * NEXP],
                        start=(dt == 0), stop=(dt == NT - 1)),
                        wait=w, inc=([("pe", 1)] if dt == NT - 1 else []))
                def vsem(fn, wait=()):
                    sch.op("v", fn, wait=list(wait) + [("sv", sch.n("sv"))],
                           inc=[("sv", 1)])
                vsem(lambda v: v.tensor_add(lgt[:], psG[:], bgb_s[:]), wait=["pe"])
                vsem(lambda v: v.reduce_max(gmx[:, 0:1], lgt[:], axis=AX.X))
                vsem(lambda v: v.tensor_scalar(
                    out=gex[:], in0=lgt[:], scalar1=gmx[:, 0:1], scalar2=None,
                    op0=ALU.subtract))
                sch.op("s", lambda s: s.activation(out=gex2[:], in_=gex[:],
                                                   func=AF.Exp),
                       wait=[("sv", sch.n("sv"))], inc=[("ss", 1)])
                vsem(lambda v: v.reduce_sum(gmx[:, 1:2], gex2[:], axis=AX.X),
                     wait=[("ss", sch.n("ss"))])
                vsem(lambda v: v.tensor_scalar(
                    out=geq[:], in0=lgt[:], scalar1=gmx[:, 0:1], scalar2=None,
                    op0=ALU.is_ge))
                vsem(lambda v: v.tensor_scalar(
                    out=ginc[:], in0=geq[:], scalar1=-1.0, scalar2=1.0,
                    op0=ALU.mult, op1=ALU.add))
                sch.op("s", lambda s: s.activation(
                    out=gex[:], in_=ones8[:], func=AF.Identity,
                    scale=gmx[:, 1:2]), wait=[("sv", sch.n("sv"))], inc=[("ss", 1)])
                vsem(lambda v: v.reciprocal(gex2[:], gex[:]),
                     wait=[("ss", sch.n("ss"))])
                vsem(lambda v: v.memset(gq[:, 0:2], 1.0))
                vsem(lambda v: v.tensor_tensor_scan(
                    out=gq[:, 2:10], data0=ginc[:], data1=ones8[:], initial=1.0,
                    op0=ALU.mult, op1=ALU.mult))
                vsem(lambda v: v.tensor_mul(ginc[:], geq[:], gq[:, 1:9]))
                vsem(lambda v: v.tensor_mul(wsel[:], ginc[:], gex2[:]))
                if gi == 0 and sti == 0 and TAP("wsel0", [128, NEXP]) is not None:
                    sch.op("sy", lambda sy: sy.dma_start(
                        out=tap_h["wsel0"][:], in_=wsel[:]),
                        wait=[("sv", sch.n("sv"))], inc=[("st", 16)])
                if gi == 0 and sti == 0 and TAP("glgt0", [128, NEXP]) is not None:
                    sch.op("sy", lambda sy: sy.dma_start(
                        out=tap_h["glgt0"][:], in_=lgt[:]),
                        wait=[("sv", sch.n("sv"))], inc=[("st", 16)])
                if gi == 0 and sti == 0 and TAP("ggmx0", [128, 4]) is not None:
                    sch.op("sy", lambda sy: sy.dma_start(
                        out=tap_h["ggmx0"][:], in_=gmx[:]),
                        wait=[("sv", sch.n("sv"))], inc=[("st", 16)])
                if gi == 0 and sti == 0 and TAP("grc0", [128, NEXP]) is not None:
                    sch.op("sy", lambda sy: sy.dma_start(
                        out=tap_h["grc0"][:], in_=gex2[:]),
                        wait=[("sv", sch.n("sv"))], inc=[("st", 16)])
                if gi == 0 and sti == 0 and TAP("gq0", [128, 12]) is not None:
                    sch.op("sy", lambda sy: sy.dma_start(
                        out=tap_h["gq0"][:], in_=gq[:]),
                        wait=[("sv", sch.n("sv"))], inc=[("st", 16)])
                if gi == 0 and sti == 0 and taps:
                    sch.barrier("v", [("st", sch.n("st"))])
                    sch.barrier("s", [("st", sch.n("st"))])
                sch.op("t", lambda t: t.transpose(psWT[:], wsel[:], ident[:]),
                       wait=[("sv", sch.n("sv")), ("st", sch.n("st"))],
                       inc=[("pe", 1)])
                sch.op("v", lambda v, co=co, sti=sti: v.tensor_copy(
                    wselT_s[:, co + sti * 128: co + sti * 128 + 128], psWT[:]),
                    wait=["pe"])
                sch.op("v", lambda v, co=co, sti=sti: v.tensor_copy(
                    wselT_bf[:, co + sti * 128: co + sti * 128 + 128], psWT[:]),
                    inc=[("sv", 1)])
            ln_stats(GRP, xnf1, lambda dt: dt * GRP, None, rrow, EPS1,
                     psP[0][0:1, :], psP[1][0:1, :], sp1, rta, rtb)
            SS_R2 = sch.n("ss")
            sch.op("t", lambda t: t.matmul(
                psBC1[:, 0:GRP], ones1[:], rrow[0:1, 0:GRP], start=True,
                stop=True), wait=[("ss", SS_R2), ("sv", sch.n("sv"))], inc=[("pe", 1)])
            PE_R2 = sch.n("pe")
            for dt in range(NT):
                sch.op("v", lambda v, dt=dt: v.scalar_tensor_tensor(
                    out=xn2f[:, dt * GRP:(dt + 1) * GRP],
                    in0=xnf1[:, dt * GRP:(dt + 1) * GRP],
                    scalar=normw_s[:, dt:dt + 1], in1=psBC1[:, 0:GRP],
                    op0=ALU.mult, op1=ALU.mult),
                    wait=([("pe", PE_R2)] if dt == 0 else ()))
                sch.op("v", lambda v, dt=dt: v.tensor_copy(
                    xn2bf[:, dt * GRP:(dt + 1) * GRP], xn2f[:, dt * GRP:(dt + 1) * GRP]),
                    inc=[("sv", 1)])
            SV_XN2 = sch.n("sv")
            for mt in range(2):
                msz = 128 if mt == 0 else 64
                for dt in range(NT):
                    w = [("sv", SV_XN2), ("lset", LDG_SET)] if (mt == 0 and dt == 0) else ()
                    sch.op("t", lambda t, mt=mt, dt=dt, msz=msz: t.matmul(
                        psP[mt][0:msz, :],
                        xprojT_s[:, dt * 192 + mt * 128: dt * 192 + mt * 128 + msz],
                        xn2bf[:, dt * GRP:(dt + 1) * GRP],
                        start=(dt == 0), stop=(dt == NT - 1)),
                        wait=w, inc=([("pe", 1)] if dt == NT - 1 else []))
            sch.op("v", lambda v: v.tensor_copy(dlo_bf[:], psP[0][0:64, :]),
                   wait=["pe"])
            sch.op("v", lambda v: v.tensor_copy(BCrow[64:128, :], psP[0][64:128, :]))
            sch.op("v", lambda v: v.tensor_copy(BCrow[0:64, :], psP[1][0:64, :]),
                   inc=[("sv", 1)])
            SV_PROJ = sch.n("sv")
            for dt in range(NT):
                b = dt % 2
                sch.op("t", lambda t, dt=dt, b=b: t.matmul(
                    psP[b][:], dtprojT_s[:, dt * 128:(dt + 1) * 128], dlo_bf[:],
                    start=True, stop=True),
                    wait=([("sv", SV_PROJ)] if dt <= 1 else [("ss", sch.n("ss") - 1)]),
                    inc=[("pe", 1)])
                sch.op("s", lambda s, dt=dt, b=b: s.activation(
                    out=sp1[:], in_=psP[b][:], func=AF.Abs,
                    bias=dtprojb_s[:, dt:dt + 1]),
                    wait=["pe", ("sv", sch.n("sv"))])
                sch.op("s", lambda s, dt=dt, b=b: s.activation(
                    out=deltaf[:, dt * GRP:(dt + 1) * GRP], in_=psP[b][:],
                    func=AF.Relu, bias=dtprojb_s[:, dt:dt + 1]))
                sch.op("s", lambda s: s.activation(
                    out=sp1[:], in_=sp1[:], func=AF.Exp, scale=-1.0))
                sch.op("s", lambda s: s.activation(
                    out=sp1[:], in_=sp1[:], func=AF.Ln, bias=1.0), inc=[("ss", 1)])
                sch.op("v", lambda v, dt=dt: v.tensor_add(
                    deltaf[:, dt * GRP:(dt + 1) * GRP],
                    deltaf[:, dt * GRP:(dt + 1) * GRP], sp1[:]),
                    wait=[("ss", sch.n("ss"))])
                sch.op("v", lambda v, dt=dt: v.tensor_mul(
                    Uf[:, dt * GRP:(dt + 1) * GRP], deltaf[:, dt * GRP:(dt + 1) * GRP],
                    xn2f[:, dt * GRP:(dt + 1) * GRP]), inc=[("sv", 1)])
            SV_U = sch.n("sv")
            SS0 = sch.n("ss")
            SV0 = sch.n("sv")
            PE0 = sch.n("pe")
            for n in range(STATE):
                wpe = [("sv", SV_U)] if n == 0 else [("ss", SS0 + 9 * (n - 1) + 1)]
                sch.op("t", lambda t, n=n: t.matmul(
                    psBb[:], ident[:, 64 + n:64 + n + 1].broadcast_to([128, 128]),
                    BCrow[:], start=True, stop=True),
                    wait=wpe, inc=[("pe", 1)])
                sch.op("t", lambda t, n=n: t.matmul(
                    psCb[:], ident[:, n:n + 1].broadcast_to([128, 128]),
                    BCrow[:], start=True, stop=True),
                    inc=[("pe", 1)])
                wsc = [("pe", PE0 + 2 * n + 2)]
                if n > 0:
                    wsc.append(("sv", SV0 + 8 * n))
                sch.op("s", lambda s: s.activation(
                    out=Bbig[:], in_=psBb[:], func=AF.Copy), wait=wsc)
                sch.op("s", lambda s: s.activation(
                    out=Cbig[:], in_=psCb[:], func=AF.Copy), inc=[("ss", 1)])
                SS_BC = SS0 + 9 * n + 1
                for dt in range(NT):
                    it = n * NT + dt
                    db = dt % 2
                    wda = [("sv", SV0 + it - 1)] if it >= 2 else ()
                    sch.op("s", lambda s, dt=dt, n=n, db=db: s.activation(
                        out=dAb[db][:], in_=deltaf[:, dt * GRP:(dt + 1) * GRP],
                        func=AF.Exp,
                        scale=aneg_s[:, dt * STATE + n: dt * STATE + n + 1]),
                        wait=wda, inc=[("ss", 1)])
                    SS_DA = SS0 + 9 * n + 2 + dt
                    sch.op("v", lambda v, db=db: v.memset(
                        dAb[db][:].rearrange("p (m t) -> p m t", m=MG,
                                             t=CHUNK)[:, :, 0:1], 0.0),
                        wait=[("ss", SS_DA)])
                    sch.op("v", lambda v, dt=dt, db=db: v.tensor_mul(
                        wSb[:], Uf[:, dt * GRP:(dt + 1) * GRP], Bbig[:]),
                        wait=[("ss", SS_BC)])
                    sch.op("v", lambda v, db=db: v.tensor_tensor_scan(
                        out=hSb[:], data0=dAb[db][:], data1=wSb[:],
                        initial=0.0, op0=ALU.mult, op1=ALU.add))
                    if n == 0:
                        sch.op("v", lambda v, dt=dt: v.tensor_mul(
                            Yf[:, dt * GRP:(dt + 1) * GRP], hSb[:], Cbig[:]),
                            inc=[("sv", 1)])
                    else:
                        sch.op("v", lambda v: v.tensor_mul(tmpS[:], hSb[:], Cbig[:]))
                        sch.op("v", lambda v, dt=dt: v.tensor_add(
                            Yf[:, dt * GRP:(dt + 1) * GRP],
                            Yf[:, dt * GRP:(dt + 1) * GRP], tmpS[:]),
                            inc=[("sv", 1)])
            SV_SCAN = sch.n("sv")
            for dt in range(NT):
                sch.op("v", lambda v, dt=dt, co=co: v.scalar_tensor_tensor(
                    out=oin_bf[:, dt * GRP:(dt + 1) * GRP],
                    in0=xnbf[:, dt * S + co: dt * S + co + GRP],
                    scalar=dvec_s[:, dt:dt + 1],
                    in1=Yf[:, dt * GRP:(dt + 1) * GRP], op0=ALU.mult, op1=ALU.add),
                    inc=([("sv", 1)] if dt == NT - 1 else []))
            SV_OIN = sch.n("sv")
            for et in range(NT):
                buf = 0
                sch.op("g", lambda g, et=et, b=buf: g.dma_start(
                    out=opslc[b][:].rearrange("p (dt e) -> p dt e", dt=NT),
                    in_=outprojT_h[:, et * 128:(et + 1) * 128].rearrange(
                        "(dt p) e -> p dt e", p=128)),
                    wait=[("gc", sch.n("gc"))], inc=[("lop", 16)])
                LDW = sch.n("lop")
                b2_ = et % 2
                for di in range(NT):
                    w = [("lop", LDW), ("sv", SV_OIN), ("sv", sch.n("sv"))] \
                        if di == 0 else ()
                    sch.op("t", lambda t, b=buf, b2_=b2_, di=di: t.matmul(
                        psP[b2_][:], opslc[b][:, di * 128:(di + 1) * 128],
                        oin_bf[:, di * GRP:(di + 1) * GRP],
                        start=(di == 0), stop=(di == NT - 1)),
                        wait=w, inc=([("pe", 1)] if di == NT - 1 else []))
                sch.op("v", lambda v, et=et, co=co, b2_=b2_: v.tensor_add(
                    xT[:, et * S + co: et * S + co + GRP],
                    xT[:, et * S + co: et * S + co + GRP], psP[b2_][:]),
                    wait=["pe"], inc=[("sv", 1)])
                sch.op("g", lambda g: g.engine_nop(), wait=[("sv", sch.n("sv"))],
                       inc=[("gc", 1)])
            if gi == 0 and TAP("delta0", [128, NT * GRP]) is not None:
                sch.op("sy", lambda sy: sy.dma_start(
                    out=tap_h["delta0"][:], in_=deltaf[:]),
                    wait=[("sv", SV_U)], inc=[("st", 16)])
            if gi == 0 and TAP("Y0", [128, NT * GRP]) is not None:
                sch.op("sy", lambda sy: sy.dma_start(out=tap_h["Y0"][:], in_=Yf[:]),
                       wait=[("sv", SV_SCAN)], inc=[("st", 16)])
        if TAP("wselT", [NEXP, S]) is not None:
            sch.op("sy", lambda sy: sy.dma_start(
                out=tap_h["wselT"][:], in_=wselT_s[:]),
                wait=[("sv", sch.n("sv"))], inc=[("st", 16)])
        flush()
        l1.close()
        SV_L1A = sch.n("sv")
        if phases < 4:
            l1c.close()
            emit_output()
            return nc, h_in, tap_h
        if TAP("xl1a", [128, NT * S]) is not None:
            sch.op("sy", lambda sy: sy.dma_start(out=tap_h["xl1a"][:], in_=xT[:]),
                   wait=[("sv", SV_L1A)], inc=[("st", 16)])
            flush()

        mo = ExitStack()
        w1b = [sb(mo, "w1b0", [128, NT * JB * 128], BF16),
               sb(mo, "w1b1", [128, NT * JB * 128], BF16)]
        w2b = [sb(mo, "w2b0", [128, JB * D], BF16),
               sb(mo, "w2b1", [128, JB * D], BF16)]
        hts = sb(mo, "hts", [128, JB * TC], BF16)
        hfr = sb(mo, "hfr", [128, TC], F32)
        wsb = sb(mo, "wsb", [128, TC], F32)
        psH = [ps(mo, "psH0", [128, TC]), ps(mo, "psH1", [128, TC])]
        psO = [ps(mo, "psO0", [128, 2 * TC]), ps(mo, "psO1", [128, 2 * TC]),
               ps(mo, "psO2", [128, 2 * TC]), ps(mo, "psO3", [128, 2 * TC])]
        psWB = ps(mo, "psWB", [128, TC])

        def psOut(et):
            return psO[et // 2][:, (et % 2) * TC:(et % 2 + 1) * TC]

        GC_MOE = sch.n("gc")
        PE_W2PREV = sch.n("pe")
        for e in range(NEXP):
            for jb in range(NJB):
                blki = e * NJB + jb
                buf = blki % 2
                lsem = "lma" if buf == 0 else "lmb"
                wdma = [("gc", GC_MOE + max(0, blki - 1))]
                sch.op("g", lambda g, e=e, jb=jb, b=buf: g.dma_start(
                    out=w1b[b][:].rearrange("p (dt m) -> p dt m", dt=NT),
                    in_=w1_h[e, :, jb * JB * 128:(jb + 1) * JB * 128].rearrange(
                        "(dt p) m -> p dt m", p=128)),
                    wait=wdma, inc=[(lsem, 16)])
                sch.op("g", lambda g, e=e, jb=jb, b=buf: g.dma_start(
                    out=w2b[b][:].rearrange("p (j d) -> p j d", j=JB),
                    in_=w2_h[e, jb * JB * 128:(jb + 1) * JB * 128, :].rearrange(
                        "(j p) d -> p j d", p=128)),
                    inc=[(lsem, 16)])
                LDW = sch.n(lsem)
                for tc in range(NTC):
                    to = tc * TC
                    sch.op("t", lambda t, e=e, to=to: t.matmul(
                        psWB[:], ident[0:NEXP, e:e + 1].broadcast_to([NEXP, 128]),
                        wselT_s[0:NEXP, to:to + TC],
                        start=True, stop=True),
                        wait=[("sv", SV_L1A), ("ss", sch.n("ss"))], inc=[("pe", 1)])
                    sch.op("s", lambda s: s.activation(
                        out=wsb[:], in_=psWB[:], func=AF.Copy),
                        wait=["pe", ("sv", sch.n("sv"))],
                        inc=[("ss", 1)])
                    SS_WSB = sch.n("ss")
                    relu_ss = []
                    SV_EVAC = sch.n("sv")
                    for j in range(JB):
                        hb = j % 2
                        jj = jb * JB + j
                        for di in range(NT):
                            w = ()
                            if di == 0:
                                rg = relu_ss[j - 2] if j >= 2 else SS_WSB
                                w = [(lsem, LDW), ("ss", rg)]
                            sch.op("t", lambda t, b=buf, hb=hb, di=di, j=j, to=to:
                                t.matmul(
                                    psH[hb][:],
                                    w1b[b][:, di * JB * 128 + j * 128:
                                           di * JB * 128 + (j + 1) * 128],
                                    xnbf[:, di * S + to: di * S + to + TC],
                                    start=(di == 0), stop=(di == NT - 1)),
                                wait=w, inc=([("pe", 1)] if di == NT - 1 else []))
                        sch.op("s", lambda s, hb=hb, e=e, jj=jj: s.activation(
                            out=hfr[:], in_=psH[hb][:], func=AF.Relu,
                            bias=b1_s[:, e * NMT + jj: e * NMT + jj + 1]),
                            wait=["pe", ("ss", SS_WSB), ("sv", sch.n("sv"))],
                            inc=[("ss", 1)])
                        relu_ss.append(sch.n("ss"))
                        wm = [("ss", sch.n("ss"))]
                        if j == 0:
                            wm.append(("pe", PE_W2PREV))
                        sch.op("v", lambda v, j=j: v.tensor_mul(
                            hts[:, j * TC:(j + 1) * TC], hfr[:], wsb[:]),
                            wait=wm, inc=[("sv", 1)])
                    SV_HT = sch.n("sv")
                    if e == 0 and jb == 0 and tc == 0:
                        if TAP("hts0", [128, JB * TC], BF16) is not None:
                            sch.op("sy", lambda sy: sy.dma_start(
                                out=tap_h["hts0"][:], in_=hts[:]),
                                wait=[("sv", SV_HT)], inc=[("st", 16)])
                        if TAP("wsb0", [128, TC]) is not None:
                            sch.op("sy", lambda sy: sy.dma_start(
                                out=tap_h["wsb0"][:], in_=wsb[:]),
                                wait=[("sv", SV_HT)], inc=[("st", 16)])
                    for et in range(NT):
                        for j in range(JB):
                            w = []
                            if et == 0 and j == 0:
                                w = [("sv", SV_HT), ("sv", SV_EVAC)]
                            last = (j == JB - 1 and blki != 0 and et == NT - 1)
                            sch.op("t", lambda t, b=buf, et=et, j=j, blki=blki:
                                t.matmul(
                                    psOut(et),
                                    w2b[b][:, j * D + et * 128: j * D + (et + 1) * 128],
                                    hts[:, j * TC:(j + 1) * TC], start=(j == 0),
                                    stop=(j == JB - 1 and blki != 0)),
                                wait=w, inc=([("pe", 1)] if last else []))
                        if blki == 0:
                            sch.op("t", lambda t, et=et, to=to: t.matmul(
                                psOut(et), b2_s[:, et * 128:(et + 1) * 128],
                                wselT_bf[0:NEXP, to:to + TC], start=False, stop=True),
                                wait=([("lset", LDG_SET)] if et == 0 else ()),
                                inc=([("pe", 1)] if et == NT - 1 else []))
                    PE_W2PREV = sch.n("pe")
                    PE_GRP = sch.n("pe")
                    for et in range(NT):
                        sch.op("v", lambda v, et=et, to=to: v.tensor_add(
                            xT[:, et * S + to: et * S + to + TC],
                            xT[:, et * S + to: et * S + to + TC], psOut(et)),
                            wait=([("pe", PE_GRP)] if et == 0 else ()),
                            inc=([("sv", 1)] if et == NT - 1 else []))
                sch.op("g", lambda g: g.engine_nop(),
                       wait=[("pe", sch.n("pe"))], inc=[("gc", 1)])
        flush()
        mo.close()
        l1c.close()
        SV_MOE = sch.n("sv")

        emit_output()

    return nc, h_in, tap_h


def _layout_inputs(inputs):
    f = np.ascontiguousarray
    def pp(v):
        return f(v.reshape(NT, 128).T)
    return {
        "wqT": f(inputs["L0_attn_Wq"].T), "wkT": f(inputs["L0_attn_Wk"].T),
        "wvT": f(inputs["L0_attn_Wv"].T), "woT": f(inputs["L0_attn_Wo"].T),
        "bq": pp(inputs["L0_attn_bq"]), "bk": pp(inputs["L0_attn_bk"]),
        "bv": pp(inputs["L0_attn_bv"]), "bo": pp(inputs["L0_attn_bo"]),
        "wrT": f(inputs["L0_ffn_W_r"].T), "wkkT": f(inputs["L0_ffn_W_k"].T),
        "wvvT": f(inputs["L0_ffn_W_v"].T),
        "mur": pp(inputs["L0_ffn_mu_r"]), "muk": pp(inputs["L0_ffn_mu_k"]),
        "normw": pp(inputs["L1_attn_norm_w"]),
        "xprojT": f(inputs["L1_attn_x_proj_w"].T),
        "dtprojT": f(inputs["L1_attn_dt_proj_w"].T),
        "dtprojb": pp(inputs["L1_attn_dt_proj_b"]),
        "alog": f(inputs["L1_attn_A_log"].reshape(NT, 128, STATE)
                  .transpose(1, 0, 2).reshape(128, NT * STATE)),
        "dvec": pp(inputs["L1_attn_D"]),
        "outprojT": f(inputs["L1_attn_out_proj_w"].T),
        "wgate": f(inputs["L1_ffn_w_gate"].reshape(NT, 128, NEXP)
                   .transpose(1, 0, 2).reshape(128, NT * NEXP)),
        "bgate": f(inputs["L1_ffn_b_gate"].reshape(1, NEXP)),
        "w1": f(inputs["L1_ffn_W1"]),
        "b1": f(inputs["L1_ffn_b1"].reshape(NEXP, NMT, 128)
                .transpose(2, 0, 1).reshape(128, NEXP * NMT)),
        "w2": f(inputs["L1_ffn_W2"]),
        "b2": f(inputs["L1_ffn_b2"]),
    }


def kernel(**inputs):
    nc, h_in, _ = build()
    shared = _layout_inputs(inputs)
    x = inputs["L0_attn_x"]
    in_maps = []
    for b in range(8):
        im = dict(shared)
        im["x"] = np.ascontiguousarray(x[b])
        in_maps.append(im)
    res = run_bass_kernel_spmd(nc, in_maps, core_ids=list(range(8)))
    return np.stack([np.asarray(res.results[i]["y"]) for i in range(8)], axis=0)


# version 02
# speedup vs baseline: 1.0297x; 1.0297x over previous
import numpy as np
from contextlib import ExitStack
import concourse.bass as bass
import concourse.mybir as mybir
from concourse.bass_utils import run_bass_kernel_spmd

F32 = mybir.dt.float32
BF16 = mybir.dt.bfloat16
AF = mybir.ActivationFunctionType
ALU = mybir.AluOpType
AX = mybir.AxisListType

D, S = 1024, 2048
NT = D // 128
H, HD = 16, 64
NST = S // 128
CH = 256
NCH = S // CH
STATE, CHUNK = 64, 64
GRP = 512
NGRP = S // GRP
MG = GRP // CHUNK
NEXP, MLP = 8, 4096
NMT = MLP // 128
JB = 8
NJB = NMT // JB
TC = 256
NTC = S // TC
EPS0, EPS1 = 1e-5, 1e-6
SMSHIFT = -12.0
VROW = H * 68


class Sched:
    FIFO_SEM = {"t": "pe", "v": "sv", "s": "ss"}

    def __init__(self):
        self.ops = []
        self.count = {}
        self.waited = {}

    def n(self, sem):
        return self.count.get(sem, 0)

    def op(self, eng, fn, wait=(), inc=()):
        waits = []
        for w in wait:
            waits.append(w if isinstance(w, tuple) else (w, self.count.get(w, 0)))
        incs = []
        for s_, amt in inc:
            self.count[s_] = self.count.get(s_, 0) + amt
            incs.append((s_, amt))
        self.ops.append((eng, fn, waits, incs))

    def barrier(self, eng, sems):
        self.op(eng, None, wait=sems)

    def flush(self, blk, semh):
        ops, self.ops = self.ops, []
        for name in ("sy", "g", "t", "v", "s"):
            mine = [o for o in ops if o[0] == name]
            if not mine:
                continue
            waited = self.waited.setdefault(name, {})

            def body(eng, mine=mine, waited=waited):
                for _, fn, waits, incs in mine:
                    for s_, v_ in waits:
                        if v_ <= 0 or waited.get(s_, -1) >= v_:
                            continue
                        eng.wait_ge(semh[s_], v_)
                        waited[s_] = v_
                    if fn is None:
                        continue
                    instr = fn(eng)
                    for s_, amt in incs:
                        instr.then_inc(semh[s_], amt)

            getattr(blk, {"sy": "sync", "g": "gpsimd", "t": "tensor",
                          "v": "vector", "s": "scalar"}[name])(body)


def build(taps=(), phases=99, racecheck=False):
    taps = set(taps)
    nc = bass.Bass("TRN2", target_bir_lowering=False,
                   detect_race_conditions=racecheck)

    h_in = {}
    def din(name, shape, dt=F32):
        h_in[name] = nc.dram_tensor(name, list(shape), dt, kind="ExternalInput").ap()
        return h_in[name]

    x_h = din("x", [S, D])
    wqT_h = din("wqT", [D, D]); wkT_h = din("wkT", [D, D])
    wvT_h = din("wvT", [D, D]); woT_h = din("woT", [D, D])
    bq_h = din("bq", [128, NT]); bk_h = din("bk", [128, NT])
    bv_h = din("bv", [128, NT]); bo_h = din("bo", [128, NT])
    wrT_h = din("wrT", [D, D]); wkkT_h = din("wkkT", [D, D]); wvvT_h = din("wvvT", [D, D])
    mur_h = din("mur", [128, NT]); muk_h = din("muk", [128, NT])
    normw_h = din("normw", [128, NT])
    xprojT_h = din("xprojT", [D, 192])
    dtprojT_h = din("dtprojT", [64, D])
    dtprojb_h = din("dtprojb", [128, NT])
    alog_h = din("alog", [128, NT * STATE])
    dvec_h = din("dvec", [128, NT])
    outprojT_h = din("outprojT", [D, D])
    wgate_h = din("wgate", [128, NT * NEXP])
    bgate_h = din("bgate", [1, NEXP])
    w1_h = din("w1", [NEXP, D, MLP])
    b1_h = din("b1", [128, NEXP * NMT])
    w2_h = din("w2", [NEXP, MLP, D])
    b2_h = din("b2", [NEXP, D])

    y_h = nc.dram_tensor("y", [S, D], F32, kind="ExternalOutput").ap()
    tap_h = {}
    def TAP(name, shape, dt=F32):
        if name in taps and name not in tap_h:
            tap_h[name] = nc.dram_tensor("tap_" + name, list(shape), dt,
                                         kind="ExternalOutput").ap()
        return tap_h.get(name)

    sch = Sched()
    with ExitStack() as CTX:
        blk = CTX.enter_context(nc.Block())
        SEMH = {}
        for nm in ("ld", "ldx", "lset", "lffn", "lwv", "lqk", "lwo", "lop",
                   "lma", "lmb", "lbc", "gc", "pe", "sv", "ss", "st", "sta", "stb"):
            SEMH[nm] = CTX.enter_context(nc.semaphore(nm))

        def sb(stack, name, shape, dt):
            return stack.enter_context(nc.sbuf_tensor(name, shape, dt))
        def ps(stack, name, shape, dt=F32):
            return stack.enter_context(nc.psum_tensor(name, shape, dt))
        def flush():
            sch.flush(blk, SEMH)

        xT = sb(CTX, "xT", [128, NT * S], F32)
        xnbf = sb(CTX, "xnbf", [128, NT * S], BF16)
        ones1 = sb(CTX, "ones1", [1, 128], F32)
        onesc = sb(CTX, "onesc", [128, 1], F32)
        ident = sb(CTX, "ident", [128, 128], F32)
        bq_s = sb(CTX, "bq_s", [128, NT], F32); bk_s = sb(CTX, "bk_s", [128, NT], F32)
        bv_s = sb(CTX, "bv_s", [128, NT], F32); bo_s = sb(CTX, "bo_s", [128, NT], F32)
        bv_bf = sb(CTX, "bv_bf", [128, NT], BF16)
        bo2_s = sb(CTX, "bo2_s", [128, NT], F32)
        mur_s = sb(CTX, "mur_s", [128, NT], F32); muk_s = sb(CTX, "muk_s", [128, NT], F32)
        murc_s = sb(CTX, "murc_s", [128, NT], F32); mukc_s = sb(CTX, "mukc_s", [128, NT], F32)
        normw_s = sb(CTX, "normw_s", [128, NT], F32)
        dtprojb_s = sb(CTX, "dtprojb_s", [128, NT], F32)
        aneg_s = sb(CTX, "aneg_s", [128, NT * STATE], F32)
        dvec_s = sb(CTX, "dvec_s", [128, NT], F32)
        wgate_s = sb(CTX, "wgate_s", [128, NT * NEXP], F32)
        bgate_s = sb(CTX, "bgate_s", [1, NEXP], F32)
        bgb_s = sb(CTX, "bgb_s", [128, NEXP], F32)
        b1_s = sb(CTX, "b1_s", [128, NEXP * NMT], F32)
        b2_s = sb(CTX, "b2_s", [NEXP, D], BF16)
        xprojT_s = sb(CTX, "xprojT_s", [128, NT * 192], BF16)
        dtprojT_s = sb(CTX, "dtprojT_s", [64, D], BF16)
        ones8 = sb(CTX, "ones8", [128, NEXP], F32)
        prevcol = sb(CTX, "prevcol", [128, NT], F32)
        CONSTS = [float(EPS0), float(EPS1), 1.0, float(SMSHIFT)]
        cst = sb(CTX, "cst", [128, len(CONSTS)], F32)
        for ci, cv in enumerate(CONSTS):
            nc.const_aps.register(F32, cv, cst[:, ci:ci + 1])

        su = ExitStack()
        x_nat = sb(su, "x_nat", [128, (NST // 2) * D], F32)
        alog_s = sb(su, "alog_s", [128, NT * STATE], F32)
        psT = [ps(su, "psT0", [128, 512]), ps(su, "psT1", [128, 512])]
        psBG = ps(su, "psBG", [128, NEXP])

        for dst, src in [(bq_s, bq_h), (bk_s, bk_h), (bv_s, bv_h), (bo_s, bo_h),
                         (mur_s, mur_h), (muk_s, muk_h), (normw_s, normw_h),
                         (dtprojb_s, dtprojb_h), (alog_s, alog_h), (dvec_s, dvec_h),
                         (wgate_s, wgate_h), (bgate_s, bgate_h), (b1_s, b1_h)]:
            sch.op("sy", lambda sy, d=dst, s_=src: sy.dma_start(out=d[:], in_=s_[:]),
                   inc=[("ld", 16)])
        LD_SM = sch.n("ld")

        sch.op("g", lambda g: g.memset(ident[:], 1.0), inc=[("gc", 1)])
        sch.op("g", lambda g: g.affine_select(
            out=ident[:], in_=ident[:], pattern=[[1, 128]], compare_op=ALU.is_equal,
            fill=0.0, base=0, channel_multiplier=-1), wait=["gc"], inc=[("gc", 1)])
        GC_ID = sch.n("gc")
        sch.op("g", lambda g: g.dma_start(out=b2_s[:], in_=b2_h[:]), inc=[("lset", 16)])
        sch.op("g", lambda g: g.dma_start(
            out=xprojT_s[:].rearrange("p (dt r) -> p dt r", dt=NT),
            in_=xprojT_h[:].rearrange("(dt p) r -> p dt r", p=128)),
            inc=[("lset", 16)])
        sch.op("g", lambda g: g.dma_start(out=dtprojT_s[:], in_=dtprojT_h[:]),
               inc=[("lset", 16)])
        LDG_SET = sch.n("lset")

        sch.op("v", lambda v: v.memset(ones1[:], 1.0))
        sch.op("v", lambda v: v.memset(onesc[:], 1.0))
        sch.op("v", lambda v: v.memset(ones8[:], 1.0))
        for ci, cv in enumerate(CONSTS):
            sch.op("v", lambda v, ci=ci, cv=cv: v.memset(cst[:, ci:ci + 1], cv))
        sch.op("v", lambda v: v.memset(prevcol[:], 0.0), inc=[("sv", 1)])
        SV_ONES = sch.n("sv")
        sch.barrier("v", [("ld", LD_SM)])
        sch.op("v", lambda v: v.tensor_scalar(
            out=murc_s[:], in0=mur_s[:], scalar1=-1.0, scalar2=1.0,
            op0=ALU.mult, op1=ALU.add))
        sch.op("v", lambda v: v.tensor_scalar(
            out=mukc_s[:], in0=muk_s[:], scalar1=-1.0, scalar2=1.0,
            op0=ALU.mult, op1=ALU.add), inc=[("sv", 1)])
        sch.op("v", lambda v: v.tensor_copy(bv_bf[:], bv_s[:]),
               wait=[("ld", LD_SM)], inc=[("sv", 1)])
        sch.op("s", lambda s: s.activation(out=aneg_s[:], in_=alog_s[:], func=AF.Exp),
               wait=[("ld", LD_SM)], inc=[("ss", 1)])
        SS_ANEG = sch.n("ss")
        sch.op("v", lambda v: v.tensor_scalar_mul(aneg_s[:], aneg_s[:], -1.0),
               wait=[("ss", SS_ANEG)], inc=[("sv", 1)])
        sch.op("t", lambda t: t.matmul(psBG[:], ones1[:], bgate_s[:], start=True,
                                       stop=True),
               wait=[("sv", SV_ONES), ("ld", LD_SM)], inc=[("pe", 1)])
        sch.op("v", lambda v: v.tensor_copy(bgb_s[:], psBG[:]), wait=["pe"],
               inc=[("sv", 1)])

        SV_T0 = sch.n("sv")
        gi = 0
        for half in range(2):
            for i in range(NST // 2):
                sti = half * 8 + i
                sch.op("sy", lambda sy, i=i, sti=sti: sy.dma_start(
                    out=x_nat[:, i * D:(i + 1) * D],
                    in_=x_h[sti * 128:(sti + 1) * 128, :]), inc=[("ldx", 16)])
            LD_H = sch.n("ldx")
            for dt_j in range(NT):
                for sg in range(2):
                    bank = gi % 2
                    for q in range(4):
                        i = sg * 4 + q
                        w = [("gc", GC_ID), ("ldx", LD_H)]
                        if gi >= 2 and q == 0:
                            w.append(("sv", SV_T0 + gi - 1))
                        sch.op("t", lambda t, b=bank, q=q, i=i, dj=dt_j: t.transpose(
                            psT[b][:, q * 128:(q + 1) * 128],
                            x_nat[:, i * D + dj * 128: i * D + dj * 128 + 128],
                            ident[:]),
                            wait=w, inc=([("pe", 1)] if q == 3 else []))
                    so = half * 1024 + sg * 512
                    sch.op("v", lambda v, b=bank, dj=dt_j, so=so: v.tensor_copy(
                        xT[:, dj * S + so: dj * S + so + 512], psT[b][:]),
                        wait=["pe"], inc=[("sv", 1)])
                    gi += 1
            if half == 0:
                sch.barrier("sy", [("pe", sch.n("pe"))])
        flush()
        su.close()

        def emit_output():
            ou = ExitStack()
            ynat = [sb(ou, "ynat0", [128, D], F32), sb(ou, "ynat1", [128, D], F32)]
            psY = [ps(ou, "psY0", [128, 512]), ps(ou, "psY1", [128, 512]),
                   ps(ou, "psY2", [128, 512]), ps(ou, "psY3", [128, 512])]
            SV_ALL = sch.n("sv")
            SS_ALL = sch.n("ss")
            for sti in range(NST):
                yb = sti % 2
                for dt in range(NT):
                    bank = yb * 2 + dt // 4
                    w = []
                    if sti == 0 and dt == 0:
                        w = [("sv", SV_ALL), ("ss", SS_ALL)]
                    if sti >= 2 and dt == 0:
                        w = [("sta" if sti % 2 == 0 else "stb",
                              sch.n("sta" if sti % 2 == 0 else "stb"))]
                    sch.op("t", lambda t, dt=dt, sti=sti, bank=bank: t.transpose(
                        psY[bank][:, (dt % 4) * 128:(dt % 4 + 1) * 128],
                        xT[:, dt * S + sti * 128: dt * S + sti * 128 + 128],
                        ident[:]),
                        wait=w, inc=([("pe", 1)] if dt == NT - 1 else []))
                sch.op("v", lambda v, yb=yb: v.tensor_copy(
                    ynat[yb][:, 0:512], psY[yb * 2][:]), wait=["pe"])
                sch.op("v", lambda v, yb=yb: v.tensor_copy(
                    ynat[yb][:, 512:1024], psY[yb * 2 + 1][:]), inc=[("sv", 1)])
                sch.op("sy", lambda sy, sti=sti, yb=yb: sy.dma_start(
                    out=y_h[sti * 128:(sti + 1) * 128, :], in_=ynat[yb][:]),
                    wait=[("sv", sch.n("sv"))],
                    inc=[("sta" if sti % 2 == 0 else "stb", 16)])
            sch.barrier("sy", [("sta", sch.n("sta")), ("stb", sch.n("stb")),
                               ("st", sch.n("st"))])
            flush()
            ou.close()

        if phases < 1:
            emit_output()
            return nc, h_in, tap_h

        l0 = ExitStack()
        mrow = sb(l0, "mrow", [1, GRP], F32)
        rrow = sb(l0, "rrow", [1, GRP], F32)
        rta = sb(l0, "rta", [1, GRP], F32)
        rtb = sb(l0, "rtb", [1, GRP], F32)
        wr_sb = sb(l0, "wr_sb", [128, NT * D], BF16)
        wkk_sb = sb(l0, "wkk_sb", [128, NT * D], BF16)
        wvv_sb = sb(l0, "wvv_sb", [128, NT * D], BF16)
        xnf = sb(l0, "xnf", [128, NT * CH], F32)
        xr_bf = sb(l0, "xr_bf", [128, NT * CH], BF16)
        xk_bf = sb(l0, "xk_bf", [128, NT * CH], BF16)
        ktmp = sb(l0, "ktmp", [128, CH], F32)
        kact = sb(l0, "kact", [128, NT * CH], BF16)
        rsig = sb(l0, "rsig", [128, CH], F32)
        mixt = sb(l0, "mixt", [128, CH], F32)
        xsq = sb(l0, "xsq", [128, CH], F32)
        psRow = ps(l0, "psRow", [1, 2 * CH])
        psBC = ps(l0, "psBC", [128, 2 * CH])
        psMM = [ps(l0, "psMM0", [128, CH]), ps(l0, "psMM1", [128, CH])]

        PE_SETUP = sch.n("pe")
        for wdst, wsrc in ((wr_sb, wrT_h), (wkk_sb, wkkT_h), (wvv_sb, wvvT_h)):
            sch.op("g", lambda g, d=wdst, s_=wsrc: g.dma_start(
                out=d[:].rearrange("p (dt e) -> p dt e", dt=NT),
                in_=s_[:].rearrange("(dt p) e -> p dt e", p=128)),
                wait=[("pe", PE_SETUP)], inc=[("lffn", 16)])
        LDG_FFN = sch.n("lffn")

        def ln_stats(W, xsrc, xoff, mdst, rdst, eps, prow_a, prow_b, sqbuf, rta, rtb):
            sch.barrier("t", [("sv", sch.n("sv")), ("ss", sch.n("ss"))])
            for dt in range(NT):
                sch.op("t", lambda t, dt=dt: t.matmul(
                    prow_a[0:1, 0:W], onesc[:], xsrc[:, xoff(dt): xoff(dt) + W],
                    start=(dt == 0), stop=(dt == NT - 1)),
                    inc=([("pe", 1)] if dt == NT - 1 else []))
            PE_SQ = sch.n("pe")
            for dt in range(NT):
                wsq = [("sv", sch.n("sv"))] if dt == 0 else \
                      [("pe", PE_SQ + dt)]
                sch.op("s", lambda s, dt=dt: s.activation(
                    out=sqbuf[:, 0:W], in_=xsrc[:, xoff(dt): xoff(dt) + W],
                    func=AF.Square), wait=wsq, inc=[("ss", 1)])
                sch.op("t", lambda t, dt=dt: t.matmul(
                    prow_b[0:1, 0:W], onesc[:], sqbuf[:, 0:W],
                    start=(dt == 0), stop=(dt == NT - 1)), wait=["ss"],
                    inc=[("pe", 1)])
            if mdst is not None:
                sch.op("v", lambda v: v.tensor_scalar_mul(
                    mdst[0:1, 0:W], prow_a[0:1, 0:W], 1.0 / D),
                    wait=["pe", ("ss", sch.n("ss"))])
                sch.op("v", lambda v: v.tensor_scalar_mul(
                    rta[0:1, 0:W], prow_b[0:1, 0:W], 1.0 / D))
                sch.op("v", lambda v: v.tensor_mul(
                    rtb[0:1, 0:W], mdst[0:1, 0:W], mdst[0:1, 0:W]))
                sch.op("v", lambda v: v.tensor_sub(
                    rta[0:1, 0:W], rta[0:1, 0:W], rtb[0:1, 0:W]), inc=[("sv", 1)])
            else:
                sch.op("v", lambda v: v.tensor_scalar_mul(
                    rta[0:1, 0:W], prow_b[0:1, 0:W], 1.0 / D),
                    wait=["pe", ("ss", sch.n("ss"))], inc=[("sv", 1)])
            sch.op("s", lambda s: s.activation(
                out=rtb[0:1, 0:W], in_=rta[0:1, 0:W], func=AF.Ln,
                bias=float(eps)), wait=["sv"])
            sch.op("s", lambda s, rdst=rdst: s.activation(
                out=rdst[0:1, 0:W], in_=rtb[0:1, 0:W], func=AF.Exp,
                scale=-0.5), inc=[("ss", 1)])

        for c in range(NCH):
            co = c * CH
            ln_stats(CH, xT, lambda dt, co=co: dt * S + co, mrow, rrow, EPS0,
                     psRow[0:1, 0:CH], psRow[0:1, CH:2 * CH], xsq, rta, rtb)
            SS_R = sch.n("ss")
            sch.op("t", lambda t: t.matmul(
                psBC[:, 0:CH], ones1[:], mrow[0:1, 0:CH], start=True, stop=True),
                wait=[("ss", SS_R), ("sv", sch.n("sv"))])
            sch.op("t", lambda t: t.matmul(
                psBC[:, CH:2 * CH], ones1[:], rrow[0:1, 0:CH], start=True,
                stop=True), inc=[("pe", 1)])
            PE_BC = sch.n("pe")
            for dt in range(NT):
                sch.op("v", lambda v, dt=dt, co=co: v.scalar_tensor_tensor(
                    out=xnf[:, dt * CH:(dt + 1) * CH],
                    in0=xT[:, dt * S + co: dt * S + co + CH], scalar=1.0,
                    in1=psBC[:, 0:CH], op0=ALU.mult, op1=ALU.subtract),
                    wait=([("pe", PE_BC)] if dt == 0 else ()))
                sch.op("v", lambda v, dt=dt: v.tensor_mul(
                    xnf[:, dt * CH:(dt + 1) * CH], xnf[:, dt * CH:(dt + 1) * CH],
                    psBC[:, CH:2 * CH]))
                sch.op("v", lambda v, dt=dt, co=co: v.tensor_copy(
                    xnbf[:, dt * S + co: dt * S + co + CH],
                    xnf[:, dt * CH:(dt + 1) * CH]), inc=[("sv", 1)])
            for (dst, mu, muc) in ((xr_bf, mur_s, murc_s), (xk_bf, muk_s, mukc_s)):
                for dt in range(NT):
                    sch.op("v", lambda v, dt=dt, mu=mu: v.tensor_scalar(
                        out=mixt[:, 0:CH], in0=xnf[:, dt * CH:(dt + 1) * CH],
                        scalar1=mu[:, dt:dt + 1], scalar2=None, op0=ALU.mult))
                    sch.op("v", lambda v, dt=dt, dst=dst, muc=muc:
                        v.scalar_tensor_tensor(
                            out=dst[:, dt * CH + 1:(dt + 1) * CH],
                            in0=xnf[:, dt * CH: dt * CH + CH - 1],
                            scalar=muc[:, dt:dt + 1], in1=mixt[:, 1:CH],
                            op0=ALU.mult, op1=ALU.add))
                    sch.op("v", lambda v, dt=dt, dst=dst, muc=muc:
                        v.scalar_tensor_tensor(
                            out=dst[:, dt * CH: dt * CH + 1],
                            in0=prevcol[:, dt:dt + 1], scalar=muc[:, dt:dt + 1],
                            in1=mixt[:, 0:1], op0=ALU.mult, op1=ALU.add))
            for dt in range(NT):
                sch.op("v", lambda v, dt=dt: v.tensor_copy(
                    prevcol[:, dt:dt + 1], xnf[:, dt * CH + CH - 1: (dt + 1) * CH]),
                    inc=([("sv", 1)] if dt == NT - 1 else []))
            SV_MIX = sch.n("sv")
            for do in range(NT):
                b = do % 2
                for di in range(NT):
                    w = []
                    if di == 0:
                        w = [("sv", SV_MIX), ("ss", sch.n("ss")), ("lffn", LDG_FFN)]
                    sch.op("t", lambda t, do=do, di=di, b=b: t.matmul(
                        psMM[b][:], wkk_sb[:, di * D + do * 128: di * D + (do + 1) * 128],
                        xk_bf[:, di * CH:(di + 1) * CH],
                        start=(di == 0), stop=(di == NT - 1)), wait=w,
                        inc=([("pe", 1)] if di == NT - 1 else []))
                sch.op("s", lambda s, b=b: s.activation(
                    out=ktmp[:], in_=psMM[b][:], func=AF.Relu),
                    wait=["pe", ("sv", sch.n("sv"))], inc=[("ss", 1)])
                sch.op("v", lambda v, do=do: v.tensor_mul(
                    kact[:, do * CH:(do + 1) * CH], ktmp[:], ktmp[:]),
                    wait=["ss"], inc=[("sv", 1)])
            for do in range(NT):
                b = do % 2
                for di in range(NT):
                    w = [("ss", sch.n("ss")), ("sv", sch.n("sv"))] if di == 0 else ()
                    sch.op("t", lambda t, do=do, di=di, b=b: t.matmul(
                        psMM[b][:], wr_sb[:, di * D + do * 128: di * D + (do + 1) * 128],
                        xr_bf[:, di * CH:(di + 1) * CH],
                        start=(di == 0), stop=(di == NT - 1)), wait=w,
                        inc=([("pe", 1)] if di == NT - 1 else []))
                sch.op("s", lambda s, b=b: s.activation(
                    out=rsig[:], in_=psMM[b][:], func=AF.Sigmoid),
                    wait=["pe", ("sv", sch.n("sv"))], inc=[("ss", 1)])
                SS_RS = sch.n("ss")
                b2_ = (do + 1) % 2
                for di in range(NT):
                    w = [("ss", SS_RS), ("sv", sch.n("sv"))] if di == 0 else ()
                    sch.op("t", lambda t, do=do, di=di, b=b2_: t.matmul(
                        psMM[b][:], wvv_sb[:, di * D + do * 128: di * D + (do + 1) * 128],
                        kact[:, di * CH:(di + 1) * CH],
                        start=(di == 0), stop=(di == NT - 1)), wait=w,
                        inc=([("pe", 1)] if di == NT - 1 else []))
                sch.op("v", lambda v, b=b2_: v.tensor_mul(
                    mixt[:], psMM[b][:], rsig[:]), wait=["pe"])
                sch.op("v", lambda v, do=do, co=co: v.tensor_add(
                    xT[:, do * S + co: do * S + co + CH],
                    xT[:, do * S + co: do * S + co + CH], mixt[:]), inc=[("sv", 1)])
        flush()
        l0.close()

        if TAP("xnbf", [128, NT * S], BF16) is not None:
            sch.op("sy", lambda sy: sy.dma_start(out=tap_h["xnbf"][:], in_=xnbf[:]),
                   wait=[("sv", sch.n("sv"))], inc=[("st", 16)])
        if TAP("xl0f", [128, NT * S]) is not None:
            sch.op("sy", lambda sy: sy.dma_start(out=tap_h["xl0f"][:], in_=xT[:]),
                   wait=[("sv", sch.n("sv"))], inc=[("st", 16)])
        flush()
        if phases < 2:
            emit_output()
            return nc, h_in, tap_h

        vp = ExitStack()
        VB = sb(vp, "VB", [128, NST * VROW], BF16)
        vpw = ExitStack()
        wvslc = [sb(vpw, "wvslc0", [128, NT * 256], BF16),
                 sb(vpw, "wvslc1", [128, NT * 256], BF16)]
        psV = [ps(vpw, "psV0", [128, 256]), ps(vpw, "psV1", [128, 256])]

        SV_L0 = sch.n("sv")
        sch.op("v", lambda v: v.memset(VB[:], 0.0), wait=[("sv", SV_L0)],
               inc=[("sv", 1)])
        sch.op("v", lambda v: v.memset(
            VB[:].rearrange("p (t h c) -> p t h c", t=NST, h=H, c=68)[:, :, :, 64:65],
            1.0), inc=[("sv", 1)])
        SV_VONES = sch.n("sv")

        for ec in range(4):
            buf = ec % 2
            wv_w = [("gc", sch.n("gc"))]
            if ec == 0:
                wv_w += [("sv", SV_L0), ("ss", sch.n("ss")), ("pe", sch.n("pe"))]
            sch.op("g", lambda g, ec=ec, b=buf: g.dma_start(
                out=wvslc[b][:].rearrange("p (dt e) -> p dt e", dt=NT),
                in_=wvT_h[:, ec * 256:(ec + 1) * 256].rearrange(
                    "(dt p) e -> p dt e", p=128)),
                wait=wv_w, inc=[("lwv", 16)])
            LDW = sch.n("lwv")
            for sti in range(NST):
                b2_ = sti % 2
                for di in range(NT):
                    w = []
                    if di == 0:
                        w = [("lwv", LDW), ("sv", sch.n("sv"))]
                        if sti == 0 and ec == 0:
                            w.append(("sv", SV_VONES))
                    sch.op("t", lambda t, b=buf, b2_=b2_, di=di, sti=sti: t.matmul(
                        psV[b2_][:], xnbf[:, di * S + sti * 128: di * S + sti * 128 + 128],
                        wvslc[b][:, di * 256:(di + 1) * 256],
                        start=(di == 0), stop=(di == NT - 1)),
                        wait=w, inc=([("pe", 1)] if di == NT - 1 else []))
                sch.op("v", lambda v, sti=sti, ec=ec, b2_=b2_: v.tensor_copy(
                    VB[:, sti * VROW + (ec * 4) * 68: sti * VROW + (ec * 4 + 4) * 68]
                      .rearrange("p (h c) -> p h c", h=4, c=68)[:, :, 0:64],
                    psV[b2_][:].rearrange("p (h c) -> p h c", h=4, c=64)),
                    wait=["pe"], inc=[("sv", 1)])
            sch.op("g", lambda g: g.engine_nop(), wait=[("sv", sch.n("sv"))],
                   inc=[("gc", 1)])
        flush()
        vpw.close()
        if TAP("VB", [128, NST * VROW], BF16) is not None:
            sch.op("sy", lambda sy: sy.dma_start(out=tap_h["VB"][:], in_=VB[:]),
                   wait=[("sv", sch.n("sv"))], inc=[("st", 16)])
            flush()

        at = ExitStack()
        QTh = sb(at, "QTh", [64, S], BF16)
        KTh = sb(at, "KTh", [64, S], BF16)
        expS = sb(at, "expS", [128, NST * 512], BF16)
        rb_sb = sb(at, "rb_sb", [128, 512], F32)
        ctxf = sb(at, "ctxf", [128, NT * S], BF16)
        recipr = sb(at, "recipr", [1, 512], F32)
        wqkslc = [sb(at, "wqks0", [128, NT * 128], BF16)]
        woslc = [sb(at, "woslc0", [128, NT * 128], BF16)]
        psS = [ps(at, "psS0", [128, 512]), ps(at, "psS1", [128, 512]),
               ps(at, "psS2", [128, 512])]
        psU = ps(at, "psU", [128, 512])
        psR = ps(at, "psR", [128, 512])
        psQK = ps(at, "psQK", [128, 512])

        for hh in range(H):
            et, sub = divmod(hh, 2)
            buf = 0
            sch.op("g", lambda g, hh=hh, b=buf: g.dma_start(
                out=wqkslc[b][:, 0:NT * 64].rearrange("p (dt e) -> p dt e", dt=NT),
                in_=wqT_h[:, hh * 64:(hh + 1) * 64].rearrange(
                    "(dt p) e -> p dt e", p=128)),
                wait=[("gc", sch.n("gc"))], inc=[("lqk", 16)])
            sch.op("g", lambda g, hh=hh, b=buf: g.dma_start(
                out=wqkslc[b][:, NT * 64:NT * 128].rearrange(
                    "p (dt e) -> p dt e", dt=NT),
                in_=wkT_h[:, hh * 64:(hh + 1) * 64].rearrange(
                    "(dt p) e -> p dt e", p=128)),
                inc=[("lqk", 16)])
            LDW = sch.n("lqk")
            for (dst, wof, bias) in ((QTh, 0, bq_s), (KTh, NT * 64, bk_s)):
                for sc in range(4):
                    for di in range(NT):
                        w = [("lqk", LDW), ("ss", sch.n("ss"))] if di == 0 else ()
                        sch.op("t", lambda t, b=buf, wof=wof, di=di, sc=sc: t.matmul(
                            psQK[0:64, :],
                            wqkslc[b][:, wof + di * 64: wof + (di + 1) * 64],
                            xnbf[:, di * S + sc * 512: di * S + (sc + 1) * 512],
                            start=(di == 0), stop=(di == NT - 1)),
                            wait=w, inc=([("pe", 1)] if di == NT - 1 else []))
                    sch.op("s", lambda s, dst=dst, sc=sc, bias=bias, et=et, sub=sub:
                        s.activation(
                            out=dst[:, sc * 512:(sc + 1) * 512], in_=psQK[0:64, :],
                            func=AF.Identity,
                            bias=bias[sub * 64:sub * 64 + 64, et:et + 1]),
                        wait=["pe"], inc=[("ss", 1)])
            SS_QK = sch.n("ss")
            sch.op("g", lambda g: g.engine_nop(), wait=[("ss", SS_QK)],
                   inc=[("gc", 1)])
            for qc in range(4):
                qo = qc * 512
                for kt in range(NST):
                    b = kt % 3
                    w = []
                    if hh == 0 and qc == 0 and kt == 0:
                        w = [("ss", SS_QK), ("sv", sch.n("sv"))]
                    elif kt >= 3 or not (hh == 0 and qc == 0):
                        w = [("ss", sch.n("ss") - 2)]
                    sch.op("t", lambda t, b=b, kt=kt, qo=qo: t.matmul(
                        psS[b][:], KTh[:, kt * 128:(kt + 1) * 128],
                        QTh[:, qo:qo + 512], start=True, stop=True),
                        wait=(w + [("ss", SS_QK)] if kt == 0 else w),
                        inc=[("pe", 1)])
                    sch.op("s", lambda s, b=b, kt=kt: s.activation(
                        out=expS[:, kt * 512:(kt + 1) * 512], in_=psS[b][:],
                        func=AF.Exp, scale=0.125, bias=SMSHIFT),
                        wait=["pe"], inc=[("ss", 1)])
                SS_EXP = sch.n("ss")
                if sub == 0:
                    for kt in range(NST):
                        w = [("ss", SS_EXP), ("sv", sch.n("sv"))] if kt == 0 else ()
                        sch.op("t", lambda t, kt=kt, hh=hh: t.matmul(
                            psU[0:65, :],
                            VB[:, kt * VROW + hh * 68: kt * VROW + hh * 68 + 65],
                            expS[:, kt * 512:(kt + 1) * 512],
                            start=(kt == 0), stop=(kt == NST - 1)), wait=w,
                            inc=([("pe", 1)] if kt == NST - 1 else []))
                    sch.op("v", lambda v: v.reciprocal(recipr[:], psU[64:65, :]),
                           wait=["pe"], inc=[("sv", 1)])
                else:
                    for kt in range(NST):
                        w = [("ss", SS_EXP), ("sv", sch.n("sv"))] if kt == 0 else ()
                        sch.op("t", lambda t, kt=kt, hh=hh: t.matmul(
                            psU[64:128, :],
                            VB[:, kt * VROW + hh * 68: kt * VROW + hh * 68 + 64],
                            expS[:, kt * 512:(kt + 1) * 512],
                            start=(kt == 0), stop=(kt == NST - 1),
                            tile_position=(0, 64)), wait=w,
                            inc=([("pe", 1)] if kt == NST - 1 else []))
                    for kt in range(NST):
                        sch.op("t", lambda t, kt=kt, hh=hh: t.matmul(
                            psU[0:1, :],
                            VB[:, kt * VROW + hh * 68 + 64: kt * VROW + hh * 68 + 65],
                            expS[:, kt * 512:(kt + 1) * 512],
                            start=(kt == 0), stop=(kt == NST - 1)),
                            inc=([("pe", 1)] if kt == NST - 1 else []))
                    sch.op("v", lambda v: v.reciprocal(recipr[:], psU[0:1, :]),
                           wait=["pe"], inc=[("sv", 1)])
                sch.op("t", lambda t: t.matmul(
                    psR[:], ones1[:], recipr[:], start=True, stop=True),
                    wait=["sv"], inc=[("pe", 1)])
                sch.op("s", lambda s: s.activation(
                    out=rb_sb[:], in_=psR[:], func=AF.Copy),
                    wait=["pe", ("sv", sch.n("sv"))],
                    inc=[("ss", 1)])
                sch.op("v", lambda v, et=et, sub=sub, qo=qo: v.tensor_mul(
                    ctxf[sub * 64:sub * 64 + 64, et * S + qo: et * S + qo + 512],
                    psU[sub * 64:sub * 64 + 64, :],
                    rb_sb[sub * 64:sub * 64 + 64, :]),
                    wait=[("ss", sch.n("ss"))], inc=[("sv", 1)])
        SV_CTX = sch.n("sv")
        for et in range(NT):
            buf = 0
            sch.op("g", lambda g, et=et, b=buf: g.dma_start(
                out=woslc[b][:].rearrange("p (dt e) -> p dt e", dt=NT),
                in_=woT_h[:, et * 128:(et + 1) * 128].rearrange(
                    "(dt p) e -> p dt e", p=128)),
                wait=[("gc", sch.n("gc"))], inc=[("lwo", 16)])
            LDW = sch.n("lwo")
            for di in range(NT):
                sch.op("t", lambda t, b=buf, di=di, et=et: t.matmul(
                    psS[2][:, et:et + 1],
                    woslc[b][:, di * 128:(di + 1) * 128],
                    bv_bf[:, di:di + 1], start=(di == 0), stop=(di == NT - 1)),
                    wait=([("lwo", LDW), ("sv", SV_CTX)] if di == 0 else ()),
                    inc=([("pe", 1)] if di == NT - 1 else []))
            sch.op("v", lambda v, et=et: v.tensor_add(
                bo2_s[:, et:et + 1], psS[2][:, et:et + 1], bo_s[:, et:et + 1]),
                wait=["pe"], inc=[("sv", 1)])
            SV_BO = sch.n("sv")
            for qc in range(4):
                b2_ = qc % 2
                for di in range(NT):
                    w = [("sv", SV_BO)] if di == 0 else ()
                    sch.op("t", lambda t, b=buf, b2_=b2_, di=di, qc=qc: t.matmul(
                        psS[b2_][:], woslc[b][:, di * 128:(di + 1) * 128],
                        ctxf[:, di * S + qc * 512: di * S + (qc + 1) * 512],
                        start=(di == 0), stop=(di == NT - 1)),
                        wait=w, inc=([("pe", 1)] if di == NT - 1 else []))
                sch.op("v", lambda v, et=et, qc=qc, b2_=b2_:
                    v.scalar_tensor_tensor(
                        out=xT[:, et * S + qc * 512: et * S + (qc + 1) * 512],
                        in0=psS[b2_][:], scalar=bo2_s[:, et:et + 1],
                        in1=xT[:, et * S + qc * 512: et * S + (qc + 1) * 512],
                        op0=ALU.add, op1=ALU.add), wait=["pe"], inc=[("sv", 1)])
            sch.op("g", lambda g: g.engine_nop(), wait=[("sv", sch.n("sv"))],
                   inc=[("gc", 1)])
        if TAP("ctxf", [128, NT * S], BF16) is not None:
            sch.op("sy", lambda sy: sy.dma_start(out=tap_h["ctxf"][:], in_=ctxf[:]),
                   wait=[("sv", sch.n("sv"))], inc=[("st", 16)])
        if TAP("QTh", [64, S], BF16) is not None:
            sch.op("sy", lambda sy: sy.dma_start(out=tap_h["QTh"][:], in_=QTh[:]),
                   wait=[("sv", sch.n("sv"))], inc=[("st", 16)])
        if TAP("KTh", [64, S], BF16) is not None:
            sch.op("sy", lambda sy: sy.dma_start(out=tap_h["KTh"][:], in_=KTh[:]),
                   wait=[("sv", sch.n("sv"))], inc=[("st", 16)])
        if TAP("expS", [128, NST * 512], BF16) is not None:
            sch.op("sy", lambda sy: sy.dma_start(out=tap_h["expS"][:], in_=expS[:]),
                   wait=[("sv", sch.n("sv"))], inc=[("st", 16)])
        flush()
        at.close()
        vp.close()
        if TAP("xl0", [128, NT * S]) is not None:
            sch.op("sy", lambda sy: sy.dma_start(out=tap_h["xl0"][:], in_=xT[:]),
                   wait=[("sv", sch.n("sv"))], inc=[("st", 16)])
            flush()
        if phases < 3:
            emit_output()
            return nc, h_in, tap_h

        l1c = ExitStack()
        wselT_s = sb(l1c, "wselT_s", [NEXP, S], F32)
        wselT_bf = sb(l1c, "wselT_bf", [NEXP, S], BF16)
        l1 = ExitStack()
        mrow = sb(l1, "mrow1", [1, GRP], F32)
        rrow = sb(l1, "rrow1", [1, GRP], F32)
        rta = sb(l1, "rta1", [1, GRP], F32)
        rtb = sb(l1, "rtb1", [1, GRP], F32)
        xnf1 = sb(l1, "xnf1", [128, NT * GRP], F32)
        deltaf = xnf1
        Uf = sb(l1, "Uf", [128, NT * GRP], F32)
        xn2f = sb(l1, "xn2f", [128, NT * GRP], F32)
        Yf = xn2f
        xn2bf = sb(l1, "xn2bf", [128, NT * GRP], BF16)
        dlo_bf = sb(l1, "dlo_bf", [64, GRP], BF16)
        BCrow = sb(l1, "BCrow", [128, GRP], F32)
        Bbig = [sb(l1, "Bbig0", [128, GRP], F32), sb(l1, "Bbig1", [128, GRP], F32)]
        Cbig = [sb(l1, "Cbig0", [128, GRP], F32), sb(l1, "Cbig1", [128, GRP], F32)]
        dAb = [sb(l1, "dA0", [128, GRP], F32), sb(l1, "dA1", [128, GRP], F32)]
        wSb = sb(l1, "wS0", [128, GRP], F32)
        hSb = sb(l1, "hS0", [128, GRP], F32)
        tmpS = wSb
        oin_bf = xn2bf
        opslc = [sb(l1, "opslc0", [128, NT * 128], BF16)]
        sp1 = sb(l1, "sp1", [128, GRP], F32)
        lgt = sb(l1, "lgt", [128, NEXP], F32)
        gex = sb(l1, "gex", [128, NEXP], F32)
        gex2 = sb(l1, "gex2", [128, NEXP], F32)
        gmx = sb(l1, "gmx", [128, 4], F32)
        geq = sb(l1, "geq", [128, NEXP], F32)
        ginc = sb(l1, "ginc", [128, NEXP], F32)
        gq = sb(l1, "gq", [128, 12], F32)
        wsel = sb(l1, "wsel", [128, NEXP], F32)
        psBC1 = ps(l1, "psBC1", [128, 2 * GRP])
        psG = ps(l1, "psG", [128, NEXP])
        psWT = ps(l1, "psWT", [NEXP, 128])
        psP = [ps(l1, "psP0", [128, GRP]), ps(l1, "psP1", [128, GRP])]


        for gi in range(NGRP):
            co = gi * GRP
            ln_stats(GRP, xT, lambda dt, co=co: dt * S + co, mrow, rrow, EPS0,
                     psP[0][0:1, :], psP[1][0:1, :], sp1, rta, rtb)
            SS_R = sch.n("ss")
            sch.op("t", lambda t: t.matmul(
                psBC1[:, 0:GRP], ones1[:], mrow[0:1, 0:GRP], start=True,
                stop=True), wait=[("ss", SS_R), ("sv", sch.n("sv"))])
            sch.op("t", lambda t: t.matmul(
                psBC1[:, GRP:2 * GRP], ones1[:], rrow[0:1, 0:GRP], start=True,
                stop=True), inc=[("pe", 1)])
            PE_BC1 = sch.n("pe")
            for dt in range(NT):
                sch.op("v", lambda v, dt=dt, co=co: v.scalar_tensor_tensor(
                    out=xnf1[:, dt * GRP:(dt + 1) * GRP],
                    in0=xT[:, dt * S + co: dt * S + co + GRP], scalar=1.0,
                    in1=psBC1[:, 0:GRP], op0=ALU.mult, op1=ALU.subtract),
                    wait=([("pe", PE_BC1)] if dt == 0 else ()))
                sch.op("v", lambda v, dt=dt: v.tensor_mul(
                    xnf1[:, dt * GRP:(dt + 1) * GRP], xnf1[:, dt * GRP:(dt + 1) * GRP],
                    psBC1[:, GRP:2 * GRP]))
                sch.op("v", lambda v, dt=dt, co=co: v.tensor_copy(
                    xnbf[:, dt * S + co: dt * S + co + GRP],
                    xnf1[:, dt * GRP:(dt + 1) * GRP]), inc=[("sv", 1)])
            SV_XN1 = sch.n("sv")
            for sti in range(GRP // 128):
                for dt in range(NT):
                    w = [("sv", SV_XN1 if sti == 0 else sch.n("sv"))] \
                        if dt == 0 else ()
                    sch.op("t", lambda t, sti=sti, dt=dt: t.matmul(
                        psG[:], xnf1[:, dt * GRP + sti * 128: dt * GRP + sti * 128 + 128],
                        wgate_s[:, dt * NEXP:(dt + 1) * NEXP],
                        start=(dt == 0), stop=(dt == NT - 1)),
                        wait=w, inc=([("pe", 1)] if dt == NT - 1 else []))
                def vsem(fn, wait=()):
                    sch.op("v", fn, wait=list(wait) + [("sv", sch.n("sv"))],
                           inc=[("sv", 1)])
                vsem(lambda v: v.tensor_add(lgt[:], psG[:], bgb_s[:]), wait=["pe"])
                vsem(lambda v: v.reduce_max(gmx[:, 0:1], lgt[:], axis=AX.X))
                vsem(lambda v: v.tensor_scalar(
                    out=gex[:], in0=lgt[:], scalar1=gmx[:, 0:1], scalar2=None,
                    op0=ALU.subtract))
                sch.op("s", lambda s: s.activation(out=gex2[:], in_=gex[:],
                                                   func=AF.Exp),
                       wait=[("sv", sch.n("sv"))], inc=[("ss", 1)])
                vsem(lambda v: v.reduce_sum(gmx[:, 1:2], gex2[:], axis=AX.X),
                     wait=[("ss", sch.n("ss"))])
                vsem(lambda v: v.tensor_scalar(
                    out=geq[:], in0=lgt[:], scalar1=gmx[:, 0:1], scalar2=None,
                    op0=ALU.is_ge))
                vsem(lambda v: v.tensor_scalar(
                    out=ginc[:], in0=geq[:], scalar1=-1.0, scalar2=1.0,
                    op0=ALU.mult, op1=ALU.add))
                sch.op("s", lambda s: s.activation(
                    out=gex[:], in_=ones8[:], func=AF.Identity,
                    scale=gmx[:, 1:2]), wait=[("sv", sch.n("sv"))], inc=[("ss", 1)])
                vsem(lambda v: v.reciprocal(gex2[:], gex[:]),
                     wait=[("ss", sch.n("ss"))])
                vsem(lambda v: v.memset(gq[:, 0:2], 1.0))
                vsem(lambda v: v.tensor_tensor_scan(
                    out=gq[:, 2:10], data0=ginc[:], data1=ones8[:], initial=1.0,
                    op0=ALU.mult, op1=ALU.mult))
                vsem(lambda v: v.tensor_mul(ginc[:], geq[:], gq[:, 1:9]))
                vsem(lambda v: v.tensor_mul(wsel[:], ginc[:], gex2[:]))
                if gi == 0 and sti == 0 and TAP("wsel0", [128, NEXP]) is not None:
                    sch.op("sy", lambda sy: sy.dma_start(
                        out=tap_h["wsel0"][:], in_=wsel[:]),
                        wait=[("sv", sch.n("sv"))], inc=[("st", 16)])
                if gi == 0 and sti == 0 and TAP("glgt0", [128, NEXP]) is not None:
                    sch.op("sy", lambda sy: sy.dma_start(
                        out=tap_h["glgt0"][:], in_=lgt[:]),
                        wait=[("sv", sch.n("sv"))], inc=[("st", 16)])
                if gi == 0 and sti == 0 and TAP("ggmx0", [128, 4]) is not None:
                    sch.op("sy", lambda sy: sy.dma_start(
                        out=tap_h["ggmx0"][:], in_=gmx[:]),
                        wait=[("sv", sch.n("sv"))], inc=[("st", 16)])
                if gi == 0 and sti == 0 and TAP("grc0", [128, NEXP]) is not None:
                    sch.op("sy", lambda sy: sy.dma_start(
                        out=tap_h["grc0"][:], in_=gex2[:]),
                        wait=[("sv", sch.n("sv"))], inc=[("st", 16)])
                if gi == 0 and sti == 0 and TAP("gq0", [128, 12]) is not None:
                    sch.op("sy", lambda sy: sy.dma_start(
                        out=tap_h["gq0"][:], in_=gq[:]),
                        wait=[("sv", sch.n("sv"))], inc=[("st", 16)])
                if gi == 0 and sti == 0 and taps:
                    sch.barrier("v", [("st", sch.n("st"))])
                    sch.barrier("s", [("st", sch.n("st"))])
                sch.op("t", lambda t: t.transpose(psWT[:], wsel[:], ident[:]),
                       wait=[("sv", sch.n("sv")), ("st", sch.n("st"))],
                       inc=[("pe", 1)])
                sch.op("v", lambda v, co=co, sti=sti: v.tensor_copy(
                    wselT_s[:, co + sti * 128: co + sti * 128 + 128], psWT[:]),
                    wait=["pe"])
                sch.op("v", lambda v, co=co, sti=sti: v.tensor_copy(
                    wselT_bf[:, co + sti * 128: co + sti * 128 + 128], psWT[:]),
                    inc=[("sv", 1)])
            ln_stats(GRP, xnf1, lambda dt: dt * GRP, None, rrow, EPS1,
                     psP[0][0:1, :], psP[1][0:1, :], sp1, rta, rtb)
            SS_R2 = sch.n("ss")
            sch.op("t", lambda t: t.matmul(
                psBC1[:, 0:GRP], ones1[:], rrow[0:1, 0:GRP], start=True,
                stop=True), wait=[("ss", SS_R2), ("sv", sch.n("sv"))], inc=[("pe", 1)])
            PE_R2 = sch.n("pe")
            for dt in range(NT):
                sch.op("v", lambda v, dt=dt: v.scalar_tensor_tensor(
                    out=xn2f[:, dt * GRP:(dt + 1) * GRP],
                    in0=xnf1[:, dt * GRP:(dt + 1) * GRP],
                    scalar=normw_s[:, dt:dt + 1], in1=psBC1[:, 0:GRP],
                    op0=ALU.mult, op1=ALU.mult),
                    wait=([("pe", PE_R2)] if dt == 0 else ()))
                sch.op("v", lambda v, dt=dt: v.tensor_copy(
                    xn2bf[:, dt * GRP:(dt + 1) * GRP], xn2f[:, dt * GRP:(dt + 1) * GRP]),
                    inc=[("sv", 1)])
            SV_XN2 = sch.n("sv")
            for mt in range(2):
                msz = 128 if mt == 0 else 64
                for dt in range(NT):
                    w = [("sv", SV_XN2), ("lset", LDG_SET)] if (mt == 0 and dt == 0) else ()
                    sch.op("t", lambda t, mt=mt, dt=dt, msz=msz: t.matmul(
                        psP[mt][0:msz, :],
                        xprojT_s[:, dt * 192 + mt * 128: dt * 192 + mt * 128 + msz],
                        xn2bf[:, dt * GRP:(dt + 1) * GRP],
                        start=(dt == 0), stop=(dt == NT - 1)),
                        wait=w, inc=([("pe", 1)] if dt == NT - 1 else []))
            sch.op("v", lambda v: v.tensor_copy(dlo_bf[:], psP[0][0:64, :]),
                   wait=["pe"])
            sch.op("v", lambda v: v.tensor_copy(BCrow[64:128, :], psP[0][64:128, :]))
            sch.op("v", lambda v: v.tensor_copy(BCrow[0:64, :], psP[1][0:64, :]),
                   inc=[("sv", 1)])
            SV_PROJ = sch.n("sv")
            for dt in range(NT):
                b = dt % 2
                sch.op("t", lambda t, dt=dt, b=b: t.matmul(
                    psP[b][:], dtprojT_s[:, dt * 128:(dt + 1) * 128], dlo_bf[:],
                    start=True, stop=True),
                    wait=([("sv", SV_PROJ)] if dt <= 1 else [("ss", sch.n("ss") - 1)]),
                    inc=[("pe", 1)])
                sch.op("s", lambda s, dt=dt, b=b: s.activation(
                    out=sp1[:], in_=psP[b][:], func=AF.Abs,
                    bias=dtprojb_s[:, dt:dt + 1]),
                    wait=["pe", ("sv", sch.n("sv"))])
                sch.op("s", lambda s, dt=dt, b=b: s.activation(
                    out=deltaf[:, dt * GRP:(dt + 1) * GRP], in_=psP[b][:],
                    func=AF.Relu, bias=dtprojb_s[:, dt:dt + 1]))
                sch.op("s", lambda s: s.activation(
                    out=sp1[:], in_=sp1[:], func=AF.Exp, scale=-1.0))
                sch.op("s", lambda s: s.activation(
                    out=sp1[:], in_=sp1[:], func=AF.Ln, bias=1.0), inc=[("ss", 1)])
                sch.op("v", lambda v, dt=dt: v.tensor_add(
                    deltaf[:, dt * GRP:(dt + 1) * GRP],
                    deltaf[:, dt * GRP:(dt + 1) * GRP], sp1[:]),
                    wait=[("ss", sch.n("ss"))])
                sch.op("v", lambda v, dt=dt: v.tensor_mul(
                    Uf[:, dt * GRP:(dt + 1) * GRP], deltaf[:, dt * GRP:(dt + 1) * GRP],
                    xn2f[:, dt * GRP:(dt + 1) * GRP]), inc=[("sv", 1)])
            for dt in range(NT):
                sch.op("v", lambda v, dt=dt: v.memset(
                    deltaf[:, dt * GRP:(dt + 1) * GRP].rearrange(
                        "p (m t) -> p m t", m=MG, t=CHUNK)[:, :, 0:1],
                    1e30), inc=([("sv", 1)] if dt == NT - 1 else []))
            SV_U = sch.n("sv")
            SS0 = sch.n("ss")
            SV0 = sch.n("sv")
            PE0 = sch.n("pe")
            LBC0 = sch.n("lbc")
            for n in range(STATE):
                nb = n % 2
                wd = [("sv", SV_U)] if n < 2 else [("sv", SV0 + 8 * (n - 1))]
                sch.op("sy", lambda sy, n=n, nb=nb: sy.dma_start(
                    out=Bbig[nb][:],
                    in_=BCrow[64 + n:64 + n + 1, :].unsqueeze(1)
                        .broadcast_to([1, 128, GRP])), wait=wd, inc=[("lbc", 16)])
                sch.op("sy", lambda sy, n=n, nb=nb: sy.dma_start(
                    out=Cbig[nb][:],
                    in_=BCrow[n:n + 1, :].unsqueeze(1)
                        .broadcast_to([1, 128, GRP])), inc=[("lbc", 16)])
                LBC_N = sch.n("lbc")
                for dt in range(NT):
                    it = n * NT + dt
                    db = dt % 2
                    wda = [("sv", SV0 + it - 1)] if it >= 2 else [("sv", SV_U)]
                    sch.op("s", lambda s, dt=dt, n=n, db=db: s.activation(
                        out=dAb[db][:], in_=deltaf[:, dt * GRP:(dt + 1) * GRP],
                        func=AF.Exp,
                        scale=aneg_s[:, dt * STATE + n: dt * STATE + n + 1]),
                        wait=wda, inc=[("ss", 1)])
                    SS_DA = SS0 + 8 * n + 1 + dt
                    sch.op("v", lambda v, dt=dt, db=db, nb=nb: v.tensor_mul(
                        wSb[:], Uf[:, dt * GRP:(dt + 1) * GRP], Bbig[nb][:]),
                        wait=[("lbc", LBC_N), ("ss", SS_DA)])
                    sch.op("v", lambda v, db=db: v.tensor_tensor_scan(
                        out=hSb[:], data0=dAb[db][:], data1=wSb[:],
                        initial=0.0, op0=ALU.mult, op1=ALU.add))
                    if n == 0:
                        sch.op("v", lambda v, dt=dt, nb=nb: v.tensor_mul(
                            Yf[:, dt * GRP:(dt + 1) * GRP], hSb[:], Cbig[nb][:]),
                            inc=[("sv", 1)])
                    else:
                        sch.op("v", lambda v, nb=nb: v.tensor_mul(
                            tmpS[:], hSb[:], Cbig[nb][:]))
                        sch.op("v", lambda v, dt=dt: v.tensor_add(
                            Yf[:, dt * GRP:(dt + 1) * GRP],
                            Yf[:, dt * GRP:(dt + 1) * GRP], tmpS[:]),
                            inc=[("sv", 1)])
            SV_SCAN = sch.n("sv")
            for dt in range(NT):
                sch.op("v", lambda v, dt=dt, co=co: v.scalar_tensor_tensor(
                    out=oin_bf[:, dt * GRP:(dt + 1) * GRP],
                    in0=xnbf[:, dt * S + co: dt * S + co + GRP],
                    scalar=dvec_s[:, dt:dt + 1],
                    in1=Yf[:, dt * GRP:(dt + 1) * GRP], op0=ALU.mult, op1=ALU.add),
                    inc=([("sv", 1)] if dt == NT - 1 else []))
            SV_OIN = sch.n("sv")
            for et in range(NT):
                buf = 0
                sch.op("g", lambda g, et=et, b=buf: g.dma_start(
                    out=opslc[b][:].rearrange("p (dt e) -> p dt e", dt=NT),
                    in_=outprojT_h[:, et * 128:(et + 1) * 128].rearrange(
                        "(dt p) e -> p dt e", p=128)),
                    wait=[("gc", sch.n("gc"))], inc=[("lop", 16)])
                LDW = sch.n("lop")
                b2_ = et % 2
                for di in range(NT):
                    w = [("lop", LDW), ("sv", SV_OIN), ("sv", sch.n("sv"))] \
                        if di == 0 else ()
                    sch.op("t", lambda t, b=buf, b2_=b2_, di=di: t.matmul(
                        psP[b2_][:], opslc[b][:, di * 128:(di + 1) * 128],
                        oin_bf[:, di * GRP:(di + 1) * GRP],
                        start=(di == 0), stop=(di == NT - 1)),
                        wait=w, inc=([("pe", 1)] if di == NT - 1 else []))
                sch.op("v", lambda v, et=et, co=co, b2_=b2_: v.tensor_add(
                    xT[:, et * S + co: et * S + co + GRP],
                    xT[:, et * S + co: et * S + co + GRP], psP[b2_][:]),
                    wait=["pe"], inc=[("sv", 1)])
                sch.op("g", lambda g: g.engine_nop(), wait=[("sv", sch.n("sv"))],
                       inc=[("gc", 1)])
            if gi == 0 and TAP("delta0", [128, NT * GRP]) is not None:
                sch.op("sy", lambda sy: sy.dma_start(
                    out=tap_h["delta0"][:], in_=deltaf[:]),
                    wait=[("sv", SV_U)], inc=[("st", 16)])
            if gi == 0 and TAP("Y0", [128, NT * GRP]) is not None:
                sch.op("sy", lambda sy: sy.dma_start(out=tap_h["Y0"][:], in_=Yf[:]),
                       wait=[("sv", SV_SCAN)], inc=[("st", 16)])
        if TAP("wselT", [NEXP, S]) is not None:
            sch.op("sy", lambda sy: sy.dma_start(
                out=tap_h["wselT"][:], in_=wselT_s[:]),
                wait=[("sv", sch.n("sv"))], inc=[("st", 16)])
        flush()
        l1.close()
        SV_L1A = sch.n("sv")
        if phases < 4:
            l1c.close()
            emit_output()
            return nc, h_in, tap_h
        if TAP("xl1a", [128, NT * S]) is not None:
            sch.op("sy", lambda sy: sy.dma_start(out=tap_h["xl1a"][:], in_=xT[:]),
                   wait=[("sv", SV_L1A)], inc=[("st", 16)])
            flush()

        mo = ExitStack()
        w1b = [sb(mo, "w1b0", [128, NT * JB * 128], BF16),
               sb(mo, "w1b1", [128, NT * JB * 128], BF16)]
        w2b = [sb(mo, "w2b0", [128, JB * D], BF16),
               sb(mo, "w2b1", [128, JB * D], BF16)]
        hts = sb(mo, "hts", [128, JB * TC], BF16)
        hfr = sb(mo, "hfr", [128, TC], F32)
        wsb = sb(mo, "wsb", [128, TC], F32)
        psH = [ps(mo, "psH0", [128, TC]), ps(mo, "psH1", [128, TC])]
        psO = [ps(mo, "psO0", [128, 2 * TC]), ps(mo, "psO1", [128, 2 * TC]),
               ps(mo, "psO2", [128, 2 * TC]), ps(mo, "psO3", [128, 2 * TC])]
        psWB = ps(mo, "psWB", [128, TC])

        def psOut(et):
            return psO[et // 2][:, (et % 2) * TC:(et % 2 + 1) * TC]

        GC_MOE = sch.n("gc")
        PE_W2PREV = sch.n("pe")
        for e in range(NEXP):
            for jb in range(NJB):
                blki = e * NJB + jb
                buf = blki % 2
                lsem = "lma" if buf == 0 else "lmb"
                wdma = [("gc", GC_MOE + max(0, blki - 1))]
                sch.op("g", lambda g, e=e, jb=jb, b=buf: g.dma_start(
                    out=w1b[b][:].rearrange("p (dt m) -> p dt m", dt=NT),
                    in_=w1_h[e, :, jb * JB * 128:(jb + 1) * JB * 128].rearrange(
                        "(dt p) m -> p dt m", p=128)),
                    wait=wdma, inc=[(lsem, 16)])
                sch.op("g", lambda g, e=e, jb=jb, b=buf: g.dma_start(
                    out=w2b[b][:].rearrange("p (j d) -> p j d", j=JB),
                    in_=w2_h[e, jb * JB * 128:(jb + 1) * JB * 128, :].rearrange(
                        "(j p) d -> p j d", p=128)),
                    inc=[(lsem, 16)])
                LDW = sch.n(lsem)
                for tc in range(NTC):
                    to = tc * TC
                    sch.op("t", lambda t, e=e, to=to: t.matmul(
                        psWB[:], ident[0:NEXP, e:e + 1].broadcast_to([NEXP, 128]),
                        wselT_s[0:NEXP, to:to + TC],
                        start=True, stop=True),
                        wait=[("sv", SV_L1A), ("ss", sch.n("ss"))], inc=[("pe", 1)])
                    sch.op("s", lambda s: s.activation(
                        out=wsb[:], in_=psWB[:], func=AF.Copy),
                        wait=["pe", ("sv", sch.n("sv"))],
                        inc=[("ss", 1)])
                    SS_WSB = sch.n("ss")
                    relu_ss = []
                    SV_EVAC = sch.n("sv")
                    for j in range(JB):
                        hb = j % 2
                        jj = jb * JB + j
                        for di in range(NT):
                            w = ()
                            if di == 0:
                                rg = relu_ss[j - 2] if j >= 2 else SS_WSB
                                w = [(lsem, LDW), ("ss", rg)]
                            sch.op("t", lambda t, b=buf, hb=hb, di=di, j=j, to=to:
                                t.matmul(
                                    psH[hb][:],
                                    w1b[b][:, di * JB * 128 + j * 128:
                                           di * JB * 128 + (j + 1) * 128],
                                    xnbf[:, di * S + to: di * S + to + TC],
                                    start=(di == 0), stop=(di == NT - 1)),
                                wait=w, inc=([("pe", 1)] if di == NT - 1 else []))
                        sch.op("s", lambda s, hb=hb, e=e, jj=jj: s.activation(
                            out=hfr[:], in_=psH[hb][:], func=AF.Relu,
                            bias=b1_s[:, e * NMT + jj: e * NMT + jj + 1]),
                            wait=["pe", ("ss", SS_WSB), ("sv", sch.n("sv"))],
                            inc=[("ss", 1)])
                        relu_ss.append(sch.n("ss"))
                        wm = [("ss", sch.n("ss"))]
                        if j == 0:
                            wm.append(("pe", PE_W2PREV))
                        sch.op("v", lambda v, j=j: v.tensor_mul(
                            hts[:, j * TC:(j + 1) * TC], hfr[:], wsb[:]),
                            wait=wm, inc=[("sv", 1)])
                    SV_HT = sch.n("sv")
                    if e == 0 and jb == 0 and tc == 0:
                        if TAP("hts0", [128, JB * TC], BF16) is not None:
                            sch.op("sy", lambda sy: sy.dma_start(
                                out=tap_h["hts0"][:], in_=hts[:]),
                                wait=[("sv", SV_HT)], inc=[("st", 16)])
                        if TAP("wsb0", [128, TC]) is not None:
                            sch.op("sy", lambda sy: sy.dma_start(
                                out=tap_h["wsb0"][:], in_=wsb[:]),
                                wait=[("sv", SV_HT)], inc=[("st", 16)])
                    for et in range(NT):
                        for j in range(JB):
                            w = []
                            if et == 0 and j == 0:
                                w = [("sv", SV_HT), ("sv", SV_EVAC)]
                            last = (j == JB - 1 and blki != 0 and et == NT - 1)
                            sch.op("t", lambda t, b=buf, et=et, j=j, blki=blki:
                                t.matmul(
                                    psOut(et),
                                    w2b[b][:, j * D + et * 128: j * D + (et + 1) * 128],
                                    hts[:, j * TC:(j + 1) * TC], start=(j == 0),
                                    stop=(j == JB - 1 and blki != 0)),
                                wait=w, inc=([("pe", 1)] if last else []))
                        if blki == 0:
                            sch.op("t", lambda t, et=et, to=to: t.matmul(
                                psOut(et), b2_s[:, et * 128:(et + 1) * 128],
                                wselT_bf[0:NEXP, to:to + TC], start=False, stop=True),
                                wait=([("lset", LDG_SET)] if et == 0 else ()),
                                inc=([("pe", 1)] if et == NT - 1 else []))
                    PE_W2PREV = sch.n("pe")
                    PE_GRP = sch.n("pe")
                    for et in range(NT):
                        sch.op("v", lambda v, et=et, to=to: v.tensor_add(
                            xT[:, et * S + to: et * S + to + TC],
                            xT[:, et * S + to: et * S + to + TC], psOut(et)),
                            wait=([("pe", PE_GRP)] if et == 0 else ()),
                            inc=([("sv", 1)] if et == NT - 1 else []))
                sch.op("g", lambda g: g.engine_nop(),
                       wait=[("pe", sch.n("pe"))], inc=[("gc", 1)])
        flush()
        mo.close()
        l1c.close()
        SV_MOE = sch.n("sv")

        emit_output()

    return nc, h_in, tap_h


def _layout_inputs(inputs):
    f = np.ascontiguousarray
    def pp(v):
        return f(v.reshape(NT, 128).T)
    return {
        "wqT": f(inputs["L0_attn_Wq"].T), "wkT": f(inputs["L0_attn_Wk"].T),
        "wvT": f(inputs["L0_attn_Wv"].T), "woT": f(inputs["L0_attn_Wo"].T),
        "bq": pp(inputs["L0_attn_bq"]), "bk": pp(inputs["L0_attn_bk"]),
        "bv": pp(inputs["L0_attn_bv"]), "bo": pp(inputs["L0_attn_bo"]),
        "wrT": f(inputs["L0_ffn_W_r"].T), "wkkT": f(inputs["L0_ffn_W_k"].T),
        "wvvT": f(inputs["L0_ffn_W_v"].T),
        "mur": pp(inputs["L0_ffn_mu_r"]), "muk": pp(inputs["L0_ffn_mu_k"]),
        "normw": pp(inputs["L1_attn_norm_w"]),
        "xprojT": f(inputs["L1_attn_x_proj_w"].T),
        "dtprojT": f(inputs["L1_attn_dt_proj_w"].T),
        "dtprojb": pp(inputs["L1_attn_dt_proj_b"]),
        "alog": f(inputs["L1_attn_A_log"].reshape(NT, 128, STATE)
                  .transpose(1, 0, 2).reshape(128, NT * STATE)),
        "dvec": pp(inputs["L1_attn_D"]),
        "outprojT": f(inputs["L1_attn_out_proj_w"].T),
        "wgate": f(inputs["L1_ffn_w_gate"].reshape(NT, 128, NEXP)
                   .transpose(1, 0, 2).reshape(128, NT * NEXP)),
        "bgate": f(inputs["L1_ffn_b_gate"].reshape(1, NEXP)),
        "w1": f(inputs["L1_ffn_W1"]),
        "b1": f(inputs["L1_ffn_b1"].reshape(NEXP, NMT, 128)
                .transpose(2, 0, 1).reshape(128, NEXP * NMT)),
        "w2": f(inputs["L1_ffn_W2"]),
        "b2": f(inputs["L1_ffn_b2"]),
    }


def kernel(**inputs):
    nc, h_in, _ = build()
    shared = _layout_inputs(inputs)
    x = inputs["L0_attn_x"]
    in_maps = []
    for b in range(8):
        im = dict(shared)
        im["x"] = np.ascontiguousarray(x[b])
        in_maps.append(im)
    res = run_bass_kernel_spmd(nc, in_maps, core_ids=list(range(8)))
    return np.stack([np.asarray(res.results[i]["y"]) for i in range(8)], axis=0)


# version 03
# speedup vs baseline: 1.1658x; 1.1322x over previous
import numpy as np
from contextlib import ExitStack
import concourse.bass as bass
import concourse.mybir as mybir
from concourse.bass_utils import run_bass_kernel_spmd

F32 = mybir.dt.float32
BF16 = mybir.dt.bfloat16
AF = mybir.ActivationFunctionType
ALU = mybir.AluOpType
AX = mybir.AxisListType

D, S = 1024, 2048
NT = D // 128
H, HD = 16, 64
NST = S // 128
CH = 256
NCH = S // CH
STATE, CHUNK = 64, 64
GRP = 512
NGRP = S // GRP
MG = GRP // CHUNK
NEXP, MLP = 8, 4096
NMT = MLP // 128
JB = 8
NJB = NMT // JB
TC = 256
NTC = S // TC
EPS0, EPS1 = 1e-5, 1e-6
SMSHIFT = -12.0
VROW = H * 68


class Sched:
    FIFO_SEM = {"t": "pe", "v": "sv", "s": "ss"}

    def __init__(self):
        self.ops = []
        self.count = {}
        self.waited = {}

    def n(self, sem):
        return self.count.get(sem, 0)

    def op(self, eng, fn, wait=(), inc=()):
        waits = []
        for w in wait:
            waits.append(w if isinstance(w, tuple) else (w, self.count.get(w, 0)))
        incs = []
        for s_, amt in inc:
            self.count[s_] = self.count.get(s_, 0) + amt
            incs.append((s_, amt))
        self.ops.append((eng, fn, waits, incs))

    def barrier(self, eng, sems):
        self.op(eng, None, wait=sems)

    def flush(self, blk, semh):
        ops, self.ops = self.ops, []
        for name in ("sy", "g", "t", "v", "s"):
            mine = [o for o in ops if o[0] == name]
            if not mine:
                continue
            waited = self.waited.setdefault(name, {})

            def body(eng, mine=mine, waited=waited):
                for _, fn, waits, incs in mine:
                    for s_, v_ in waits:
                        if v_ <= 0 or waited.get(s_, -1) >= v_:
                            continue
                        eng.wait_ge(semh[s_], v_)
                        waited[s_] = v_
                    if fn is None:
                        continue
                    instr = fn(eng)
                    for s_, amt in incs:
                        instr.then_inc(semh[s_], amt)

            getattr(blk, {"sy": "sync", "g": "gpsimd", "t": "tensor",
                          "v": "vector", "s": "scalar"}[name])(body)


def build(taps=(), phases=99, racecheck=False):
    taps = set(taps)
    nc = bass.Bass("TRN2", target_bir_lowering=False,
                   detect_race_conditions=racecheck)

    h_in = {}
    def din(name, shape, dt=F32):
        h_in[name] = nc.dram_tensor(name, list(shape), dt, kind="ExternalInput").ap()
        return h_in[name]

    x_h = din("x", [S, D])
    wqT_h = din("wqT", [D, D]); wkT_h = din("wkT", [D, D])
    wvT_h = din("wvT", [D, D]); woT_h = din("woT", [D, D])
    bq_h = din("bq", [128, NT]); bk_h = din("bk", [128, NT])
    bv_h = din("bv", [128, NT]); bo_h = din("bo", [128, NT])
    wrT_h = din("wrT", [D, D]); wkkT_h = din("wkkT", [D, D]); wvvT_h = din("wvvT", [D, D])
    mur_h = din("mur", [128, NT]); muk_h = din("muk", [128, NT])
    normw_h = din("normw", [128, NT])
    xprojT_h = din("xprojT", [D, 192])
    dtprojT_h = din("dtprojT", [64, D])
    dtprojb_h = din("dtprojb", [128, NT])
    alog_h = din("alog", [128, NT * STATE])
    dvec_h = din("dvec", [128, NT])
    outprojT_h = din("outprojT", [D, D])
    wgate_h = din("wgate", [128, NT * NEXP])
    bgate_h = din("bgate", [1, NEXP])
    w1_h = din("w1", [NEXP, D, MLP])
    b1_h = din("b1", [128, NEXP * NMT])
    w2_h = din("w2", [NEXP, MLP, D])
    b2_h = din("b2", [NEXP, D])

    y_h = nc.dram_tensor("y", [S, D], F32, kind="ExternalOutput").ap()
    tap_h = {}
    def TAP(name, shape, dt=F32):
        if name in taps and name not in tap_h:
            tap_h[name] = nc.dram_tensor("tap_" + name, list(shape), dt,
                                         kind="ExternalOutput").ap()
        return tap_h.get(name)

    sch = Sched()
    with ExitStack() as CTX:
        blk = CTX.enter_context(nc.Block())
        SEMH = {}
        for nm in ("ld", "ldx", "lset", "lffn", "lwv", "lqk", "lwo", "lop",
                   "lma", "lmb", "lbc", "gc", "pe", "sv", "ss", "st", "sta", "stb"):
            SEMH[nm] = CTX.enter_context(nc.semaphore(nm))

        def sb(stack, name, shape, dt):
            return stack.enter_context(nc.sbuf_tensor(name, shape, dt))
        def ps(stack, name, shape, dt=F32):
            return stack.enter_context(nc.psum_tensor(name, shape, dt))
        def flush():
            sch.flush(blk, SEMH)

        xT = sb(CTX, "xT", [128, NT * S], F32)
        ones1 = sb(CTX, "ones1", [1, 128], F32)
        onesc = sb(CTX, "onesc", [128, 1], F32)
        ident = sb(CTX, "ident", [128, 128], F32)
        bq_s = sb(CTX, "bq_s", [128, NT], F32); bk_s = sb(CTX, "bk_s", [128, NT], F32)
        bv_s = sb(CTX, "bv_s", [128, NT], F32); bo_s = sb(CTX, "bo_s", [128, NT], F32)
        bv_bf = sb(CTX, "bv_bf", [128, NT], BF16)
        bo2_s = sb(CTX, "bo2_s", [128, NT], F32)
        mur_s = sb(CTX, "mur_s", [128, NT], F32); muk_s = sb(CTX, "muk_s", [128, NT], F32)
        murc_s = sb(CTX, "murc_s", [128, NT], F32); mukc_s = sb(CTX, "mukc_s", [128, NT], F32)
        normw_s = sb(CTX, "normw_s", [128, NT], F32)
        dtprojb_s = sb(CTX, "dtprojb_s", [128, NT], F32)
        aneg_s = sb(CTX, "aneg_s", [128, NT * STATE], F32)
        dvec_s = sb(CTX, "dvec_s", [128, NT], F32)
        wgate_s = sb(CTX, "wgate_s", [128, NT * NEXP], F32)
        bgate_s = sb(CTX, "bgate_s", [1, NEXP], F32)
        bgb_s = sb(CTX, "bgb_s", [128, NEXP], F32)
        b1_s = sb(CTX, "b1_s", [128, NEXP * NMT], F32)
        b2_s = sb(CTX, "b2_s", [NEXP, D], BF16)
        xprojT_s = sb(CTX, "xprojT_s", [128, NT * 192], BF16)
        dtprojT_s = sb(CTX, "dtprojT_s", [64, D], BF16)
        ones8 = sb(CTX, "ones8", [128, NEXP], F32)
        prevcol = sb(CTX, "prevcol", [128, NT], F32)
        CONSTS = [float(EPS0), float(EPS1), 1.0, float(SMSHIFT)]
        cst = sb(CTX, "cst", [128, len(CONSTS)], F32)
        for ci, cv in enumerate(CONSTS):
            nc.const_aps.register(F32, cv, cst[:, ci:ci + 1])

        su = ExitStack()
        x_nat = sb(su, "x_nat", [128, (NST // 2) * D], F32)
        alog_s = sb(su, "alog_s", [128, NT * STATE], F32)
        psT = [ps(su, "psT0", [128, 512]), ps(su, "psT1", [128, 512])]
        psBG = ps(su, "psBG", [128, NEXP])

        for dst, src in [(bq_s, bq_h), (bk_s, bk_h), (bv_s, bv_h), (bo_s, bo_h),
                         (mur_s, mur_h), (muk_s, muk_h), (normw_s, normw_h),
                         (dtprojb_s, dtprojb_h), (alog_s, alog_h), (dvec_s, dvec_h),
                         (wgate_s, wgate_h), (bgate_s, bgate_h), (b1_s, b1_h)]:
            sch.op("sy", lambda sy, d=dst, s_=src: sy.dma_start(out=d[:], in_=s_[:]),
                   inc=[("ld", 16)])
        LD_SM = sch.n("ld")

        sch.op("g", lambda g: g.memset(ident[:], 1.0), inc=[("gc", 1)])
        sch.op("g", lambda g: g.affine_select(
            out=ident[:], in_=ident[:], pattern=[[1, 128]], compare_op=ALU.is_equal,
            fill=0.0, base=0, channel_multiplier=-1), wait=["gc"], inc=[("gc", 1)])
        GC_ID = sch.n("gc")
        sch.op("g", lambda g: g.dma_start(out=b2_s[:], in_=b2_h[:]), inc=[("lset", 16)])
        sch.op("g", lambda g: g.dma_start(
            out=xprojT_s[:].rearrange("p (dt r) -> p dt r", dt=NT),
            in_=xprojT_h[:].rearrange("(dt p) r -> p dt r", p=128)),
            inc=[("lset", 16)])
        sch.op("g", lambda g: g.dma_start(out=dtprojT_s[:], in_=dtprojT_h[:]),
               inc=[("lset", 16)])
        LDG_SET = sch.n("lset")

        sch.op("v", lambda v: v.memset(ones1[:], 1.0))
        sch.op("v", lambda v: v.memset(onesc[:], 1.0))
        sch.op("v", lambda v: v.memset(ones8[:], 1.0))
        for ci, cv in enumerate(CONSTS):
            sch.op("v", lambda v, ci=ci, cv=cv: v.memset(cst[:, ci:ci + 1], cv))
        sch.op("v", lambda v: v.memset(prevcol[:], 0.0), inc=[("sv", 1)])
        SV_ONES = sch.n("sv")
        sch.barrier("v", [("ld", LD_SM)])
        sch.op("v", lambda v: v.tensor_scalar(
            out=murc_s[:], in0=mur_s[:], scalar1=-1.0, scalar2=1.0,
            op0=ALU.mult, op1=ALU.add))
        sch.op("v", lambda v: v.tensor_scalar(
            out=mukc_s[:], in0=muk_s[:], scalar1=-1.0, scalar2=1.0,
            op0=ALU.mult, op1=ALU.add), inc=[("sv", 1)])
        sch.op("v", lambda v: v.tensor_copy(bv_bf[:], bv_s[:]),
               wait=[("ld", LD_SM)], inc=[("sv", 1)])
        sch.op("s", lambda s: s.activation(out=aneg_s[:], in_=alog_s[:], func=AF.Exp),
               wait=[("ld", LD_SM)], inc=[("ss", 1)])
        SS_ANEG = sch.n("ss")
        sch.op("v", lambda v: v.tensor_scalar_mul(aneg_s[:], aneg_s[:], -1.0),
               wait=[("ss", SS_ANEG)], inc=[("sv", 1)])
        sch.op("t", lambda t: t.matmul(psBG[:], ones1[:], bgate_s[:], start=True,
                                       stop=True),
               wait=[("sv", SV_ONES), ("ld", LD_SM)], inc=[("pe", 1)])
        sch.op("v", lambda v: v.tensor_copy(bgb_s[:], psBG[:]), wait=["pe"],
               inc=[("sv", 1)])

        SV_T0 = sch.n("sv")
        gi = 0
        for half in range(2):
            for i in range(NST // 2):
                sti = half * 8 + i
                sch.op("sy", lambda sy, i=i, sti=sti: sy.dma_start(
                    out=x_nat[:, i * D:(i + 1) * D],
                    in_=x_h[sti * 128:(sti + 1) * 128, :]), inc=[("ldx", 16)])
            LD_H = sch.n("ldx")
            for dt_j in range(NT):
                for sg in range(2):
                    bank = gi % 2
                    for q in range(4):
                        i = sg * 4 + q
                        w = [("gc", GC_ID), ("ldx", LD_H)]
                        if gi >= 2 and q == 0:
                            w.append(("sv", SV_T0 + gi - 1))
                        sch.op("t", lambda t, b=bank, q=q, i=i, dj=dt_j: t.transpose(
                            psT[b][:, q * 128:(q + 1) * 128],
                            x_nat[:, i * D + dj * 128: i * D + dj * 128 + 128],
                            ident[:]),
                            wait=w, inc=([("pe", 1)] if q == 3 else []))
                    so = half * 1024 + sg * 512
                    sch.op("v", lambda v, b=bank, dj=dt_j, so=so: v.tensor_copy(
                        xT[:, dj * S + so: dj * S + so + 512], psT[b][:]),
                        wait=["pe"], inc=[("sv", 1)])
                    gi += 1
            if half == 0:
                sch.barrier("sy", [("pe", sch.n("pe"))])
        flush()
        su.close()

        def emit_output():
            ou = ExitStack()
            ynat = [sb(ou, "ynat0", [128, D], F32), sb(ou, "ynat1", [128, D], F32)]
            psY = [ps(ou, "psY0", [128, 512]), ps(ou, "psY1", [128, 512]),
                   ps(ou, "psY2", [128, 512]), ps(ou, "psY3", [128, 512])]
            SV_ALL = sch.n("sv")
            SS_ALL = sch.n("ss")
            for sti in range(NST):
                yb = sti % 2
                for dt in range(NT):
                    bank = yb * 2 + dt // 4
                    w = []
                    if sti == 0 and dt == 0:
                        w = [("sv", SV_ALL), ("ss", SS_ALL)]
                    if sti >= 2 and dt == 0:
                        w = [("sta" if sti % 2 == 0 else "stb",
                              sch.n("sta" if sti % 2 == 0 else "stb"))]
                    sch.op("t", lambda t, dt=dt, sti=sti, bank=bank: t.transpose(
                        psY[bank][:, (dt % 4) * 128:(dt % 4 + 1) * 128],
                        xT[:, dt * S + sti * 128: dt * S + sti * 128 + 128],
                        ident[:]),
                        wait=w, inc=([("pe", 1)] if dt == NT - 1 else []))
                sch.op("v", lambda v, yb=yb: v.tensor_copy(
                    ynat[yb][:, 0:512], psY[yb * 2][:]), wait=["pe"])
                sch.op("v", lambda v, yb=yb: v.tensor_copy(
                    ynat[yb][:, 512:1024], psY[yb * 2 + 1][:]), inc=[("sv", 1)])
                sch.op("sy", lambda sy, sti=sti, yb=yb: sy.dma_start(
                    out=y_h[sti * 128:(sti + 1) * 128, :], in_=ynat[yb][:]),
                    wait=[("sv", sch.n("sv"))],
                    inc=[("sta" if sti % 2 == 0 else "stb", 16)])
            sch.barrier("sy", [("sta", sch.n("sta")), ("stb", sch.n("stb")),
                               ("st", sch.n("st"))])
            flush()
            ou.close()

        if phases < 1:
            emit_output()
            return nc, h_in, tap_h

        xn_st = ExitStack()
        xnbf = sb(xn_st, "xnbf", [128, NT * S], BF16)
        l0 = ExitStack()
        mrow = sb(l0, "mrow", [1, GRP], F32)
        rrow = sb(l0, "rrow", [1, GRP], F32)
        rta = sb(l0, "rta", [1, GRP], F32)
        rtb = sb(l0, "rtb", [1, GRP], F32)
        wr_sb = sb(l0, "wr_sb", [128, NT * D], BF16)
        wkk_sb = sb(l0, "wkk_sb", [128, NT * D], BF16)
        wvv_sb = sb(l0, "wvv_sb", [128, NT * D], BF16)
        xnf = sb(l0, "xnf", [128, NT * CH], F32)
        xr_bf = sb(l0, "xr_bf", [128, NT * CH], BF16)
        xk_bf = sb(l0, "xk_bf", [128, NT * CH], BF16)
        ktmp = sb(l0, "ktmp", [128, CH], F32)
        kact = sb(l0, "kact", [128, NT * CH], BF16)
        rsig = sb(l0, "rsig", [128, CH], F32)
        mixt = sb(l0, "mixt", [128, CH], F32)
        xsq = sb(l0, "xsq", [128, CH], F32)
        psRow = ps(l0, "psRow", [1, 2 * CH])
        psBC = ps(l0, "psBC", [128, 2 * CH])
        psMM = [ps(l0, "psMM0", [128, CH]), ps(l0, "psMM1", [128, CH])]

        PE_SETUP = sch.n("pe")
        for wdst, wsrc in ((wr_sb, wrT_h), (wkk_sb, wkkT_h), (wvv_sb, wvvT_h)):
            sch.op("g", lambda g, d=wdst, s_=wsrc: g.dma_start(
                out=d[:].rearrange("p (dt e) -> p dt e", dt=NT),
                in_=s_[:].rearrange("(dt p) e -> p dt e", p=128)),
                wait=[("pe", PE_SETUP)], inc=[("lffn", 16)])
        LDG_FFN = sch.n("lffn")

        def ln_stats(W, xsrc, xoff, mdst, rdst, eps, prow_a, prow_b, sqbuf, rta, rtb):
            sch.barrier("t", [("sv", sch.n("sv")), ("ss", sch.n("ss"))])
            for dt in range(NT):
                sch.op("t", lambda t, dt=dt: t.matmul(
                    prow_a[0:1, 0:W], onesc[:], xsrc[:, xoff(dt): xoff(dt) + W],
                    start=(dt == 0), stop=(dt == NT - 1)),
                    inc=([("pe", 1)] if dt == NT - 1 else []))
            PE_SQ = sch.n("pe")
            for dt in range(NT):
                wsq = [("sv", sch.n("sv"))] if dt == 0 else \
                      [("pe", PE_SQ + dt)]
                sch.op("s", lambda s, dt=dt: s.activation(
                    out=sqbuf[:, 0:W], in_=xsrc[:, xoff(dt): xoff(dt) + W],
                    func=AF.Square), wait=wsq, inc=[("ss", 1)])
                sch.op("t", lambda t, dt=dt: t.matmul(
                    prow_b[0:1, 0:W], onesc[:], sqbuf[:, 0:W],
                    start=(dt == 0), stop=(dt == NT - 1)), wait=["ss"],
                    inc=[("pe", 1)])
            if mdst is not None:
                sch.op("v", lambda v: v.tensor_scalar_mul(
                    mdst[0:1, 0:W], prow_a[0:1, 0:W], 1.0 / D),
                    wait=["pe", ("ss", sch.n("ss"))])
                sch.op("v", lambda v: v.tensor_scalar_mul(
                    rta[0:1, 0:W], prow_b[0:1, 0:W], 1.0 / D), inc=[("sv", 1)])
                sch.op("v", lambda v: v.tensor_mul(
                    rtb[0:1, 0:W], mdst[0:1, 0:W], mdst[0:1, 0:W]),
                    wait=[("sv", sch.n("sv"))], inc=[("sv", 1)])
                sch.op("v", lambda v: v.tensor_sub(
                    rta[0:1, 0:W], rta[0:1, 0:W], rtb[0:1, 0:W]),
                    wait=[("sv", sch.n("sv"))], inc=[("sv", 1)])
            else:
                sch.op("v", lambda v: v.tensor_scalar_mul(
                    rta[0:1, 0:W], prow_b[0:1, 0:W], 1.0 / D),
                    wait=["pe", ("ss", sch.n("ss"))], inc=[("sv", 1)])
            sch.op("s", lambda s: s.activation(
                out=rtb[0:1, 0:W], in_=rta[0:1, 0:W], func=AF.Ln,
                bias=float(eps)), wait=["sv"])
            sch.op("s", lambda s, rdst=rdst: s.activation(
                out=rdst[0:1, 0:W], in_=rtb[0:1, 0:W], func=AF.Exp,
                scale=-0.5), inc=[("ss", 1)])

        for c in range(NCH):
            co = c * CH
            ln_stats(CH, xT, lambda dt, co=co: dt * S + co, mrow, rrow, EPS0,
                     psRow[0:1, 0:CH], psRow[0:1, CH:2 * CH], xsq, rta, rtb)
            SS_R = sch.n("ss")
            sch.op("t", lambda t: t.matmul(
                psBC[:, 0:CH], ones1[:], mrow[0:1, 0:CH], start=True, stop=True),
                wait=[("ss", SS_R), ("sv", sch.n("sv"))])
            sch.op("t", lambda t: t.matmul(
                psBC[:, CH:2 * CH], ones1[:], rrow[0:1, 0:CH], start=True,
                stop=True), inc=[("pe", 1)])
            PE_BC = sch.n("pe")
            for dt in range(NT):
                sch.op("v", lambda v, dt=dt, co=co: v.scalar_tensor_tensor(
                    out=xnf[:, dt * CH:(dt + 1) * CH],
                    in0=xT[:, dt * S + co: dt * S + co + CH], scalar=1.0,
                    in1=psBC[:, 0:CH], op0=ALU.mult, op1=ALU.subtract),
                    wait=([("pe", PE_BC)] if dt == 0 else ()))
                sch.op("v", lambda v, dt=dt: v.tensor_mul(
                    xnf[:, dt * CH:(dt + 1) * CH], xnf[:, dt * CH:(dt + 1) * CH],
                    psBC[:, CH:2 * CH]))
                sch.op("v", lambda v, dt=dt, co=co: v.tensor_copy(
                    xnbf[:, dt * S + co: dt * S + co + CH],
                    xnf[:, dt * CH:(dt + 1) * CH]), inc=[("sv", 1)])
            for (dst, mu, muc) in ((xr_bf, mur_s, murc_s), (xk_bf, muk_s, mukc_s)):
                for dt in range(NT):
                    sch.op("v", lambda v, dt=dt, mu=mu: v.tensor_scalar(
                        out=mixt[:, 0:CH], in0=xnf[:, dt * CH:(dt + 1) * CH],
                        scalar1=mu[:, dt:dt + 1], scalar2=None, op0=ALU.mult))
                    sch.op("v", lambda v, dt=dt, dst=dst, muc=muc:
                        v.scalar_tensor_tensor(
                            out=dst[:, dt * CH + 1:(dt + 1) * CH],
                            in0=xnf[:, dt * CH: dt * CH + CH - 1],
                            scalar=muc[:, dt:dt + 1], in1=mixt[:, 1:CH],
                            op0=ALU.mult, op1=ALU.add))
                    sch.op("v", lambda v, dt=dt, dst=dst, muc=muc:
                        v.scalar_tensor_tensor(
                            out=dst[:, dt * CH: dt * CH + 1],
                            in0=prevcol[:, dt:dt + 1], scalar=muc[:, dt:dt + 1],
                            in1=mixt[:, 0:1], op0=ALU.mult, op1=ALU.add))
            for dt in range(NT):
                sch.op("v", lambda v, dt=dt: v.tensor_copy(
                    prevcol[:, dt:dt + 1], xnf[:, dt * CH + CH - 1: (dt + 1) * CH]),
                    inc=([("sv", 1)] if dt == NT - 1 else []))
            SV_MIX = sch.n("sv")
            for do in range(NT):
                b = do % 2
                for di in range(NT):
                    w = []
                    if di == 0:
                        w = [("sv", SV_MIX), ("ss", sch.n("ss")), ("lffn", LDG_FFN)]
                    sch.op("t", lambda t, do=do, di=di, b=b: t.matmul(
                        psMM[b][:], wkk_sb[:, di * D + do * 128: di * D + (do + 1) * 128],
                        xk_bf[:, di * CH:(di + 1) * CH],
                        start=(di == 0), stop=(di == NT - 1)), wait=w,
                        inc=([("pe", 1)] if di == NT - 1 else []))
                sch.op("s", lambda s, b=b: s.activation(
                    out=ktmp[:], in_=psMM[b][:], func=AF.Relu),
                    wait=["pe", ("sv", sch.n("sv"))], inc=[("ss", 1)])
                sch.op("v", lambda v, do=do: v.tensor_mul(
                    kact[:, do * CH:(do + 1) * CH], ktmp[:], ktmp[:]),
                    wait=["ss"], inc=[("sv", 1)])
            for do in range(NT):
                b = do % 2
                for di in range(NT):
                    w = [("ss", sch.n("ss")), ("sv", sch.n("sv"))] if di == 0 else ()
                    sch.op("t", lambda t, do=do, di=di, b=b: t.matmul(
                        psMM[b][:], wr_sb[:, di * D + do * 128: di * D + (do + 1) * 128],
                        xr_bf[:, di * CH:(di + 1) * CH],
                        start=(di == 0), stop=(di == NT - 1)), wait=w,
                        inc=([("pe", 1)] if di == NT - 1 else []))
                sch.op("s", lambda s, b=b: s.activation(
                    out=rsig[:], in_=psMM[b][:], func=AF.Sigmoid),
                    wait=["pe", ("sv", sch.n("sv"))], inc=[("ss", 1)])
                SS_RS = sch.n("ss")
                b2_ = (do + 1) % 2
                for di in range(NT):
                    w = [("ss", SS_RS), ("sv", sch.n("sv"))] if di == 0 else ()
                    sch.op("t", lambda t, do=do, di=di, b=b2_: t.matmul(
                        psMM[b][:], wvv_sb[:, di * D + do * 128: di * D + (do + 1) * 128],
                        kact[:, di * CH:(di + 1) * CH],
                        start=(di == 0), stop=(di == NT - 1)), wait=w,
                        inc=([("pe", 1)] if di == NT - 1 else []))
                sch.op("v", lambda v, b=b2_: v.tensor_mul(
                    mixt[:], psMM[b][:], rsig[:]), wait=["pe"])
                sch.op("v", lambda v, do=do, co=co: v.tensor_add(
                    xT[:, do * S + co: do * S + co + CH],
                    xT[:, do * S + co: do * S + co + CH], mixt[:]), inc=[("sv", 1)])
        flush()
        l0.close()

        if TAP("xnbf", [128, NT * S], BF16) is not None:
            sch.op("sy", lambda sy: sy.dma_start(out=tap_h["xnbf"][:], in_=xnbf[:]),
                   wait=[("sv", sch.n("sv"))], inc=[("st", 16)])
        if TAP("xl0f", [128, NT * S]) is not None:
            sch.op("sy", lambda sy: sy.dma_start(out=tap_h["xl0f"][:], in_=xT[:]),
                   wait=[("sv", sch.n("sv"))], inc=[("st", 16)])
        flush()
        if phases < 2:
            emit_output()
            return nc, h_in, tap_h

        vp = ExitStack()
        VB = sb(vp, "VB", [128, NST * VROW], BF16)
        vpw = ExitStack()
        wvslc = [sb(vpw, "wvslc0", [128, NT * 256], BF16),
                 sb(vpw, "wvslc1", [128, NT * 256], BF16)]
        psV = [ps(vpw, "psV0", [128, 256]), ps(vpw, "psV1", [128, 256])]

        SV_L0 = sch.n("sv")
        sch.op("v", lambda v: v.memset(VB[:], 0.0), wait=[("sv", SV_L0)],
               inc=[("sv", 1)])
        sch.op("v", lambda v: v.memset(
            VB[:].rearrange("p (t h c) -> p t h c", t=NST, h=H, c=68)[:, :, :, 64:65],
            1.0), inc=[("sv", 1)])
        SV_VONES = sch.n("sv")

        for ec in range(4):
            buf = ec % 2
            wv_w = [("gc", sch.n("gc"))]
            if ec == 0:
                wv_w += [("sv", SV_L0), ("ss", sch.n("ss")), ("pe", sch.n("pe"))]
            sch.op("g", lambda g, ec=ec, b=buf: g.dma_start(
                out=wvslc[b][:].rearrange("p (dt e) -> p dt e", dt=NT),
                in_=wvT_h[:, ec * 256:(ec + 1) * 256].rearrange(
                    "(dt p) e -> p dt e", p=128)),
                wait=wv_w, inc=[("lwv", 16)])
            LDW = sch.n("lwv")
            for sti in range(NST):
                b2_ = sti % 2
                for di in range(NT):
                    w = []
                    if di == 0:
                        w = [("lwv", LDW), ("sv", sch.n("sv"))]
                        if sti == 0 and ec == 0:
                            w.append(("sv", SV_VONES))
                    sch.op("t", lambda t, b=buf, b2_=b2_, di=di, sti=sti: t.matmul(
                        psV[b2_][:], xnbf[:, di * S + sti * 128: di * S + sti * 128 + 128],
                        wvslc[b][:, di * 256:(di + 1) * 256],
                        start=(di == 0), stop=(di == NT - 1)),
                        wait=w, inc=([("pe", 1)] if di == NT - 1 else []))
                sch.op("v", lambda v, sti=sti, ec=ec, b2_=b2_: v.tensor_copy(
                    VB[:, sti * VROW + (ec * 4) * 68: sti * VROW + (ec * 4 + 4) * 68]
                      .rearrange("p (h c) -> p h c", h=4, c=68)[:, :, 0:64],
                    psV[b2_][:].rearrange("p (h c) -> p h c", h=4, c=64)),
                    wait=["pe"], inc=[("sv", 1)])
            sch.op("g", lambda g: g.engine_nop(), wait=[("sv", sch.n("sv"))],
                   inc=[("gc", 1)])
        flush()
        vpw.close()
        if TAP("VB", [128, NST * VROW], BF16) is not None:
            sch.op("sy", lambda sy: sy.dma_start(out=tap_h["VB"][:], in_=VB[:]),
                   wait=[("sv", sch.n("sv"))], inc=[("st", 16)])
            flush()

        at = ExitStack()
        QTh = sb(at, "QTh", [64, S], BF16)
        KTh = sb(at, "KTh", [64, S], BF16)
        expS = sb(at, "expS", [128, NST * 512], BF16)
        rb_sb = sb(at, "rb_sb", [128, 512], F32)
        ctxf = sb(at, "ctxf", [128, NT * S], BF16)
        recipr = sb(at, "recipr", [1, 512], F32)
        wqkslc = [sb(at, "wqks0", [128, NT * 128], BF16)]
        woslc = [sb(at, "woslc0", [128, NT * 128], BF16)]
        psS = [ps(at, "psS0", [128, 512]), ps(at, "psS1", [128, 512]),
               ps(at, "psS2", [128, 512])]
        psU = ps(at, "psU", [128, 512])
        psR = ps(at, "psR", [128, 512])
        psQK = ps(at, "psQK", [128, 512])

        for hh in range(H):
            et, sub = divmod(hh, 2)
            buf = 0
            sch.op("g", lambda g, hh=hh, b=buf: g.dma_start(
                out=wqkslc[b][:, 0:NT * 64].rearrange("p (dt e) -> p dt e", dt=NT),
                in_=wqT_h[:, hh * 64:(hh + 1) * 64].rearrange(
                    "(dt p) e -> p dt e", p=128)),
                wait=[("gc", sch.n("gc"))], inc=[("lqk", 16)])
            sch.op("g", lambda g, hh=hh, b=buf: g.dma_start(
                out=wqkslc[b][:, NT * 64:NT * 128].rearrange(
                    "p (dt e) -> p dt e", dt=NT),
                in_=wkT_h[:, hh * 64:(hh + 1) * 64].rearrange(
                    "(dt p) e -> p dt e", p=128)),
                inc=[("lqk", 16)])
            LDW = sch.n("lqk")
            for (dst, wof, bias) in ((QTh, 0, bq_s), (KTh, NT * 64, bk_s)):
                for sc in range(4):
                    for di in range(NT):
                        w = [("lqk", LDW), ("ss", sch.n("ss"))] if di == 0 else ()
                        sch.op("t", lambda t, b=buf, wof=wof, di=di, sc=sc: t.matmul(
                            psQK[0:64, :],
                            wqkslc[b][:, wof + di * 64: wof + (di + 1) * 64],
                            xnbf[:, di * S + sc * 512: di * S + (sc + 1) * 512],
                            start=(di == 0), stop=(di == NT - 1)),
                            wait=w, inc=([("pe", 1)] if di == NT - 1 else []))
                    sch.op("s", lambda s, dst=dst, sc=sc, bias=bias, et=et, sub=sub:
                        s.activation(
                            out=dst[:, sc * 512:(sc + 1) * 512], in_=psQK[0:64, :],
                            func=AF.Identity,
                            bias=bias[sub * 64:sub * 64 + 64, et:et + 1]),
                        wait=["pe"], inc=[("ss", 1)])
            SS_QK = sch.n("ss")
            sch.op("g", lambda g: g.engine_nop(), wait=[("ss", SS_QK)],
                   inc=[("gc", 1)])
            for qc in range(4):
                qo = qc * 512
                for kt in range(NST):
                    b = kt % 3
                    w = []
                    if hh == 0 and qc == 0 and kt == 0:
                        w = [("ss", SS_QK), ("sv", sch.n("sv"))]
                    elif kt >= 3 or not (hh == 0 and qc == 0):
                        w = [("ss", sch.n("ss") - 2)]
                    sch.op("t", lambda t, b=b, kt=kt, qo=qo: t.matmul(
                        psS[b][:], KTh[:, kt * 128:(kt + 1) * 128],
                        QTh[:, qo:qo + 512], start=True, stop=True),
                        wait=(w + [("ss", SS_QK)] if kt == 0 else w),
                        inc=[("pe", 1)])
                    sch.op("s", lambda s, b=b, kt=kt: s.activation(
                        out=expS[:, kt * 512:(kt + 1) * 512], in_=psS[b][:],
                        func=AF.Exp, scale=0.125, bias=SMSHIFT),
                        wait=["pe"], inc=[("ss", 1)])
                SS_EXP = sch.n("ss")
                if sub == 0:
                    for kt in range(NST):
                        w = [("ss", SS_EXP), ("sv", sch.n("sv"))] if kt == 0 else ()
                        sch.op("t", lambda t, kt=kt, hh=hh: t.matmul(
                            psU[0:65, :],
                            VB[:, kt * VROW + hh * 68: kt * VROW + hh * 68 + 65],
                            expS[:, kt * 512:(kt + 1) * 512],
                            start=(kt == 0), stop=(kt == NST - 1)), wait=w,
                            inc=([("pe", 1)] if kt == NST - 1 else []))
                    sch.op("v", lambda v: v.reciprocal(recipr[:], psU[64:65, :]),
                           wait=["pe"], inc=[("sv", 1)])
                else:
                    for kt in range(NST):
                        w = [("ss", SS_EXP), ("sv", sch.n("sv"))] if kt == 0 else ()
                        sch.op("t", lambda t, kt=kt, hh=hh: t.matmul(
                            psU[64:128, :],
                            VB[:, kt * VROW + hh * 68: kt * VROW + hh * 68 + 64],
                            expS[:, kt * 512:(kt + 1) * 512],
                            start=(kt == 0), stop=(kt == NST - 1),
                            tile_position=(0, 64)), wait=w,
                            inc=([("pe", 1)] if kt == NST - 1 else []))
                    for kt in range(NST):
                        sch.op("t", lambda t, kt=kt, hh=hh: t.matmul(
                            psU[0:1, :],
                            VB[:, kt * VROW + hh * 68 + 64: kt * VROW + hh * 68 + 65],
                            expS[:, kt * 512:(kt + 1) * 512],
                            start=(kt == 0), stop=(kt == NST - 1)),
                            inc=([("pe", 1)] if kt == NST - 1 else []))
                    sch.op("v", lambda v: v.reciprocal(recipr[:], psU[0:1, :]),
                           wait=["pe"], inc=[("sv", 1)])
                sch.op("t", lambda t: t.matmul(
                    psR[:], ones1[:], recipr[:], start=True, stop=True),
                    wait=["sv"], inc=[("pe", 1)])
                sch.op("s", lambda s: s.activation(
                    out=rb_sb[:], in_=psR[:], func=AF.Copy),
                    wait=["pe", ("sv", sch.n("sv"))],
                    inc=[("ss", 1)])
                sch.op("v", lambda v, et=et, sub=sub, qo=qo: v.tensor_mul(
                    ctxf[sub * 64:sub * 64 + 64, et * S + qo: et * S + qo + 512],
                    psU[sub * 64:sub * 64 + 64, :],
                    rb_sb[sub * 64:sub * 64 + 64, :]),
                    wait=[("ss", sch.n("ss"))], inc=[("sv", 1)])
        SV_CTX = sch.n("sv")
        for et in range(NT):
            buf = 0
            sch.op("g", lambda g, et=et, b=buf: g.dma_start(
                out=woslc[b][:].rearrange("p (dt e) -> p dt e", dt=NT),
                in_=woT_h[:, et * 128:(et + 1) * 128].rearrange(
                    "(dt p) e -> p dt e", p=128)),
                wait=[("gc", sch.n("gc"))], inc=[("lwo", 16)])
            LDW = sch.n("lwo")
            for di in range(NT):
                sch.op("t", lambda t, b=buf, di=di, et=et: t.matmul(
                    psS[2][:, et:et + 1],
                    woslc[b][:, di * 128:(di + 1) * 128],
                    bv_bf[:, di:di + 1], start=(di == 0), stop=(di == NT - 1)),
                    wait=([("lwo", LDW), ("sv", SV_CTX)] if di == 0 else ()),
                    inc=([("pe", 1)] if di == NT - 1 else []))
            sch.op("v", lambda v, et=et: v.tensor_add(
                bo2_s[:, et:et + 1], psS[2][:, et:et + 1], bo_s[:, et:et + 1]),
                wait=["pe"], inc=[("sv", 1)])
            SV_BO = sch.n("sv")
            for qc in range(4):
                b2_ = qc % 2
                for di in range(NT):
                    w = [("sv", SV_BO)] if di == 0 else ()
                    sch.op("t", lambda t, b=buf, b2_=b2_, di=di, qc=qc: t.matmul(
                        psS[b2_][:], woslc[b][:, di * 128:(di + 1) * 128],
                        ctxf[:, di * S + qc * 512: di * S + (qc + 1) * 512],
                        start=(di == 0), stop=(di == NT - 1)),
                        wait=w, inc=([("pe", 1)] if di == NT - 1 else []))
                sch.op("v", lambda v, et=et, qc=qc, b2_=b2_:
                    v.scalar_tensor_tensor(
                        out=xT[:, et * S + qc * 512: et * S + (qc + 1) * 512],
                        in0=psS[b2_][:], scalar=bo2_s[:, et:et + 1],
                        in1=xT[:, et * S + qc * 512: et * S + (qc + 1) * 512],
                        op0=ALU.add, op1=ALU.add), wait=["pe"], inc=[("sv", 1)])
            sch.op("g", lambda g: g.engine_nop(), wait=[("sv", sch.n("sv"))],
                   inc=[("gc", 1)])
        if TAP("ctxf", [128, NT * S], BF16) is not None:
            sch.op("sy", lambda sy: sy.dma_start(out=tap_h["ctxf"][:], in_=ctxf[:]),
                   wait=[("sv", sch.n("sv"))], inc=[("st", 16)])
        if TAP("QTh", [64, S], BF16) is not None:
            sch.op("sy", lambda sy: sy.dma_start(out=tap_h["QTh"][:], in_=QTh[:]),
                   wait=[("sv", sch.n("sv"))], inc=[("st", 16)])
        if TAP("KTh", [64, S], BF16) is not None:
            sch.op("sy", lambda sy: sy.dma_start(out=tap_h["KTh"][:], in_=KTh[:]),
                   wait=[("sv", sch.n("sv"))], inc=[("st", 16)])
        if TAP("expS", [128, NST * 512], BF16) is not None:
            sch.op("sy", lambda sy: sy.dma_start(out=tap_h["expS"][:], in_=expS[:]),
                   wait=[("sv", sch.n("sv"))], inc=[("st", 16)])
        flush()
        at.close()
        vp.close()
        xn_st.close()
        if TAP("xl0", [128, NT * S]) is not None:
            sch.op("sy", lambda sy: sy.dma_start(out=tap_h["xl0"][:], in_=xT[:]),
                   wait=[("sv", sch.n("sv"))], inc=[("st", 16)])
            flush()
        if phases < 3:
            emit_output()
            return nc, h_in, tap_h

        l1c = ExitStack()
        wselT_s = sb(l1c, "wselT_s", [NEXP, GRP], F32)
        wselT_bf = sb(l1c, "wselT_bf", [NEXP, GRP], BF16)
        l1 = ExitStack()
        mb1 = sb(l1, "mb1", [128, GRP], F32)
        rb1 = sb(l1, "rb1", [128, GRP], F32)
        xnf1 = sb(l1, "xnf1", [128, NT * GRP], F32)
        deltaf = xnf1
        Uf = sb(l1, "Uf", [128, NT * GRP], F32)
        xn2f = sb(l1, "xn2f", [128, NT * GRP], F32)
        Yf = xn2f
        xn1b = sb(l1, "xn1b", [128, NT * GRP], BF16)
        xn2c = [sb(l1, "xn2c0", [128, GRP], BF16),
                sb(l1, "xn2c1", [128, GRP], BF16)]
        dlo_bf = sb(l1, "dlo_bf", [64, GRP], BF16)
        BCrow = sb(l1, "BCrow", [128, GRP], F32)
        Bbig = [sb(l1, "Bbig0", [128, GRP], F32), sb(l1, "Bbig1", [128, GRP], F32)]
        Cbig = [sb(l1, "Cbig0", [128, GRP], F32), sb(l1, "Cbig1", [128, GRP], F32)]
        mrow = BCrow[0:1, :]
        rta = Bbig[0][0:1, :]
        rtb = Bbig[1][0:1, :]
        rrow = Cbig[0][0:1, :]
        dAb = [sb(l1, "dA0", [128, GRP], F32), sb(l1, "dA1", [128, GRP], F32)]
        wSb = sb(l1, "wS0", [128, GRP], F32)
        hSb = sb(l1, "hS0", [128, GRP], F32)
        tmpS = wSb
        oin_bf = xn1b
        opslc = [sb(l1, "opslc0", [128, NT * 128], BF16)]
        sp1 = sb(l1, "sp1", [128, GRP], F32)
        lgt = sb(l1, "lgt", [128, NEXP], F32)
        gex = sb(l1, "gex", [128, NEXP], F32)
        gex2 = sb(l1, "gex2", [128, NEXP], F32)
        gmx = sb(l1, "gmx", [128, 4], F32)
        geq = sb(l1, "geq", [128, NEXP], F32)
        ginc = sb(l1, "ginc", [128, NEXP], F32)
        gq = sb(l1, "gq", [128, 12], F32)
        wsel = sb(l1, "wsel", [128, NEXP], F32)
        MJB = 4
        MNJB = NMT // MJB
        w1b = [sb(l1, "w1b0", [128, NT * MJB * 128], BF16),
               sb(l1, "w1b1", [128, NT * MJB * 128], BF16)]
        w2b = [sb(l1, "w2b0", [128, MJB * D], BF16),
               sb(l1, "w2b1", [128, MJB * D], BF16)]
        hts = [sb(l1, "hts0", [128, MJB * TC], BF16),
               sb(l1, "hts1", [128, MJB * TC], BF16)]
        wselB = [sb(l1, "wselB0", [128, 2 * TC], F32),
                 sb(l1, "wselB1", [128, 2 * TC], F32)]
        psP = [ps(l1, "psP0", [128, GRP]), ps(l1, "psP1", [128, GRP])]
        psG = psP[0][:, 0:NEXP]
        psWT = psP[1][0:NEXP, 0:128]
        psO = ps(l1, "psO", [128, NT * TC])
        psH = ps(l1, "psH", [128, 4 * TC])
        psOP = psP[0]
        hfr = sp1[:, 0:TC]

        def psOut(et):
            return psO[:, et * TC:(et + 1) * TC]

        blk_gc = {}
        blk_rec = {}
        unit_rec = {}
        hts_pe = [0, 0]
        psh_ss = [0, 0, 0, 0]
        wselB_free = {0: 0, 1: 0}
        SS_WSELB = {}
        SV_XNB = {}
        SV_XNF1 = {}
        evac_sv_last = 0
        op_pe_last = 0
        b2done = set()

        def ensure_weights(sched, u):
            e, jb, gblki, tc = sched[u]
            if gblki in blk_rec:
                return
            lsem = "lma" if gblki % 2 == 0 else "lmb"
            buf = gblki % 2
            wdma = [("pe", blk_gc[gblki - 2])] if gblki >= 2 else []
            sch.op("g", lambda g, e=e, jb=jb, b=buf: g.dma_start(
                out=w1b[b][:].rearrange("p (dt m) -> p dt m", dt=NT),
                in_=w1_h[e, :, jb * MJB * 128:(jb + 1) * MJB * 128].rearrange(
                    "(dt p) m -> p dt m", p=128)),
                wait=wdma, inc=[(lsem, 16)])
            sch.op("g", lambda g, e=e, jb=jb, b=buf: g.dma_start(
                out=w2b[b][:].rearrange("p (j d) -> p j d", j=MJB),
                in_=w2_h[e, jb * MJB * 128:(jb + 1) * MJB * 128, :].rearrange(
                    "(j p) d -> p j d", p=128)),
                inc=[(lsem, 16)])
            blk_rec[gblki] = (lsem, sch.n(lsem), buf)

        def ensure_wselB(e, gi):
            if (e, gi) in SS_WSELB or e >= NEXP:
                return
            sch.op("sy", lambda sy, e=e, b=e % 2: sy.dma_start(
                out=wselB[b][:],
                in_=wselT_s[e:e + 1, 0:2 * TC]
                    .unsqueeze(1).broadcast_to([1, 128, 2 * TC])),
                wait=[("sv", max(wselB_free[e % 2], SV_XNB[gi]))],
                inc=[("ldx", 16)])
            SS_WSELB[(e, gi)] = sch.n("ldx")

        def h_group(u, r, sched, gi):
            e, jb, gblki, tc = sched[u]
            lsem, LDW, buf = blk_rec[gblki]
            rec = unit_rec.setdefault(u, {"h_pe": {}, "uhb": u % 2})
            to = tc * TC
            for di in range(NT):
                w = ()
                if di == 0:
                    w = [(lsem, LDW), ("ss", psh_ss[r]), ("sv", SV_XNB[gi])]
                lo = (tc % 2) * TC
                sch.op("t", lambda t, b=buf, r=r, di=di, lo=lo: t.matmul(
                    psH[:, r * TC:(r + 1) * TC],
                    w1b[b][:, di * MJB * 128 + r * 128:
                           di * MJB * 128 + (r + 1) * 128],
                    xn1b[:, di * GRP + lo: di * GRP + lo + TC],
                    start=(di == 0), stop=(di == NT - 1)),
                    wait=w, inc=([("pe", 1)] if di == NT - 1 else []))
            rec["h_pe"][r] = sch.n("pe")

        def relu_hts(u, r, sched, gi):
            e, jb, gblki, tc = sched[u]
            rec = unit_rec[u]
            jj = jb * MJB + r
            hb = rec["uhb"]
            wr = [("pe", rec["h_pe"][r]), ("sv", sch.n("sv"))]
            nr = unit_rec.get(u + 1)
            if r % 2 == 1 and nr is not None and (r - 1) in nr["h_pe"]:
                wr.append(("pe", nr["h_pe"][r - 1]))
            sch.op("s", lambda s, r=r, e=e, jj=jj: s.activation(
                out=hfr[:], in_=psH[:, r * TC:(r + 1) * TC], func=AF.Relu,
                bias=b1_s[:, e * NMT + jj: e * NMT + jj + 1]),
                wait=wr, inc=[("ss", 1)])
            psh_ss[r] = sch.n("ss")
            wm = [("ss", sch.n("ss")), ("ldx", SS_WSELB[(e, gi)])]
            if r == 0:
                wm.append(("pe", hts_pe[hb]))
            sch.op("v", lambda v, hb=hb, r=r, tc=tc, b=e % 2: v.tensor_mul(
                hts[hb][:, r * TC:(r + 1) * TC], hfr[:],
                wselB[b][:, (tc % 2) * TC:(tc % 2 + 1) * TC]),
                wait=wm, inc=[("sv", 1)])
            rec["hts_sv"] = sch.n("sv")

        def w2_unit(u, sched):
            nonlocal evac_sv_last
            e, jb, gblki, tc = sched[u]
            rec = unit_rec[u]
            lsem, LDW, buf = blk_rec[gblki]
            hb = rec["uhb"]
            to = tc * TC
            carry_b2 = (e == 0 and jb == 0 and tc not in b2done)
            if carry_b2:
                b2done.add(tc)
            for et in range(NT):
                for j in range(MJB):
                    w = []
                    if et == 0 and j == 0:
                        w = [("sv", rec["hts_sv"]), ("sv", evac_sv_last)]
                    last = (j == MJB - 1 and et == NT - 1 and not carry_b2)
                    sch.op("t", lambda t, b=buf, hb=hb, et=et, j=j, cb=carry_b2:
                        t.matmul(
                            psOut(et),
                            w2b[b][:, j * D + et * 128: j * D + (et + 1) * 128],
                            hts[hb][:, j * TC:(j + 1) * TC], start=(j == 0),
                            stop=(j == MJB - 1 and not cb)),
                        wait=w, inc=([("pe", 1)] if last else []))
                if carry_b2:
                    sch.op("t", lambda t, et=et, lo=(tc % 2) * TC: t.matmul(
                        psOut(et), b2_s[:, et * 128:(et + 1) * 128],
                        wselT_bf[0:NEXP, lo:lo + TC], start=False, stop=True),
                        wait=([("lset", LDG_SET)] if et == 0 else ()),
                        inc=([("pe", 1)] if et == NT - 1 else []))
            rec["w2_pe"] = sch.n("pe")
            hts_pe[hb] = rec["w2_pe"]

        def evac_unit(u, sched):
            nonlocal evac_sv_last
            e, jb, gblki, tc = sched[u]
            rec = unit_rec.pop(u)
            to = tc * TC
            sch.op("v", lambda v, to=to: v.tensor_add(
                xT[:].rearrange("p (et s_) -> p et s_", et=NT)[:, :, to:to + TC],
                xT[:].rearrange("p (et s_) -> p et s_", et=NT)[:, :, to:to + TC],
                psO[:].rearrange("p (et c) -> p et c", et=NT)),
                wait=[("pe", rec["w2_pe"])], inc=[("sv", 1)])
            evac_sv_last = sch.n("sv")

        def pair_nop(u, sched):
            e, jb, gblki, tc = sched[u]
            blk_gc[gblki] = unit_rec.get(u, {}).get("w2_pe", sch.n("pe"))

        def slot_end(k, sched, gi, NU, moe_lvl=4):
            if 0 <= k - 1 < NU:
                if moe_lvl >= 3:
                    w2_unit(k - 1, sched)
                if (k - 1) % 2 == 1:
                    pair_nop(k - 1, sched)
            for u2 in (k + 1, k + 2):
                if u2 < NU:
                    ensure_weights(sched, u2)

        def bcast_rows(wait_ss):
            sch.op("sy", lambda sy: sy.dma_start(
                out=mb1[:], in_=mrow[0:1, :].unsqueeze(1)
                    .broadcast_to([1, 128, GRP])),
                wait=[("ss", wait_ss), ("sv", sch.n("sv"))], inc=[("ld", 16)])
            sch.op("sy", lambda sy: sy.dma_start(
                out=rb1[:], in_=rrow[0:1, :].unsqueeze(1)
                    .broadcast_to([1, 128, GRP])), inc=[("ld", 16)])
            return sch.n("ld")

        for gi in range(NGRP):
            co = gi * GRP
            ln_stats(GRP, xT, lambda dt, co=co: dt * S + co, mrow, rrow, EPS0,
                     psP[0][0:1, :], psP[1][0:1, :], sp1, rta, rtb)
            LBC_LN = bcast_rows(sch.n("ss"))
            for dt in range(NT):
                sch.op("v", lambda v, dt=dt, co=co: v.tensor_sub(
                    xnf1[:, dt * GRP:(dt + 1) * GRP],
                    xT[:, dt * S + co: dt * S + co + GRP], mb1[:]),
                    wait=([("ld", LBC_LN)] if dt == 0 else ()))
                sch.op("v", lambda v, dt=dt: v.tensor_mul(
                    xnf1[:, dt * GRP:(dt + 1) * GRP], xnf1[:, dt * GRP:(dt + 1) * GRP],
                    rb1[:]))
                sch.op("v", lambda v, dt=dt: v.tensor_copy(
                    xn1b[:, dt * GRP:(dt + 1) * GRP],
                    xnf1[:, dt * GRP:(dt + 1) * GRP]),
                    wait=([("pe", sch.n("pe"))] if dt == 0 else ()),
                    inc=[("sv", 1)])
            SV_XN1 = sch.n("sv")
            SV_XNF1[gi] = SV_XN1
            for sti in range(GRP // 128):
                for dt in range(NT):
                    w = [("sv", SV_XN1 if sti == 0 else sch.n("sv"))] \
                        if dt == 0 else ()
                    sch.op("t", lambda t, sti=sti, dt=dt: t.matmul(
                        psG, xnf1[:, dt * GRP + sti * 128: dt * GRP + sti * 128 + 128],
                        wgate_s[:, dt * NEXP:(dt + 1) * NEXP],
                        start=(dt == 0), stop=(dt == NT - 1)),
                        wait=w, inc=([("pe", 1)] if dt == NT - 1 else []))
                def vsem(fn, wait=()):
                    sch.op("v", fn, wait=list(wait) + [("sv", sch.n("sv"))],
                           inc=[("sv", 1)])
                vsem(lambda v: v.tensor_add(lgt[:], psG, bgb_s[:]), wait=["pe"])
                vsem(lambda v: v.reduce_max(gmx[:, 0:1], lgt[:], axis=AX.X))
                vsem(lambda v: v.tensor_scalar(
                    out=gex[:], in0=lgt[:], scalar1=gmx[:, 0:1], scalar2=None,
                    op0=ALU.subtract))
                sch.op("s", lambda s: s.activation(out=gex2[:], in_=gex[:],
                                                   func=AF.Exp),
                       wait=[("sv", sch.n("sv"))], inc=[("ss", 1)])
                vsem(lambda v: v.reduce_sum(gmx[:, 1:2], gex2[:], axis=AX.X),
                     wait=[("ss", sch.n("ss"))])
                vsem(lambda v: v.tensor_scalar(
                    out=geq[:], in0=lgt[:], scalar1=gmx[:, 0:1], scalar2=None,
                    op0=ALU.is_ge))
                vsem(lambda v: v.tensor_scalar(
                    out=ginc[:], in0=geq[:], scalar1=-1.0, scalar2=1.0,
                    op0=ALU.mult, op1=ALU.add))
                sch.op("s", lambda s: s.activation(
                    out=gex[:], in_=ones8[:], func=AF.Identity,
                    scale=gmx[:, 1:2]), wait=[("sv", sch.n("sv"))], inc=[("ss", 1)])
                vsem(lambda v: v.reciprocal(gex2[:], gex[:]),
                     wait=[("ss", sch.n("ss"))])
                vsem(lambda v: v.memset(gq[:, 0:2], 1.0))
                vsem(lambda v: v.tensor_tensor_scan(
                    out=gq[:, 2:10], data0=ginc[:], data1=ones8[:], initial=1.0,
                    op0=ALU.mult, op1=ALU.mult))
                vsem(lambda v: v.tensor_mul(ginc[:], geq[:], gq[:, 1:9]))
                vsem(lambda v: v.tensor_mul(wsel[:], ginc[:], gex2[:]))
                sch.op("t", lambda t: t.transpose(psWT, wsel[:], ident[:]),
                       wait=[("sv", sch.n("sv"))], inc=[("pe", 1)])
                sch.op("v", lambda v, sti=sti: v.tensor_copy(
                    wselT_s[:, sti * 128: sti * 128 + 128], psWT),
                    wait=["pe", ("ldx", sch.n("ldx"))])
                sch.op("v", lambda v, sti=sti: v.tensor_copy(
                    wselT_bf[:, sti * 128: sti * 128 + 128], psWT),
                    inc=[("sv", 1)])
            SV_XNB[gi] = sch.n("sv")
            ln_stats(GRP, xnf1, lambda dt: dt * GRP, None, rrow, EPS1,
                     psP[0][0:1, :], psP[1][0:1, :], sp1, rta, rtb)
            sch.op("sy", lambda sy: sy.dma_start(
                out=rb1[:], in_=rrow[0:1, :].unsqueeze(1)
                    .broadcast_to([1, 128, GRP])),
                wait=[("ss", sch.n("ss")), ("sv", sch.n("sv"))], inc=[("ld", 16)])
            LBC_R2 = sch.n("ld")
            for dt in range(NT):
                sch.op("v", lambda v, dt=dt: v.scalar_tensor_tensor(
                    out=xn2f[:, dt * GRP:(dt + 1) * GRP],
                    in0=xnf1[:, dt * GRP:(dt + 1) * GRP],
                    scalar=normw_s[:, dt:dt + 1], in1=rb1[:],
                    op0=ALU.mult, op1=ALU.mult),
                    wait=([("ld", LBC_R2)] if dt == 0 else ()),
                    inc=([("sv", 1)] if dt == NT - 1 else ()))
            SV_XN2 = sch.n("sv")
            pj_pe = []
            for dt in range(NT):
                wcp = [("sv", SV_XN2)] if dt < 2 else [("pe", pj_pe[dt - 2])]
                sch.op("v", lambda v, b=dt % 2, dt=dt: v.tensor_copy(
                    xn2c[b][:], xn2f[:, dt * GRP:(dt + 1) * GRP]), wait=wcp,
                    inc=[("sv", 1)])
                w = [("sv", sch.n("sv"))]
                if dt == 0:
                    w.append(("lset", LDG_SET))
                sch.op("t", lambda t, dt=dt, b=dt % 2: t.matmul(
                    psP[0][:], xprojT_s[:, dt * 192: dt * 192 + 128],
                    xn2c[b][:], start=(dt == 0), stop=(dt == NT - 1)), wait=w)
                sch.op("t", lambda t, dt=dt, b=dt % 2: t.matmul(
                    psP[1][0:64, :], xprojT_s[:, dt * 192 + 128: dt * 192 + 192],
                    xn2c[b][:], start=(dt == 0), stop=(dt == NT - 1)),
                    inc=[("pe", 1)])
                pj_pe.append(sch.n("pe"))
            sch.op("v", lambda v: v.tensor_copy(dlo_bf[:], psP[0][0:64, :]),
                   wait=["pe", ("lbc", sch.n("lbc"))])
            sch.op("v", lambda v: v.tensor_copy(BCrow[64:128, :], psP[0][64:128, :]))
            sch.op("v", lambda v: v.tensor_copy(BCrow[0:64, :], psP[1][0:64, :]),
                   inc=[("sv", 1)])
            SV_PROJ = sch.n("sv")
            for dt in range(NT):
                b = dt % 2
                sch.op("t", lambda t, dt=dt, b=b: t.matmul(
                    psP[b][:], dtprojT_s[:, dt * 128:(dt + 1) * 128], dlo_bf[:],
                    start=True, stop=True),
                    wait=([("sv", SV_PROJ)] if dt <= 1 else [("ss", sch.n("ss") - 1)]),
                    inc=[("pe", 1)])
                sch.op("s", lambda s, dt=dt, b=b: s.activation(
                    out=sp1[:], in_=psP[b][:], func=AF.Abs,
                    bias=dtprojb_s[:, dt:dt + 1]),
                    wait=["pe", ("sv", sch.n("sv"))])
                sch.op("s", lambda s, dt=dt, b=b: s.activation(
                    out=deltaf[:, dt * GRP:(dt + 1) * GRP], in_=psP[b][:],
                    func=AF.Relu, bias=dtprojb_s[:, dt:dt + 1]))
                sch.op("s", lambda s: s.activation(
                    out=sp1[:], in_=sp1[:], func=AF.Exp, scale=-1.0))
                sch.op("s", lambda s: s.activation(
                    out=sp1[:], in_=sp1[:], func=AF.Ln, bias=1.0), inc=[("ss", 1)])
                sch.op("v", lambda v, dt=dt: v.tensor_add(
                    deltaf[:, dt * GRP:(dt + 1) * GRP],
                    deltaf[:, dt * GRP:(dt + 1) * GRP], sp1[:]),
                    wait=[("ss", sch.n("ss"))])
                sch.op("v", lambda v, dt=dt: v.tensor_mul(
                    Uf[:, dt * GRP:(dt + 1) * GRP], deltaf[:, dt * GRP:(dt + 1) * GRP],
                    xn2f[:, dt * GRP:(dt + 1) * GRP]), inc=[("sv", 1)])
            for dt in range(NT):
                sch.op("v", lambda v, dt=dt: v.memset(
                    deltaf[:, dt * GRP:(dt + 1) * GRP].rearrange(
                        "p (m t) -> p m t", m=MG, t=CHUNK)[:, :, 0:1],
                    1e30), inc=([("sv", 1)] if dt == NT - 1 else []))
            SV_U = sch.n("sv")
            if phases < 11:
                break
            moe_on = (phases != 15)
            moe_lvl = {21: 1, 22: 2, 23: 3, 24: 3}.get(phases, 4)
            if phases == 24:
                b2done.update(range(NTC))
            NUCAP = {26: 4, 27: 32, 28: 48, 29: 96, 30: 64, 31: 80}.get(phases, 10 ** 9)
            sched = []
            for e in range(NEXP):
                for jb in range(MNJB):
                    gblki = gi * NEXP * MNJB + e * MNJB + jb
                    sched.append((e, jb, gblki, 2 * gi))
                    sched.append((e, jb, gblki, 2 * gi + 1))
            NU = min(len(sched), NUCAP) if moe_on else 0
            if moe_on:
                ensure_wselB(0, gi)
                ensure_wselB(1, gi)
                ensure_weights(sched, 0)
                if NU > 2:
                    ensure_weights(sched, 2)
            scan_sv = []
            for n in range(STATE):
                nb = n % 2
                wd = [("sv", SV_U)] if n < 2 else \
                     [("sv", scan_sv[(n - 1) * NT - 1])]
                sch.op("sy", lambda sy, n=n, nb=nb: sy.dma_start(
                    out=Bbig[nb][:],
                    in_=BCrow[64 + n:64 + n + 1, :].unsqueeze(1)
                        .broadcast_to([1, 128, GRP])), wait=wd, inc=[("lbc", 16)])
                sch.op("sy", lambda sy, n=n, nb=nb: sy.dma_start(
                    out=Cbig[nb][:],
                    in_=BCrow[n:n + 1, :].unsqueeze(1)
                        .broadcast_to([1, 128, GRP])), inc=[("lbc", 16)])
                LBC_N = sch.n("lbc")
                for dt in range(NT):
                    it = n * NT + dt
                    db = dt % 2
                    k, r = it // 4, it % 4
                    wda = [("sv", scan_sv[it - 2])] if it >= 2 else \
                          [("sv", SV_U)]
                    sch.op("s", lambda s, dt=dt, n=n, db=db: s.activation(
                        out=dAb[db][:], in_=deltaf[:, dt * GRP:(dt + 1) * GRP],
                        func=AF.Exp,
                        scale=aneg_s[:, dt * STATE + n: dt * STATE + n + 1]),
                        wait=wda, inc=[("ss", 1)])
                    SS_DA = sch.n("ss")
                    sch.op("v", lambda v, dt=dt, db=db, nb=nb: v.tensor_mul(
                        wSb[:], Uf[:, dt * GRP:(dt + 1) * GRP], Bbig[nb][:]),
                        wait=[("lbc", LBC_N), ("ss", SS_DA)])
                    sch.op("v", lambda v, db=db: v.tensor_tensor_scan(
                        out=hSb[:], data0=dAb[db][:], data1=wSb[:],
                        initial=0.0, op0=ALU.mult, op1=ALU.add))
                    if n == 0:
                        sch.op("v", lambda v, dt=dt, nb=nb: v.tensor_mul(
                            Yf[:, dt * GRP:(dt + 1) * GRP], hSb[:], Cbig[nb][:]),
                            inc=[("sv", 1)])
                    else:
                        sch.op("v", lambda v, nb=nb: v.tensor_mul(
                            tmpS[:], hSb[:], Cbig[nb][:]))
                        sch.op("v", lambda v, dt=dt: v.tensor_add(
                            Yf[:, dt * GRP:(dt + 1) * GRP],
                            Yf[:, dt * GRP:(dt + 1) * GRP], tmpS[:]),
                            inc=[("sv", 1)])
                    scan_sv.append(sch.n("sv"))
                    if 0 <= k - 1 < NU and moe_lvl >= 2:
                        relu_hts(k - 1, r, sched, gi)
                        if r == 3 and (k - 1) % 16 == 15:
                            e_done = sched[k - 1][0]
                            wselB_free[e_done % 2] = sch.n("sv")
                            ensure_wselB(e_done + 2, gi)
                    if k < NU:
                        h_group(k, r, sched, gi)
                    if r == 1 and 0 <= k - 2 < NU and moe_lvl >= 4:
                        evac_unit(k - 2, sched)
                    if r == 3:
                        slot_end(k, sched, gi, NU, moe_lvl)
            if moe_on and NU == len(sched):
                if moe_lvl >= 2:
                    for r in range(4):
                        relu_hts(NU - 1, r, sched, gi)
                if moe_lvl >= 4:
                    evac_unit(NU - 2, sched)
                if moe_lvl >= 3:
                    w2_unit(NU - 1, sched)
                pair_nop(NU - 1, sched)
                if moe_lvl >= 4:
                    evac_unit(NU - 1, sched)
                wselB_free[0] = sch.n("sv")
                wselB_free[1] = sch.n("sv")
            if gi == 0:
                for nm, buf in (("Yg0", Yf), ("Dg0", deltaf), ("Ug0", Uf)):
                    if TAP(nm, [128, NT * GRP]) is not None:
                        sch.op("sy", lambda sy, nm=nm, b=buf: sy.dma_start(
                            out=tap_h[nm][:], in_=b[:]),
                            wait=[("sv", sch.n("sv")), ("ss", sch.n("ss")),
                                  ("pe", sch.n("pe"))], inc=[("st", 16)])
                if TAP("Wg0", [NEXP, GRP]) is not None:
                    sch.op("sy", lambda sy: sy.dma_start(
                        out=tap_h["Wg0"][:], in_=wselT_s[:]),
                        wait=[("sv", sch.n("sv"))], inc=[("st", 16)])
                if tap_h:
                    sch.barrier("v", [("st", sch.n("st"))])
            for dt in range(NT):
                sch.op("v", lambda v, dt=dt, co=co: v.scalar_tensor_tensor(
                    out=oin_bf[:, dt * GRP:(dt + 1) * GRP],
                    in0=xn1b[:, dt * GRP:(dt + 1) * GRP],
                    scalar=dvec_s[:, dt:dt + 1],
                    in1=Yf[:, dt * GRP:(dt + 1) * GRP], op0=ALU.mult, op1=ALU.add),
                    inc=([("sv", 1)] if dt == NT - 1 else []))
            SV_OIN = sch.n("sv")
            for et in range(NT):
                sch.op("g", lambda g, et=et: g.dma_start(
                    out=opslc[0][:].rearrange("p (dt e) -> p dt e", dt=NT),
                    in_=outprojT_h[:, et * 128:(et + 1) * 128].rearrange(
                        "(dt p) e -> p dt e", p=128)),
                    wait=[("pe", op_pe_last)], inc=[("lop", 16)])
                LDW = sch.n("lop")
                for di in range(NT):
                    w = [("lop", LDW), ("sv", SV_OIN), ("sv", sch.n("sv"))] \
                        if di == 0 else ()
                    sch.op("t", lambda t, di=di: t.matmul(
                        psOP[:], opslc[0][:, di * 128:(di + 1) * 128],
                        oin_bf[:, di * GRP:(di + 1) * GRP],
                        start=(di == 0), stop=(di == NT - 1)),
                        wait=w, inc=([("pe", 1)] if di == NT - 1 else []))
                op_pe_last = sch.n("pe")
                sch.op("v", lambda v, et=et, co=co: v.tensor_add(
                    xT[:, et * S + co: et * S + co + GRP],
                    xT[:, et * S + co: et * S + co + GRP], psOP[:]),
                    wait=["pe"], inc=[("sv", 1)])
            if phases < 12 + gi or phases in (21, 22, 23, 24, 26, 27, 28, 29, 30, 31):
                break
        flush()
        l1.close()
        l1c.close()

        emit_output()

    return nc, h_in, tap_h


def _layout_inputs(inputs):
    f = np.ascontiguousarray
    def pp(v):
        return f(v.reshape(NT, 128).T)
    return {
        "wqT": f(inputs["L0_attn_Wq"].T), "wkT": f(inputs["L0_attn_Wk"].T),
        "wvT": f(inputs["L0_attn_Wv"].T), "woT": f(inputs["L0_attn_Wo"].T),
        "bq": pp(inputs["L0_attn_bq"]), "bk": pp(inputs["L0_attn_bk"]),
        "bv": pp(inputs["L0_attn_bv"]), "bo": pp(inputs["L0_attn_bo"]),
        "wrT": f(inputs["L0_ffn_W_r"].T), "wkkT": f(inputs["L0_ffn_W_k"].T),
        "wvvT": f(inputs["L0_ffn_W_v"].T),
        "mur": pp(inputs["L0_ffn_mu_r"]), "muk": pp(inputs["L0_ffn_mu_k"]),
        "normw": pp(inputs["L1_attn_norm_w"]),
        "xprojT": f(inputs["L1_attn_x_proj_w"].T),
        "dtprojT": f(inputs["L1_attn_dt_proj_w"].T),
        "dtprojb": pp(inputs["L1_attn_dt_proj_b"]),
        "alog": f(inputs["L1_attn_A_log"].reshape(NT, 128, STATE)
                  .transpose(1, 0, 2).reshape(128, NT * STATE)),
        "dvec": pp(inputs["L1_attn_D"]),
        "outprojT": f(inputs["L1_attn_out_proj_w"].T),
        "wgate": f(inputs["L1_ffn_w_gate"].reshape(NT, 128, NEXP)
                   .transpose(1, 0, 2).reshape(128, NT * NEXP)),
        "bgate": f(inputs["L1_ffn_b_gate"].reshape(1, NEXP)),
        "w1": f(inputs["L1_ffn_W1"]),
        "b1": f(inputs["L1_ffn_b1"].reshape(NEXP, NMT, 128)
                .transpose(2, 0, 1).reshape(128, NEXP * NMT)),
        "w2": f(inputs["L1_ffn_W2"]),
        "b2": f(inputs["L1_ffn_b2"]),
    }


def kernel(**inputs):
    nc, h_in, _ = build()
    shared = _layout_inputs(inputs)
    x = inputs["L0_attn_x"]
    in_maps = []
    for b in range(8):
        im = dict(shared)
        im["x"] = np.ascontiguousarray(x[b])
        in_maps.append(im)
    res = run_bass_kernel_spmd(nc, in_maps, core_ids=list(range(8)))
    return np.stack([np.asarray(res.results[i]["y"]) for i in range(8)], axis=0)


# version 04 (final)
# speedup vs baseline: 1.1781x; 1.0106x over previous
import numpy as np
from contextlib import ExitStack
import concourse.bass as bass
import concourse.mybir as mybir
from concourse.bass_utils import run_bass_kernel_spmd

F32 = mybir.dt.float32
BF16 = mybir.dt.bfloat16
AF = mybir.ActivationFunctionType
ALU = mybir.AluOpType
AX = mybir.AxisListType

D, S = 1024, 2048
NT = D // 128
H, HD = 16, 64
NST = S // 128
CH = 256
NCH = S // CH
STATE, CHUNK = 64, 64
GRP = 512
NGRP = S // GRP
MG = GRP // CHUNK
NEXP, MLP = 8, 4096
NMT = MLP // 128
JB = 8
NJB = NMT // JB
TC = 256
NTC = S // TC
EPS0, EPS1 = 1e-5, 1e-6
SMSHIFT = -12.0
VROW = H * 68


class Sched:
    FIFO_SEM = {"t": "pe", "v": "sv", "s": "ss"}

    def __init__(self):
        self.ops = []
        self.count = {}
        self.waited = {}

    def n(self, sem):
        return self.count.get(sem, 0)

    def op(self, eng, fn, wait=(), inc=()):
        waits = []
        for w in wait:
            waits.append(w if isinstance(w, tuple) else (w, self.count.get(w, 0)))
        incs = []
        for s_, amt in inc:
            self.count[s_] = self.count.get(s_, 0) + amt
            incs.append((s_, amt))
        self.ops.append((eng, fn, waits, incs))

    def barrier(self, eng, sems):
        self.op(eng, None, wait=sems)

    def flush(self, blk, semh):
        ops, self.ops = self.ops, []
        for name in ("sy", "g", "t", "v", "s"):
            mine = [o for o in ops if o[0] == name]
            if not mine:
                continue
            waited = self.waited.setdefault(name, {})

            def body(eng, mine=mine, waited=waited):
                for _, fn, waits, incs in mine:
                    for s_, v_ in waits:
                        if v_ <= 0 or waited.get(s_, -1) >= v_:
                            continue
                        eng.wait_ge(semh[s_], v_)
                        waited[s_] = v_
                    if fn is None:
                        continue
                    instr = fn(eng)
                    for s_, amt in incs:
                        instr.then_inc(semh[s_], amt)

            getattr(blk, {"sy": "sync", "g": "gpsimd", "t": "tensor",
                          "v": "vector", "s": "scalar"}[name])(body)


def build(taps=(), phases=99, racecheck=False):
    taps = set(taps)
    nc = bass.Bass("TRN2", target_bir_lowering=False,
                   detect_race_conditions=racecheck)

    h_in = {}
    def din(name, shape, dt=F32):
        h_in[name] = nc.dram_tensor(name, list(shape), dt, kind="ExternalInput").ap()
        return h_in[name]

    x_h = din("x", [S, D])
    wqT_h = din("wqT", [D, D]); wkT_h = din("wkT", [D, D])
    wvT_h = din("wvT", [D, D]); woT_h = din("woT", [D, D])
    bq_h = din("bq", [128, NT]); bk_h = din("bk", [128, NT])
    bv_h = din("bv", [128, NT]); bo_h = din("bo", [128, NT])
    wrT_h = din("wrT", [D, D]); wkkT_h = din("wkkT", [D, D]); wvvT_h = din("wvvT", [D, D])
    mur_h = din("mur", [128, NT]); muk_h = din("muk", [128, NT])
    normw_h = din("normw", [128, NT])
    xprojT_h = din("xprojT", [D, 192])
    dtprojT_h = din("dtprojT", [64, D])
    dtprojb_h = din("dtprojb", [128, NT])
    alog_h = din("alog", [128, NT * STATE])
    dvec_h = din("dvec", [128, NT])
    outprojT_h = din("outprojT", [D, D])
    wgate_h = din("wgate", [128, NT * NEXP])
    bgate_h = din("bgate", [1, NEXP])
    w1_h = din("w1", [NEXP, D, MLP])
    b1_h = din("b1", [128, NEXP * NMT])
    w2_h = din("w2", [NEXP, MLP, D])
    b2_h = din("b2", [NEXP, D])

    y_h = nc.dram_tensor("y", [S, D], F32, kind="ExternalOutput").ap()
    tap_h = {}
    def TAP(name, shape, dt=F32):
        if name in taps and name not in tap_h:
            tap_h[name] = nc.dram_tensor("tap_" + name, list(shape), dt,
                                         kind="ExternalOutput").ap()
        return tap_h.get(name)

    sch = Sched()
    with ExitStack() as CTX:
        blk = CTX.enter_context(nc.Block())
        SEMH = {}
        for nm in ("ld", "ldx", "lset", "lffn", "lwv", "lqk", "lwo", "lop",
                   "lma", "lmb", "lbc", "gc", "pe", "sv", "ss", "st", "sta", "stb"):
            SEMH[nm] = CTX.enter_context(nc.semaphore(nm))

        def sb(stack, name, shape, dt):
            return stack.enter_context(nc.sbuf_tensor(name, shape, dt))
        def ps(stack, name, shape, dt=F32):
            return stack.enter_context(nc.psum_tensor(name, shape, dt))
        def flush():
            sch.flush(blk, SEMH)

        xT = sb(CTX, "xT", [128, NT * S], F32)
        ones1 = sb(CTX, "ones1", [1, 128], F32)
        onesc = sb(CTX, "onesc", [128, 1], F32)
        ident = sb(CTX, "ident", [128, 128], F32)
        bq_s = sb(CTX, "bq_s", [128, NT], F32); bk_s = sb(CTX, "bk_s", [128, NT], F32)
        bv_s = sb(CTX, "bv_s", [128, NT], F32); bo_s = sb(CTX, "bo_s", [128, NT], F32)
        bv_bf = sb(CTX, "bv_bf", [128, NT], BF16)
        bo2_s = sb(CTX, "bo2_s", [128, NT], F32)
        mur_s = sb(CTX, "mur_s", [128, NT], F32); muk_s = sb(CTX, "muk_s", [128, NT], F32)
        murc_s = sb(CTX, "murc_s", [128, NT], F32); mukc_s = sb(CTX, "mukc_s", [128, NT], F32)
        normw_s = sb(CTX, "normw_s", [128, NT], F32)
        dtprojb_s = sb(CTX, "dtprojb_s", [128, NT], F32)
        aneg_s = sb(CTX, "aneg_s", [128, NT * STATE], F32)
        dvec_s = sb(CTX, "dvec_s", [128, NT], F32)
        wgate_s = sb(CTX, "wgate_s", [128, NT * NEXP], F32)
        bgate_s = sb(CTX, "bgate_s", [1, NEXP], F32)
        bgb_s = sb(CTX, "bgb_s", [128, NEXP], F32)
        b1_s = sb(CTX, "b1_s", [128, NEXP * NMT], F32)
        b2_s = sb(CTX, "b2_s", [NEXP, D], BF16)
        xprojT_s = sb(CTX, "xprojT_s", [128, NT * 192], BF16)
        dtprojT_s = sb(CTX, "dtprojT_s", [64, D], BF16)
        ones8 = sb(CTX, "ones8", [128, NEXP], F32)
        prevcol = sb(CTX, "prevcol", [128, NT], F32)
        CONSTS = [float(EPS0), float(EPS1), 1.0, float(SMSHIFT)]
        cst = sb(CTX, "cst", [128, len(CONSTS)], F32)
        for ci, cv in enumerate(CONSTS):
            nc.const_aps.register(F32, cv, cst[:, ci:ci + 1])

        su = ExitStack()
        x_nat = sb(su, "x_nat", [128, (NST // 2) * D], F32)
        alog_s = sb(su, "alog_s", [128, NT * STATE], F32)
        psT = [ps(su, "psT0", [128, 512]), ps(su, "psT1", [128, 512])]
        psBG = ps(su, "psBG", [128, NEXP])

        for dst, src in [(bq_s, bq_h), (bk_s, bk_h), (bv_s, bv_h), (bo_s, bo_h),
                         (mur_s, mur_h), (muk_s, muk_h), (normw_s, normw_h),
                         (dtprojb_s, dtprojb_h), (alog_s, alog_h), (dvec_s, dvec_h),
                         (wgate_s, wgate_h), (bgate_s, bgate_h), (b1_s, b1_h)]:
            sch.op("sy", lambda sy, d=dst, s_=src: sy.dma_start(out=d[:], in_=s_[:]),
                   inc=[("ld", 16)])
        LD_SM = sch.n("ld")

        sch.op("g", lambda g: g.memset(ident[:], 1.0), inc=[("gc", 1)])
        sch.op("g", lambda g: g.affine_select(
            out=ident[:], in_=ident[:], pattern=[[1, 128]], compare_op=ALU.is_equal,
            fill=0.0, base=0, channel_multiplier=-1), wait=["gc"], inc=[("gc", 1)])
        GC_ID = sch.n("gc")
        sch.op("g", lambda g: g.dma_start(out=b2_s[:], in_=b2_h[:]), inc=[("lset", 16)])
        sch.op("g", lambda g: g.dma_start(
            out=xprojT_s[:].rearrange("p (dt r) -> p dt r", dt=NT),
            in_=xprojT_h[:].rearrange("(dt p) r -> p dt r", p=128)),
            inc=[("lset", 16)])
        sch.op("g", lambda g: g.dma_start(out=dtprojT_s[:], in_=dtprojT_h[:]),
               inc=[("lset", 16)])
        LDG_SET = sch.n("lset")

        sch.op("v", lambda v: v.memset(ones1[:], 1.0))
        sch.op("v", lambda v: v.memset(onesc[:], 1.0))
        sch.op("v", lambda v: v.memset(ones8[:], 1.0))
        for ci, cv in enumerate(CONSTS):
            sch.op("v", lambda v, ci=ci, cv=cv: v.memset(cst[:, ci:ci + 1], cv))
        sch.op("v", lambda v: v.memset(prevcol[:], 0.0), inc=[("sv", 1)])
        SV_ONES = sch.n("sv")
        sch.barrier("v", [("ld", LD_SM)])
        sch.op("v", lambda v: v.tensor_scalar(
            out=murc_s[:], in0=mur_s[:], scalar1=-1.0, scalar2=1.0,
            op0=ALU.mult, op1=ALU.add))
        sch.op("v", lambda v: v.tensor_scalar(
            out=mukc_s[:], in0=muk_s[:], scalar1=-1.0, scalar2=1.0,
            op0=ALU.mult, op1=ALU.add), inc=[("sv", 1)])
        sch.op("v", lambda v: v.tensor_copy(bv_bf[:], bv_s[:]),
               wait=[("ld", LD_SM)], inc=[("sv", 1)])
        sch.op("s", lambda s: s.activation(out=aneg_s[:], in_=alog_s[:], func=AF.Exp),
               wait=[("ld", LD_SM)], inc=[("ss", 1)])
        SS_ANEG = sch.n("ss")
        sch.op("v", lambda v: v.tensor_scalar_mul(aneg_s[:], aneg_s[:], -1.0),
               wait=[("ss", SS_ANEG)], inc=[("sv", 1)])
        sch.op("t", lambda t: t.matmul(psBG[:], ones1[:], bgate_s[:], start=True,
                                       stop=True),
               wait=[("sv", SV_ONES), ("ld", LD_SM)], inc=[("pe", 1)])
        sch.op("v", lambda v: v.tensor_copy(bgb_s[:], psBG[:]), wait=["pe"],
               inc=[("sv", 1)])

        SV_T0 = sch.n("sv")
        gi = 0
        for half in range(2):
            for i in range(NST // 2):
                sti = half * 8 + i
                sch.op("sy", lambda sy, i=i, sti=sti: sy.dma_start(
                    out=x_nat[:, i * D:(i + 1) * D],
                    in_=x_h[sti * 128:(sti + 1) * 128, :]), inc=[("ldx", 16)])
            LD_H = sch.n("ldx")
            for dt_j in range(NT):
                for sg in range(2):
                    bank = gi % 2
                    for q in range(4):
                        i = sg * 4 + q
                        w = [("gc", GC_ID), ("ldx", LD_H)]
                        if gi >= 2 and q == 0:
                            w.append(("sv", SV_T0 + gi - 1))
                        sch.op("t", lambda t, b=bank, q=q, i=i, dj=dt_j: t.transpose(
                            psT[b][:, q * 128:(q + 1) * 128],
                            x_nat[:, i * D + dj * 128: i * D + dj * 128 + 128],
                            ident[:]),
                            wait=w, inc=([("pe", 1)] if q == 3 else []))
                    so = half * 1024 + sg * 512
                    sch.op("v", lambda v, b=bank, dj=dt_j, so=so: v.tensor_copy(
                        xT[:, dj * S + so: dj * S + so + 512], psT[b][:]),
                        wait=["pe"], inc=[("sv", 1)])
                    gi += 1
            if half == 0:
                sch.barrier("sy", [("pe", sch.n("pe"))])
        flush()
        su.close()

        def emit_output():
            ou = ExitStack()
            ynat = [sb(ou, "ynat0", [128, D], F32), sb(ou, "ynat1", [128, D], F32)]
            psY = [ps(ou, "psY0", [128, 512]), ps(ou, "psY1", [128, 512]),
                   ps(ou, "psY2", [128, 512]), ps(ou, "psY3", [128, 512])]
            SV_ALL = sch.n("sv")
            SS_ALL = sch.n("ss")
            for sti in range(NST):
                yb = sti % 2
                for dt in range(NT):
                    bank = yb * 2 + dt // 4
                    w = []
                    if sti == 0 and dt == 0:
                        w = [("sv", SV_ALL), ("ss", SS_ALL)]
                    if sti >= 2 and dt == 0:
                        w = [("sta" if sti % 2 == 0 else "stb",
                              sch.n("sta" if sti % 2 == 0 else "stb"))]
                    sch.op("t", lambda t, dt=dt, sti=sti, bank=bank: t.transpose(
                        psY[bank][:, (dt % 4) * 128:(dt % 4 + 1) * 128],
                        xT[:, dt * S + sti * 128: dt * S + sti * 128 + 128],
                        ident[:]),
                        wait=w, inc=([("pe", 1)] if dt == NT - 1 else []))
                sch.op("v", lambda v, yb=yb: v.tensor_copy(
                    ynat[yb][:, 0:512], psY[yb * 2][:]), wait=["pe"])
                sch.op("v", lambda v, yb=yb: v.tensor_copy(
                    ynat[yb][:, 512:1024], psY[yb * 2 + 1][:]), inc=[("sv", 1)])
                sch.op("sy", lambda sy, sti=sti, yb=yb: sy.dma_start(
                    out=y_h[sti * 128:(sti + 1) * 128, :], in_=ynat[yb][:]),
                    wait=[("sv", sch.n("sv"))],
                    inc=[("sta" if sti % 2 == 0 else "stb", 16)])
            sch.barrier("sy", [("sta", sch.n("sta")), ("stb", sch.n("stb")),
                               ("st", sch.n("st"))])
            flush()
            ou.close()

        if phases < 1:
            emit_output()
            return nc, h_in, tap_h

        xn_st = ExitStack()
        xnbf = sb(xn_st, "xnbf", [128, NT * S], BF16)
        l0 = ExitStack()
        mrow = sb(l0, "mrow", [1, GRP], F32)
        rrow = sb(l0, "rrow", [1, GRP], F32)
        rta = sb(l0, "rta", [1, GRP], F32)
        rtb = sb(l0, "rtb", [1, GRP], F32)
        wr_sb = sb(l0, "wr_sb", [128, NT * D], BF16)
        wkk_sb = sb(l0, "wkk_sb", [128, NT * D], BF16)
        wvv_sb = sb(l0, "wvv_sb", [128, NT * D], BF16)
        xnf = sb(l0, "xnf", [128, NT * CH], F32)
        xr_bf = sb(l0, "xr_bf", [128, NT * CH], BF16)
        xk_bf = sb(l0, "xk_bf", [128, NT * CH], BF16)
        ktmp = sb(l0, "ktmp", [128, CH], F32)
        kact = sb(l0, "kact", [128, NT * CH], BF16)
        rsig = sb(l0, "rsig", [128, CH], F32)
        mixt = sb(l0, "mixt", [128, CH], F32)
        xsq = sb(l0, "xsq", [128, CH], F32)
        psRow = ps(l0, "psRow", [1, 2 * CH])
        psBC = ps(l0, "psBC", [128, 2 * CH])
        psMM = [ps(l0, "psMM0", [128, CH]), ps(l0, "psMM1", [128, CH])]

        PE_SETUP = sch.n("pe")
        for wdst, wsrc in ((wr_sb, wrT_h), (wkk_sb, wkkT_h), (wvv_sb, wvvT_h)):
            sch.op("g", lambda g, d=wdst, s_=wsrc: g.dma_start(
                out=d[:].rearrange("p (dt e) -> p dt e", dt=NT),
                in_=s_[:].rearrange("(dt p) e -> p dt e", p=128)),
                wait=[("pe", PE_SETUP)], inc=[("lffn", 16)])
        LDG_FFN = sch.n("lffn")

        def ln_stats(W, xsrc, xoff, mdst, rdst, eps, prow_a, prow_b, sqbuf, rta, rtb):
            sch.barrier("t", [("sv", sch.n("sv")), ("ss", sch.n("ss"))])
            for dt in range(NT):
                sch.op("t", lambda t, dt=dt: t.matmul(
                    prow_a[0:1, 0:W], onesc[:], xsrc[:, xoff(dt): xoff(dt) + W],
                    start=(dt == 0), stop=(dt == NT - 1)),
                    inc=([("pe", 1)] if dt == NT - 1 else []))
            PE_SQ = sch.n("pe")
            for dt in range(NT):
                wsq = [("sv", sch.n("sv"))] if dt == 0 else \
                      [("pe", PE_SQ + dt)]
                sch.op("s", lambda s, dt=dt: s.activation(
                    out=sqbuf[:, 0:W], in_=xsrc[:, xoff(dt): xoff(dt) + W],
                    func=AF.Square), wait=wsq, inc=[("ss", 1)])
                sch.op("t", lambda t, dt=dt: t.matmul(
                    prow_b[0:1, 0:W], onesc[:], sqbuf[:, 0:W],
                    start=(dt == 0), stop=(dt == NT - 1)), wait=["ss"],
                    inc=[("pe", 1)])
            if mdst is not None:
                sch.op("v", lambda v: v.tensor_scalar_mul(
                    mdst[0:1, 0:W], prow_a[0:1, 0:W], 1.0 / D),
                    wait=["pe", ("ss", sch.n("ss"))])
                sch.op("v", lambda v: v.tensor_scalar_mul(
                    rta[0:1, 0:W], prow_b[0:1, 0:W], 1.0 / D), inc=[("sv", 1)])
                sch.op("v", lambda v: v.tensor_mul(
                    rtb[0:1, 0:W], mdst[0:1, 0:W], mdst[0:1, 0:W]),
                    wait=[("sv", sch.n("sv"))], inc=[("sv", 1)])
                sch.op("v", lambda v: v.tensor_sub(
                    rta[0:1, 0:W], rta[0:1, 0:W], rtb[0:1, 0:W]),
                    wait=[("sv", sch.n("sv"))], inc=[("sv", 1)])
            else:
                sch.op("v", lambda v: v.tensor_scalar_mul(
                    rta[0:1, 0:W], prow_b[0:1, 0:W], 1.0 / D),
                    wait=["pe", ("ss", sch.n("ss"))], inc=[("sv", 1)])
            sch.op("s", lambda s: s.activation(
                out=rtb[0:1, 0:W], in_=rta[0:1, 0:W], func=AF.Ln,
                bias=float(eps)), wait=["sv"])
            sch.op("s", lambda s, rdst=rdst: s.activation(
                out=rdst[0:1, 0:W], in_=rtb[0:1, 0:W], func=AF.Exp,
                scale=-0.5), inc=[("ss", 1)])

        for c in range(NCH):
            co = c * CH
            ln_stats(CH, xT, lambda dt, co=co: dt * S + co, mrow, rrow, EPS0,
                     psRow[0:1, 0:CH], psRow[0:1, CH:2 * CH], xsq, rta, rtb)
            SS_R = sch.n("ss")
            sch.op("t", lambda t: t.matmul(
                psBC[:, 0:CH], ones1[:], mrow[0:1, 0:CH], start=True, stop=True),
                wait=[("ss", SS_R), ("sv", sch.n("sv"))])
            sch.op("t", lambda t: t.matmul(
                psBC[:, CH:2 * CH], ones1[:], rrow[0:1, 0:CH], start=True,
                stop=True), inc=[("pe", 1)])
            PE_BC = sch.n("pe")
            for dt in range(NT):
                sch.op("v", lambda v, dt=dt, co=co: v.scalar_tensor_tensor(
                    out=xnf[:, dt * CH:(dt + 1) * CH],
                    in0=xT[:, dt * S + co: dt * S + co + CH], scalar=1.0,
                    in1=psBC[:, 0:CH], op0=ALU.mult, op1=ALU.subtract),
                    wait=([("pe", PE_BC)] if dt == 0 else ()))
                sch.op("v", lambda v, dt=dt: v.tensor_mul(
                    xnf[:, dt * CH:(dt + 1) * CH], xnf[:, dt * CH:(dt + 1) * CH],
                    psBC[:, CH:2 * CH]))
                sch.op("v", lambda v, dt=dt, co=co: v.tensor_copy(
                    xnbf[:, dt * S + co: dt * S + co + CH],
                    xnf[:, dt * CH:(dt + 1) * CH]), inc=[("sv", 1)])
            for (dst, mu, muc) in ((xr_bf, mur_s, murc_s), (xk_bf, muk_s, mukc_s)):
                for dt in range(NT):
                    sch.op("v", lambda v, dt=dt, mu=mu: v.tensor_scalar(
                        out=mixt[:, 0:CH], in0=xnf[:, dt * CH:(dt + 1) * CH],
                        scalar1=mu[:, dt:dt + 1], scalar2=None, op0=ALU.mult))
                    sch.op("v", lambda v, dt=dt, dst=dst, muc=muc:
                        v.scalar_tensor_tensor(
                            out=dst[:, dt * CH + 1:(dt + 1) * CH],
                            in0=xnf[:, dt * CH: dt * CH + CH - 1],
                            scalar=muc[:, dt:dt + 1], in1=mixt[:, 1:CH],
                            op0=ALU.mult, op1=ALU.add))
                    sch.op("v", lambda v, dt=dt, dst=dst, muc=muc:
                        v.scalar_tensor_tensor(
                            out=dst[:, dt * CH: dt * CH + 1],
                            in0=prevcol[:, dt:dt + 1], scalar=muc[:, dt:dt + 1],
                            in1=mixt[:, 0:1], op0=ALU.mult, op1=ALU.add))
            for dt in range(NT):
                sch.op("v", lambda v, dt=dt: v.tensor_copy(
                    prevcol[:, dt:dt + 1], xnf[:, dt * CH + CH - 1: (dt + 1) * CH]),
                    inc=([("sv", 1)] if dt == NT - 1 else []))
            SV_MIX = sch.n("sv")
            for do in range(NT):
                b = do % 2
                for di in range(NT):
                    w = []
                    if di == 0:
                        w = [("sv", SV_MIX), ("ss", sch.n("ss")), ("lffn", LDG_FFN)]
                    sch.op("t", lambda t, do=do, di=di, b=b: t.matmul(
                        psMM[b][:], wkk_sb[:, di * D + do * 128: di * D + (do + 1) * 128],
                        xk_bf[:, di * CH:(di + 1) * CH],
                        start=(di == 0), stop=(di == NT - 1)), wait=w,
                        inc=([("pe", 1)] if di == NT - 1 else []))
                sch.op("s", lambda s, b=b: s.activation(
                    out=ktmp[:], in_=psMM[b][:], func=AF.Relu),
                    wait=["pe", ("sv", sch.n("sv"))], inc=[("ss", 1)])
                sch.op("v", lambda v, do=do: v.tensor_mul(
                    kact[:, do * CH:(do + 1) * CH], ktmp[:], ktmp[:]),
                    wait=["ss"], inc=[("sv", 1)])
            for do in range(NT):
                b = do % 2
                for di in range(NT):
                    w = [("ss", sch.n("ss")), ("sv", sch.n("sv"))] if di == 0 else ()
                    sch.op("t", lambda t, do=do, di=di, b=b: t.matmul(
                        psMM[b][:], wr_sb[:, di * D + do * 128: di * D + (do + 1) * 128],
                        xr_bf[:, di * CH:(di + 1) * CH],
                        start=(di == 0), stop=(di == NT - 1)), wait=w,
                        inc=([("pe", 1)] if di == NT - 1 else []))
                sch.op("s", lambda s, b=b: s.activation(
                    out=rsig[:], in_=psMM[b][:], func=AF.Sigmoid),
                    wait=["pe", ("sv", sch.n("sv"))], inc=[("ss", 1)])
                SS_RS = sch.n("ss")
                b2_ = (do + 1) % 2
                for di in range(NT):
                    w = [("ss", SS_RS), ("sv", sch.n("sv"))] if di == 0 else ()
                    sch.op("t", lambda t, do=do, di=di, b=b2_: t.matmul(
                        psMM[b][:], wvv_sb[:, di * D + do * 128: di * D + (do + 1) * 128],
                        kact[:, di * CH:(di + 1) * CH],
                        start=(di == 0), stop=(di == NT - 1)), wait=w,
                        inc=([("pe", 1)] if di == NT - 1 else []))
                sch.op("v", lambda v, b=b2_: v.tensor_mul(
                    mixt[:], psMM[b][:], rsig[:]), wait=["pe"])
                sch.op("v", lambda v, do=do, co=co: v.tensor_add(
                    xT[:, do * S + co: do * S + co + CH],
                    xT[:, do * S + co: do * S + co + CH], mixt[:]), inc=[("sv", 1)])
        flush()
        l0.close()

        if TAP("xnbf", [128, NT * S], BF16) is not None:
            sch.op("sy", lambda sy: sy.dma_start(out=tap_h["xnbf"][:], in_=xnbf[:]),
                   wait=[("sv", sch.n("sv"))], inc=[("st", 16)])
        if TAP("xl0f", [128, NT * S]) is not None:
            sch.op("sy", lambda sy: sy.dma_start(out=tap_h["xl0f"][:], in_=xT[:]),
                   wait=[("sv", sch.n("sv"))], inc=[("st", 16)])
        flush()
        if phases < 2:
            emit_output()
            return nc, h_in, tap_h

        vp = ExitStack()
        VB = sb(vp, "VB", [128, NST * VROW], BF16)
        vpw = ExitStack()
        wvslc = [sb(vpw, "wvslc0", [128, NT * 256], BF16),
                 sb(vpw, "wvslc1", [128, NT * 256], BF16)]
        psV = [ps(vpw, "psV0", [128, 256]), ps(vpw, "psV1", [128, 256])]

        SV_L0 = sch.n("sv")
        sch.op("v", lambda v: v.memset(VB[:], 0.0), wait=[("sv", SV_L0)],
               inc=[("sv", 1)])
        sch.op("v", lambda v: v.memset(
            VB[:].rearrange("p (t h c) -> p t h c", t=NST, h=H, c=68)[:, :, :, 64:65],
            1.0), inc=[("sv", 1)])
        SV_VONES = sch.n("sv")

        for ec in range(4):
            buf = ec % 2
            wv_w = [("gc", sch.n("gc"))]
            if ec == 0:
                wv_w += [("sv", SV_L0), ("ss", sch.n("ss")), ("pe", sch.n("pe"))]
            sch.op("g", lambda g, ec=ec, b=buf: g.dma_start(
                out=wvslc[b][:].rearrange("p (dt e) -> p dt e", dt=NT),
                in_=wvT_h[:, ec * 256:(ec + 1) * 256].rearrange(
                    "(dt p) e -> p dt e", p=128)),
                wait=wv_w, inc=[("lwv", 16)])
            LDW = sch.n("lwv")
            for sti in range(NST):
                b2_ = sti % 2
                for di in range(NT):
                    w = []
                    if di == 0:
                        w = [("lwv", LDW), ("sv", sch.n("sv"))]
                        if sti == 0 and ec == 0:
                            w.append(("sv", SV_VONES))
                    sch.op("t", lambda t, b=buf, b2_=b2_, di=di, sti=sti: t.matmul(
                        psV[b2_][:], xnbf[:, di * S + sti * 128: di * S + sti * 128 + 128],
                        wvslc[b][:, di * 256:(di + 1) * 256],
                        start=(di == 0), stop=(di == NT - 1)),
                        wait=w, inc=([("pe", 1)] if di == NT - 1 else []))
                sch.op("v", lambda v, sti=sti, ec=ec, b2_=b2_: v.tensor_copy(
                    VB[:, sti * VROW + (ec * 4) * 68: sti * VROW + (ec * 4 + 4) * 68]
                      .rearrange("p (h c) -> p h c", h=4, c=68)[:, :, 0:64],
                    psV[b2_][:].rearrange("p (h c) -> p h c", h=4, c=64)),
                    wait=["pe"], inc=[("sv", 1)])
            sch.op("g", lambda g: g.engine_nop(), wait=[("sv", sch.n("sv"))],
                   inc=[("gc", 1)])
        flush()
        vpw.close()
        if TAP("VB", [128, NST * VROW], BF16) is not None:
            sch.op("sy", lambda sy: sy.dma_start(out=tap_h["VB"][:], in_=VB[:]),
                   wait=[("sv", sch.n("sv"))], inc=[("st", 16)])
            flush()

        at = ExitStack()
        QTh = sb(at, "QTh", [64, S], BF16)
        KTh = sb(at, "KTh", [64, S], BF16)
        expS = sb(at, "expS", [128, NST * 512], BF16)
        rb_sb = sb(at, "rb_sb", [128, 512], F32)
        ctxf = sb(at, "ctxf", [128, NT * S], BF16)
        recipr = sb(at, "recipr", [1, 512], F32)
        wqkslc = [sb(at, "wqks0", [128, NT * 128], BF16)]
        woslc = [sb(at, "woslc0", [128, NT * 128], BF16)]
        psS = [ps(at, "psS0", [128, 512]), ps(at, "psS1", [128, 512]),
               ps(at, "psS2", [128, 512])]
        psU = ps(at, "psU", [128, 512])
        psR = ps(at, "psR", [128, 512])
        psQK = ps(at, "psQK", [128, 512])

        for hh in range(H):
            et, sub = divmod(hh, 2)
            buf = 0
            sch.op("g", lambda g, hh=hh, b=buf: g.dma_start(
                out=wqkslc[b][:, 0:NT * 64].rearrange("p (dt e) -> p dt e", dt=NT),
                in_=wqT_h[:, hh * 64:(hh + 1) * 64].rearrange(
                    "(dt p) e -> p dt e", p=128)),
                wait=[("gc", sch.n("gc"))], inc=[("lqk", 16)])
            sch.op("g", lambda g, hh=hh, b=buf: g.dma_start(
                out=wqkslc[b][:, NT * 64:NT * 128].rearrange(
                    "p (dt e) -> p dt e", dt=NT),
                in_=wkT_h[:, hh * 64:(hh + 1) * 64].rearrange(
                    "(dt p) e -> p dt e", p=128)),
                inc=[("lqk", 16)])
            LDW = sch.n("lqk")
            for (dst, wof, bias) in ((QTh, 0, bq_s), (KTh, NT * 64, bk_s)):
                for sc in range(4):
                    for di in range(NT):
                        w = [("lqk", LDW), ("ss", sch.n("ss"))] if di == 0 else ()
                        sch.op("t", lambda t, b=buf, wof=wof, di=di, sc=sc: t.matmul(
                            psQK[0:64, :],
                            wqkslc[b][:, wof + di * 64: wof + (di + 1) * 64],
                            xnbf[:, di * S + sc * 512: di * S + (sc + 1) * 512],
                            start=(di == 0), stop=(di == NT - 1)),
                            wait=w, inc=([("pe", 1)] if di == NT - 1 else []))
                    sch.op("s", lambda s, dst=dst, sc=sc, bias=bias, et=et, sub=sub:
                        s.activation(
                            out=dst[:, sc * 512:(sc + 1) * 512], in_=psQK[0:64, :],
                            func=AF.Identity,
                            bias=bias[sub * 64:sub * 64 + 64, et:et + 1]),
                        wait=["pe"], inc=[("ss", 1)])
            SS_QK = sch.n("ss")
            sch.op("g", lambda g: g.engine_nop(), wait=[("ss", SS_QK)],
                   inc=[("gc", 1)])
            for qc in range(4):
                qo = qc * 512
                for kt in range(NST):
                    b = kt % 3
                    w = []
                    if hh == 0 and qc == 0 and kt == 0:
                        w = [("ss", SS_QK), ("sv", sch.n("sv"))]
                    elif kt >= 3 or not (hh == 0 and qc == 0):
                        w = [("ss", sch.n("ss") - 2)]
                    sch.op("t", lambda t, b=b, kt=kt, qo=qo: t.matmul(
                        psS[b][:], KTh[:, kt * 128:(kt + 1) * 128],
                        QTh[:, qo:qo + 512], start=True, stop=True),
                        wait=(w + [("ss", SS_QK)] if kt == 0 else w),
                        inc=[("pe", 1)])
                    sch.op("s", lambda s, b=b, kt=kt: s.activation(
                        out=expS[:, kt * 512:(kt + 1) * 512], in_=psS[b][:],
                        func=AF.Exp, scale=0.125, bias=SMSHIFT),
                        wait=["pe"], inc=[("ss", 1)])
                SS_EXP = sch.n("ss")
                if sub == 0:
                    for kt in range(NST):
                        w = [("ss", SS_EXP), ("sv", sch.n("sv"))] if kt == 0 else ()
                        sch.op("t", lambda t, kt=kt, hh=hh: t.matmul(
                            psU[0:65, :],
                            VB[:, kt * VROW + hh * 68: kt * VROW + hh * 68 + 65],
                            expS[:, kt * 512:(kt + 1) * 512],
                            start=(kt == 0), stop=(kt == NST - 1)), wait=w,
                            inc=([("pe", 1)] if kt == NST - 1 else []))
                    sch.op("v", lambda v: v.reciprocal(recipr[:], psU[64:65, :]),
                           wait=["pe"], inc=[("sv", 1)])
                else:
                    for kt in range(NST):
                        w = [("ss", SS_EXP), ("sv", sch.n("sv"))] if kt == 0 else ()
                        sch.op("t", lambda t, kt=kt, hh=hh: t.matmul(
                            psU[64:128, :],
                            VB[:, kt * VROW + hh * 68: kt * VROW + hh * 68 + 64],
                            expS[:, kt * 512:(kt + 1) * 512],
                            start=(kt == 0), stop=(kt == NST - 1),
                            tile_position=(0, 64)), wait=w,
                            inc=([("pe", 1)] if kt == NST - 1 else []))
                    for kt in range(NST):
                        sch.op("t", lambda t, kt=kt, hh=hh: t.matmul(
                            psU[0:1, :],
                            VB[:, kt * VROW + hh * 68 + 64: kt * VROW + hh * 68 + 65],
                            expS[:, kt * 512:(kt + 1) * 512],
                            start=(kt == 0), stop=(kt == NST - 1)),
                            inc=([("pe", 1)] if kt == NST - 1 else []))
                    sch.op("v", lambda v: v.reciprocal(recipr[:], psU[0:1, :]),
                           wait=["pe"], inc=[("sv", 1)])
                sch.op("t", lambda t: t.matmul(
                    psR[:], ones1[:], recipr[:], start=True, stop=True),
                    wait=["sv"], inc=[("pe", 1)])
                sch.op("s", lambda s: s.activation(
                    out=rb_sb[:], in_=psR[:], func=AF.Copy),
                    wait=["pe", ("sv", sch.n("sv"))],
                    inc=[("ss", 1)])
                sch.op("v", lambda v, et=et, sub=sub, qo=qo: v.tensor_mul(
                    ctxf[sub * 64:sub * 64 + 64, et * S + qo: et * S + qo + 512],
                    psU[sub * 64:sub * 64 + 64, :],
                    rb_sb[sub * 64:sub * 64 + 64, :]),
                    wait=[("ss", sch.n("ss"))], inc=[("sv", 1)])
        SV_CTX = sch.n("sv")
        for et in range(NT):
            buf = 0
            sch.op("g", lambda g, et=et, b=buf: g.dma_start(
                out=woslc[b][:].rearrange("p (dt e) -> p dt e", dt=NT),
                in_=woT_h[:, et * 128:(et + 1) * 128].rearrange(
                    "(dt p) e -> p dt e", p=128)),
                wait=[("gc", sch.n("gc"))], inc=[("lwo", 16)])
            LDW = sch.n("lwo")
            for di in range(NT):
                sch.op("t", lambda t, b=buf, di=di, et=et: t.matmul(
                    psS[2][:, et:et + 1],
                    woslc[b][:, di * 128:(di + 1) * 128],
                    bv_bf[:, di:di + 1], start=(di == 0), stop=(di == NT - 1)),
                    wait=([("lwo", LDW), ("sv", SV_CTX)] if di == 0 else ()),
                    inc=([("pe", 1)] if di == NT - 1 else []))
            sch.op("v", lambda v, et=et: v.tensor_add(
                bo2_s[:, et:et + 1], psS[2][:, et:et + 1], bo_s[:, et:et + 1]),
                wait=["pe"], inc=[("sv", 1)])
            SV_BO = sch.n("sv")
            for qc in range(4):
                b2_ = qc % 2
                for di in range(NT):
                    w = [("sv", SV_BO)] if di == 0 else ()
                    sch.op("t", lambda t, b=buf, b2_=b2_, di=di, qc=qc: t.matmul(
                        psS[b2_][:], woslc[b][:, di * 128:(di + 1) * 128],
                        ctxf[:, di * S + qc * 512: di * S + (qc + 1) * 512],
                        start=(di == 0), stop=(di == NT - 1)),
                        wait=w, inc=([("pe", 1)] if di == NT - 1 else []))
                sch.op("v", lambda v, et=et, qc=qc, b2_=b2_:
                    v.scalar_tensor_tensor(
                        out=xT[:, et * S + qc * 512: et * S + (qc + 1) * 512],
                        in0=psS[b2_][:], scalar=bo2_s[:, et:et + 1],
                        in1=xT[:, et * S + qc * 512: et * S + (qc + 1) * 512],
                        op0=ALU.add, op1=ALU.add), wait=["pe"], inc=[("sv", 1)])
            sch.op("g", lambda g: g.engine_nop(), wait=[("sv", sch.n("sv"))],
                   inc=[("gc", 1)])
        if TAP("ctxf", [128, NT * S], BF16) is not None:
            sch.op("sy", lambda sy: sy.dma_start(out=tap_h["ctxf"][:], in_=ctxf[:]),
                   wait=[("sv", sch.n("sv"))], inc=[("st", 16)])
        if TAP("QTh", [64, S], BF16) is not None:
            sch.op("sy", lambda sy: sy.dma_start(out=tap_h["QTh"][:], in_=QTh[:]),
                   wait=[("sv", sch.n("sv"))], inc=[("st", 16)])
        if TAP("KTh", [64, S], BF16) is not None:
            sch.op("sy", lambda sy: sy.dma_start(out=tap_h["KTh"][:], in_=KTh[:]),
                   wait=[("sv", sch.n("sv"))], inc=[("st", 16)])
        if TAP("expS", [128, NST * 512], BF16) is not None:
            sch.op("sy", lambda sy: sy.dma_start(out=tap_h["expS"][:], in_=expS[:]),
                   wait=[("sv", sch.n("sv"))], inc=[("st", 16)])
        flush()
        at.close()
        vp.close()
        xn_st.close()
        if TAP("xl0", [128, NT * S]) is not None:
            sch.op("sy", lambda sy: sy.dma_start(out=tap_h["xl0"][:], in_=xT[:]),
                   wait=[("sv", sch.n("sv"))], inc=[("st", 16)])
            flush()
        if phases < 3:
            emit_output()
            return nc, h_in, tap_h

        l1c = ExitStack()
        wselT_s = sb(l1c, "wselT_s", [NEXP, GRP], F32)
        wselT_bf = sb(l1c, "wselT_bf", [NEXP, GRP], BF16)
        l1 = ExitStack()
        mb1 = sb(l1, "mb1", [128, GRP], F32)
        rb1 = sb(l1, "rb1", [128, GRP], F32)
        xnf1 = sb(l1, "xnf1", [128, NT * GRP], F32)
        deltaf = xnf1
        Uf = sb(l1, "Uf", [128, NT * GRP], F32)
        xn2f = sb(l1, "xn2f", [128, NT * GRP], F32)
        Yf = xn2f
        xn1b = sb(l1, "xn1b", [128, NT * GRP], BF16)
        xn2c = [sb(l1, "xn2c0", [128, GRP], BF16),
                sb(l1, "xn2c1", [128, GRP], BF16)]
        dlo_bf = sb(l1, "dlo_bf", [64, GRP], BF16)
        BCrow = sb(l1, "BCrow", [128, GRP], F32)
        Bbig = [sb(l1, "Bbig0", [128, GRP], F32), sb(l1, "Bbig1", [128, GRP], F32)]
        Cbig = [sb(l1, "Cbig0", [128, GRP], F32), sb(l1, "Cbig1", [128, GRP], F32)]
        mrow = BCrow[0:1, :]
        rta = Bbig[0][0:1, :]
        rtb = Bbig[1][0:1, :]
        rrow = Cbig[0][0:1, :]
        dAb = [sb(l1, "dA0", [128, GRP], F32), sb(l1, "dA1", [128, GRP], F32)]
        wSb = sb(l1, "wS0", [128, GRP], F32)
        hSb = sb(l1, "hS0", [128, GRP], F32)
        tmpS = wSb
        oin_bf = xn1b
        opslc = [sb(l1, "opslc0", [128, NT * 128], BF16)]
        sp1 = sb(l1, "sp1", [128, GRP], F32)
        lgt = sb(l1, "lgt", [128, NEXP], F32)
        gex = sb(l1, "gex", [128, NEXP], F32)
        gex2 = sb(l1, "gex2", [128, NEXP], F32)
        gmx = sb(l1, "gmx", [128, 4], F32)
        geq = sb(l1, "geq", [128, NEXP], F32)
        ginc = sb(l1, "ginc", [128, NEXP], F32)
        gq = sb(l1, "gq", [128, 12], F32)
        wsel = sb(l1, "wsel", [128, NEXP], F32)
        MJB = 4
        MNJB = NMT // MJB
        w1b = [sb(l1, "w1b0", [128, NT * MJB * 128], BF16),
               sb(l1, "w1b1", [128, NT * MJB * 128], BF16)]
        w2b = [sb(l1, "w2b0", [128, MJB * D], BF16),
               sb(l1, "w2b1", [128, MJB * D], BF16)]
        hts = [sb(l1, "hts0", [128, MJB * TC], BF16),
               sb(l1, "hts1", [128, MJB * TC], BF16)]
        wselB = [sb(l1, "wselB0", [128, 2 * TC], F32),
                 sb(l1, "wselB1", [128, 2 * TC], F32)]
        psP = [ps(l1, "psP0", [128, GRP]), ps(l1, "psP1", [128, GRP])]
        psG = psP[0][:, 0:NEXP]
        psWT = psP[1][0:NEXP, 0:128]
        psO = ps(l1, "psO", [128, NT * TC])
        psH = ps(l1, "psH", [128, 4 * TC])
        psOP = psP[0]
        hfr = sp1[:, 0:TC]

        def psOut(et):
            return psO[:, et * TC:(et + 1) * TC]

        blk_gc = {}
        blk_rec = {}
        unit_rec = {}
        hts_pe = [0, 0]
        psh_ss = [0, 0, 0, 0]
        wselB_free = {0: 0, 1: 0}
        SS_WSELB = {}
        SV_XNB = {}
        SV_XNF1 = {}
        evac_sv_last = 0
        op_pe_last = 0
        b2done = set()

        def ensure_weights(sched, u):
            e, jb, gblki, tc = sched[u]
            if gblki in blk_rec:
                return
            lsem = "lma" if gblki % 2 == 0 else "lmb"
            buf = gblki % 2
            wdma = [("pe", blk_gc[gblki - 2])] if gblki >= 2 else []
            sch.op("g", lambda g, e=e, jb=jb, b=buf: g.dma_start(
                out=w1b[b][:].rearrange("p (dt m) -> p dt m", dt=NT),
                in_=w1_h[e, :, jb * MJB * 128:(jb + 1) * MJB * 128].rearrange(
                    "(dt p) m -> p dt m", p=128)),
                wait=wdma, inc=[(lsem, 16)])
            sch.op("g", lambda g, e=e, jb=jb, b=buf: g.dma_start(
                out=w2b[b][:].rearrange("p (j d) -> p j d", j=MJB),
                in_=w2_h[e, jb * MJB * 128:(jb + 1) * MJB * 128, :].rearrange(
                    "(j p) d -> p j d", p=128)),
                inc=[(lsem, 16)])
            blk_rec[gblki] = (lsem, sch.n(lsem), buf)

        def ensure_wselB(e, gi):
            if (e, gi) in SS_WSELB or e >= NEXP:
                return
            sch.op("sy", lambda sy, e=e, b=e % 2: sy.dma_start(
                out=wselB[b][:],
                in_=wselT_s[e:e + 1, 0:2 * TC]
                    .unsqueeze(1).broadcast_to([1, 128, 2 * TC])),
                wait=[("gc", wselB_free[e % 2]), ("sv", SV_XNB[gi])],
                inc=[("ldx", 16)])
            SS_WSELB[(e, gi)] = sch.n("ldx")

        def h_group(u, r, sched, gi):
            e, jb, gblki, tc = sched[u]
            lsem, LDW, buf = blk_rec[gblki]
            rec = unit_rec.setdefault(u, {"h_pe": {}, "uhb": u % 2})
            to = tc * TC
            for di in range(NT):
                w = ()
                if di == 0:
                    w = [(lsem, LDW), ("ss", psh_ss[r]), ("sv", SV_XNB[gi])]
                lo = (tc % 2) * TC
                sch.op("t", lambda t, b=buf, r=r, di=di, lo=lo: t.matmul(
                    psH[:, r * TC:(r + 1) * TC],
                    w1b[b][:, di * MJB * 128 + r * 128:
                           di * MJB * 128 + (r + 1) * 128],
                    xn1b[:, di * GRP + lo: di * GRP + lo + TC],
                    start=(di == 0), stop=(di == NT - 1)),
                    wait=w, inc=([("pe", 1)] if di == NT - 1 else []))
            rec["h_pe"][r] = sch.n("pe")

        def relu_hts(u, r, sched, gi):
            e, jb, gblki, tc = sched[u]
            rec = unit_rec[u]
            jj = jb * MJB + r
            hb = rec["uhb"]
            wr = [("pe", rec["h_pe"][r]), ("sv", sch.n("sv")),
                  ("gc", sch.n("gc"))]
            nr = unit_rec.get(u + 1)
            if r % 2 == 1 and nr is not None and (r - 1) in nr["h_pe"]:
                wr.append(("pe", nr["h_pe"][r - 1]))
            sch.op("s", lambda s, r=r, e=e, jj=jj: s.activation(
                out=hfr[:], in_=psH[:, r * TC:(r + 1) * TC], func=AF.Relu,
                bias=b1_s[:, e * NMT + jj: e * NMT + jj + 1]),
                wait=wr, inc=[("ss", 1)])
            psh_ss[r] = sch.n("ss")
            wm = [("ss", sch.n("ss")), ("ldx", SS_WSELB[(e, gi)])]
            if r == 0:
                wm.append(("pe", hts_pe[hb]))
            sch.op("g", lambda g, hb=hb, r=r, tc=tc, b=e % 2: g.tensor_mul(
                hts[hb][:, r * TC:(r + 1) * TC], hfr[:],
                wselB[b][:, (tc % 2) * TC:(tc % 2 + 1) * TC]),
                wait=wm, inc=[("gc", 1)])
            rec["hts_gc"] = sch.n("gc")

        def w2_unit(u, sched):
            nonlocal evac_sv_last
            e, jb, gblki, tc = sched[u]
            rec = unit_rec[u]
            lsem, LDW, buf = blk_rec[gblki]
            hb = rec["uhb"]
            to = tc * TC
            carry_b2 = (e == 0 and jb == 0 and tc not in b2done)
            if carry_b2:
                b2done.add(tc)
            for et in range(NT):
                for j in range(MJB):
                    w = []
                    if et == 0 and j == 0:
                        w = [("gc", rec["hts_gc"]), ("sv", evac_sv_last)]
                    last = (j == MJB - 1 and et == NT - 1 and not carry_b2)
                    sch.op("t", lambda t, b=buf, hb=hb, et=et, j=j, cb=carry_b2:
                        t.matmul(
                            psOut(et),
                            w2b[b][:, j * D + et * 128: j * D + (et + 1) * 128],
                            hts[hb][:, j * TC:(j + 1) * TC], start=(j == 0),
                            stop=(j == MJB - 1 and not cb)),
                        wait=w, inc=([("pe", 1)] if last else []))
                if carry_b2:
                    sch.op("t", lambda t, et=et, lo=(tc % 2) * TC: t.matmul(
                        psOut(et), b2_s[:, et * 128:(et + 1) * 128],
                        wselT_bf[0:NEXP, lo:lo + TC], start=False, stop=True),
                        wait=([("lset", LDG_SET)] if et == 0 else ()),
                        inc=([("pe", 1)] if et == NT - 1 else []))
            rec["w2_pe"] = sch.n("pe")
            hts_pe[hb] = rec["w2_pe"]

        def evac_unit(u, sched):
            nonlocal evac_sv_last
            e, jb, gblki, tc = sched[u]
            rec = unit_rec.pop(u)
            to = tc * TC
            sch.op("v", lambda v, to=to: v.tensor_add(
                xT[:].rearrange("p (et s_) -> p et s_", et=NT)[:, :, to:to + TC],
                xT[:].rearrange("p (et s_) -> p et s_", et=NT)[:, :, to:to + TC],
                psO[:].rearrange("p (et c) -> p et c", et=NT)),
                wait=[("pe", rec["w2_pe"])], inc=[("sv", 1)])
            evac_sv_last = sch.n("sv")

        def pair_nop(u, sched):
            e, jb, gblki, tc = sched[u]
            blk_gc[gblki] = unit_rec.get(u, {}).get("w2_pe", sch.n("pe"))

        def slot_end(k, sched, gi, NU, moe_lvl=4):
            if 0 <= k - 1 < NU:
                if moe_lvl >= 3:
                    w2_unit(k - 1, sched)
                if (k - 1) % 2 == 1:
                    pair_nop(k - 1, sched)
            for u2 in (k + 1, k + 2):
                if u2 < NU:
                    ensure_weights(sched, u2)

        def bcast_rows(wait_ss):
            sch.op("sy", lambda sy: sy.dma_start(
                out=mb1[:], in_=mrow[0:1, :].unsqueeze(1)
                    .broadcast_to([1, 128, GRP])),
                wait=[("ss", wait_ss), ("sv", sch.n("sv"))], inc=[("ld", 16)])
            sch.op("sy", lambda sy: sy.dma_start(
                out=rb1[:], in_=rrow[0:1, :].unsqueeze(1)
                    .broadcast_to([1, 128, GRP])), inc=[("ld", 16)])
            return sch.n("ld")

        for gi in range(NGRP):
            co = gi * GRP
            ln_stats(GRP, xT, lambda dt, co=co: dt * S + co, mrow, rrow, EPS0,
                     psP[0][0:1, :], psP[1][0:1, :], sp1, rta, rtb)
            LBC_LN = bcast_rows(sch.n("ss"))
            for dt in range(NT):
                sch.op("v", lambda v, dt=dt, co=co: v.tensor_sub(
                    xnf1[:, dt * GRP:(dt + 1) * GRP],
                    xT[:, dt * S + co: dt * S + co + GRP], mb1[:]),
                    wait=([("ld", LBC_LN)] if dt == 0 else ()))
                sch.op("v", lambda v, dt=dt: v.tensor_mul(
                    xnf1[:, dt * GRP:(dt + 1) * GRP], xnf1[:, dt * GRP:(dt + 1) * GRP],
                    rb1[:]))
                sch.op("v", lambda v, dt=dt: v.tensor_copy(
                    xn1b[:, dt * GRP:(dt + 1) * GRP],
                    xnf1[:, dt * GRP:(dt + 1) * GRP]),
                    wait=([("pe", sch.n("pe"))] if dt == 0 else ()),
                    inc=[("sv", 1)])
            SV_XN1 = sch.n("sv")
            SV_XNF1[gi] = SV_XN1
            for sti in range(GRP // 128):
                for dt in range(NT):
                    w = [("sv", SV_XN1 if sti == 0 else sch.n("sv"))] \
                        if dt == 0 else ()
                    sch.op("t", lambda t, sti=sti, dt=dt: t.matmul(
                        psG, xnf1[:, dt * GRP + sti * 128: dt * GRP + sti * 128 + 128],
                        wgate_s[:, dt * NEXP:(dt + 1) * NEXP],
                        start=(dt == 0), stop=(dt == NT - 1)),
                        wait=w, inc=([("pe", 1)] if dt == NT - 1 else []))
                def vsem(fn, wait=()):
                    sch.op("v", fn, wait=list(wait) + [("sv", sch.n("sv"))],
                           inc=[("sv", 1)])
                vsem(lambda v: v.tensor_add(lgt[:], psG, bgb_s[:]), wait=["pe"])
                vsem(lambda v: v.reduce_max(gmx[:, 0:1], lgt[:], axis=AX.X))
                vsem(lambda v: v.tensor_scalar(
                    out=gex[:], in0=lgt[:], scalar1=gmx[:, 0:1], scalar2=None,
                    op0=ALU.subtract))
                sch.op("s", lambda s: s.activation(out=gex2[:], in_=gex[:],
                                                   func=AF.Exp),
                       wait=[("sv", sch.n("sv"))], inc=[("ss", 1)])
                vsem(lambda v: v.reduce_sum(gmx[:, 1:2], gex2[:], axis=AX.X),
                     wait=[("ss", sch.n("ss"))])
                vsem(lambda v: v.tensor_scalar(
                    out=geq[:], in0=lgt[:], scalar1=gmx[:, 0:1], scalar2=None,
                    op0=ALU.is_ge))
                vsem(lambda v: v.tensor_scalar(
                    out=ginc[:], in0=geq[:], scalar1=-1.0, scalar2=1.0,
                    op0=ALU.mult, op1=ALU.add))
                sch.op("s", lambda s: s.activation(
                    out=gex[:], in_=ones8[:], func=AF.Identity,
                    scale=gmx[:, 1:2]), wait=[("sv", sch.n("sv"))], inc=[("ss", 1)])
                vsem(lambda v: v.reciprocal(gex2[:], gex[:]),
                     wait=[("ss", sch.n("ss"))])
                vsem(lambda v: v.memset(gq[:, 0:2], 1.0))
                vsem(lambda v: v.tensor_tensor_scan(
                    out=gq[:, 2:10], data0=ginc[:], data1=ones8[:], initial=1.0,
                    op0=ALU.mult, op1=ALU.mult))
                vsem(lambda v: v.tensor_mul(ginc[:], geq[:], gq[:, 1:9]))
                vsem(lambda v: v.tensor_mul(wsel[:], ginc[:], gex2[:]))
                sch.op("t", lambda t: t.transpose(psWT, wsel[:], ident[:]),
                       wait=[("sv", sch.n("sv"))], inc=[("pe", 1)])
                sch.op("v", lambda v, sti=sti: v.tensor_copy(
                    wselT_s[:, sti * 128: sti * 128 + 128], psWT),
                    wait=["pe", ("ldx", sch.n("ldx"))])
                sch.op("v", lambda v, sti=sti: v.tensor_copy(
                    wselT_bf[:, sti * 128: sti * 128 + 128], psWT),
                    inc=[("sv", 1)])
            SV_XNB[gi] = sch.n("sv")
            ln_stats(GRP, xnf1, lambda dt: dt * GRP, None, rrow, EPS1,
                     psP[0][0:1, :], psP[1][0:1, :], sp1, rta, rtb)
            sch.op("sy", lambda sy: sy.dma_start(
                out=rb1[:], in_=rrow[0:1, :].unsqueeze(1)
                    .broadcast_to([1, 128, GRP])),
                wait=[("ss", sch.n("ss")), ("sv", sch.n("sv"))], inc=[("ld", 16)])
            LBC_R2 = sch.n("ld")
            for dt in range(NT):
                sch.op("v", lambda v, dt=dt: v.scalar_tensor_tensor(
                    out=xn2f[:, dt * GRP:(dt + 1) * GRP],
                    in0=xnf1[:, dt * GRP:(dt + 1) * GRP],
                    scalar=normw_s[:, dt:dt + 1], in1=rb1[:],
                    op0=ALU.mult, op1=ALU.mult),
                    wait=([("ld", LBC_R2)] if dt == 0 else ()),
                    inc=([("sv", 1)] if dt == NT - 1 else ()))
            SV_XN2 = sch.n("sv")
            pj_pe = []
            for dt in range(NT):
                wcp = [("sv", SV_XN2)] if dt < 2 else [("pe", pj_pe[dt - 2])]
                sch.op("v", lambda v, b=dt % 2, dt=dt: v.tensor_copy(
                    xn2c[b][:], xn2f[:, dt * GRP:(dt + 1) * GRP]), wait=wcp,
                    inc=[("sv", 1)])
                w = [("sv", sch.n("sv"))]
                if dt == 0:
                    w.append(("lset", LDG_SET))
                sch.op("t", lambda t, dt=dt, b=dt % 2: t.matmul(
                    psP[0][:], xprojT_s[:, dt * 192: dt * 192 + 128],
                    xn2c[b][:], start=(dt == 0), stop=(dt == NT - 1)), wait=w)
                sch.op("t", lambda t, dt=dt, b=dt % 2: t.matmul(
                    psP[1][0:64, :], xprojT_s[:, dt * 192 + 128: dt * 192 + 192],
                    xn2c[b][:], start=(dt == 0), stop=(dt == NT - 1)),
                    inc=[("pe", 1)])
                pj_pe.append(sch.n("pe"))
            sch.op("v", lambda v: v.tensor_copy(dlo_bf[:], psP[0][0:64, :]),
                   wait=["pe", ("lbc", sch.n("lbc"))])
            sch.op("v", lambda v: v.tensor_copy(BCrow[64:128, :], psP[0][64:128, :]))
            sch.op("v", lambda v: v.tensor_copy(BCrow[0:64, :], psP[1][0:64, :]),
                   inc=[("sv", 1)])
            SV_PROJ = sch.n("sv")
            for dt in range(NT):
                b = dt % 2
                sch.op("t", lambda t, dt=dt, b=b: t.matmul(
                    psP[b][:], dtprojT_s[:, dt * 128:(dt + 1) * 128], dlo_bf[:],
                    start=True, stop=True),
                    wait=([("sv", SV_PROJ)] if dt <= 1 else [("ss", sch.n("ss") - 1)]),
                    inc=[("pe", 1)])
                sch.op("s", lambda s, dt=dt, b=b: s.activation(
                    out=sp1[:], in_=psP[b][:], func=AF.Abs,
                    bias=dtprojb_s[:, dt:dt + 1]),
                    wait=["pe", ("sv", sch.n("sv"))])
                sch.op("s", lambda s, dt=dt, b=b: s.activation(
                    out=deltaf[:, dt * GRP:(dt + 1) * GRP], in_=psP[b][:],
                    func=AF.Relu, bias=dtprojb_s[:, dt:dt + 1]))
                sch.op("s", lambda s: s.activation(
                    out=sp1[:], in_=sp1[:], func=AF.Exp, scale=-1.0))
                sch.op("s", lambda s: s.activation(
                    out=sp1[:], in_=sp1[:], func=AF.Ln, bias=1.0), inc=[("ss", 1)])
                sch.op("v", lambda v, dt=dt: v.tensor_add(
                    deltaf[:, dt * GRP:(dt + 1) * GRP],
                    deltaf[:, dt * GRP:(dt + 1) * GRP], sp1[:]),
                    wait=[("ss", sch.n("ss"))])
                sch.op("v", lambda v, dt=dt: v.tensor_mul(
                    Uf[:, dt * GRP:(dt + 1) * GRP], deltaf[:, dt * GRP:(dt + 1) * GRP],
                    xn2f[:, dt * GRP:(dt + 1) * GRP]), inc=[("sv", 1)])
            for dt in range(NT):
                sch.op("v", lambda v, dt=dt: v.memset(
                    deltaf[:, dt * GRP:(dt + 1) * GRP].rearrange(
                        "p (m t) -> p m t", m=MG, t=CHUNK)[:, :, 0:1],
                    1e30), inc=([("sv", 1)] if dt == NT - 1 else []))
            SV_U = sch.n("sv")
            if phases < 11:
                break
            moe_on = (phases != 15)
            moe_lvl = {21: 1, 22: 2, 23: 3, 24: 3}.get(phases, 4)
            if phases == 24:
                b2done.update(range(NTC))
            NUCAP = {26: 4, 27: 32, 28: 48, 29: 96, 30: 64, 31: 80}.get(phases, 10 ** 9)
            sched = []
            for e in range(NEXP):
                for jb in range(MNJB):
                    gblki = gi * NEXP * MNJB + e * MNJB + jb
                    sched.append((e, jb, gblki, 2 * gi))
                    sched.append((e, jb, gblki, 2 * gi + 1))
            NU = min(len(sched), NUCAP) if moe_on else 0
            if moe_on:
                ensure_wselB(0, gi)
                ensure_wselB(1, gi)
                ensure_weights(sched, 0)
                if NU > 2:
                    ensure_weights(sched, 2)
            scan_sv = []
            for n in range(STATE):
                nb = n % 2
                wd = [("sv", SV_U)] if n < 2 else \
                     [("sv", scan_sv[(n - 1) * NT - 1])]
                sch.op("sy", lambda sy, n=n, nb=nb: sy.dma_start(
                    out=Bbig[nb][:],
                    in_=BCrow[64 + n:64 + n + 1, :].unsqueeze(1)
                        .broadcast_to([1, 128, GRP])), wait=wd, inc=[("lbc", 16)])
                sch.op("sy", lambda sy, n=n, nb=nb: sy.dma_start(
                    out=Cbig[nb][:],
                    in_=BCrow[n:n + 1, :].unsqueeze(1)
                        .broadcast_to([1, 128, GRP])), inc=[("lbc", 16)])
                LBC_N = sch.n("lbc")
                for dt in range(NT):
                    it = n * NT + dt
                    db = dt % 2
                    k, r = it // 4, it % 4
                    wda = [("sv", scan_sv[it - 2])] if it >= 2 else \
                          [("sv", SV_U)]
                    sch.op("s", lambda s, dt=dt, n=n, db=db: s.activation(
                        out=dAb[db][:], in_=deltaf[:, dt * GRP:(dt + 1) * GRP],
                        func=AF.Exp,
                        scale=aneg_s[:, dt * STATE + n: dt * STATE + n + 1]),
                        wait=wda, inc=[("ss", 1)])
                    SS_DA = sch.n("ss")
                    sch.op("v", lambda v, dt=dt, db=db, nb=nb: v.tensor_mul(
                        wSb[:], Uf[:, dt * GRP:(dt + 1) * GRP], Bbig[nb][:]),
                        wait=[("lbc", LBC_N), ("ss", SS_DA)])
                    sch.op("v", lambda v, db=db: v.tensor_tensor_scan(
                        out=hSb[:], data0=dAb[db][:], data1=wSb[:],
                        initial=0.0, op0=ALU.mult, op1=ALU.add))
                    if n == 0:
                        sch.op("v", lambda v, dt=dt, nb=nb: v.tensor_mul(
                            Yf[:, dt * GRP:(dt + 1) * GRP], hSb[:], Cbig[nb][:]),
                            inc=[("sv", 1)])
                    else:
                        sch.op("v", lambda v, nb=nb: v.tensor_mul(
                            tmpS[:], hSb[:], Cbig[nb][:]))
                        sch.op("v", lambda v, dt=dt: v.tensor_add(
                            Yf[:, dt * GRP:(dt + 1) * GRP],
                            Yf[:, dt * GRP:(dt + 1) * GRP], tmpS[:]),
                            inc=[("sv", 1)])
                    scan_sv.append(sch.n("sv"))
                    if 0 <= k - 1 < NU and moe_lvl >= 2:
                        relu_hts(k - 1, r, sched, gi)
                        if r == 3 and (k - 1) % 16 == 15:
                            e_done = sched[k - 1][0]
                            wselB_free[e_done % 2] = sch.n("gc")
                            ensure_wselB(e_done + 2, gi)
                    if k < NU:
                        h_group(k, r, sched, gi)
                    if r == 1 and 0 <= k - 2 < NU and moe_lvl >= 4:
                        evac_unit(k - 2, sched)
                    if r == 3:
                        slot_end(k, sched, gi, NU, moe_lvl)
            if moe_on and NU == len(sched):
                if moe_lvl >= 2:
                    for r in range(4):
                        relu_hts(NU - 1, r, sched, gi)
                if moe_lvl >= 4:
                    evac_unit(NU - 2, sched)
                if moe_lvl >= 3:
                    w2_unit(NU - 1, sched)
                pair_nop(NU - 1, sched)
                if moe_lvl >= 4:
                    evac_unit(NU - 1, sched)
                wselB_free[0] = sch.n("gc")
                wselB_free[1] = sch.n("gc")
            if gi == 0:
                for nm, buf in (("Yg0", Yf), ("Dg0", deltaf), ("Ug0", Uf)):
                    if TAP(nm, [128, NT * GRP]) is not None:
                        sch.op("sy", lambda sy, nm=nm, b=buf: sy.dma_start(
                            out=tap_h[nm][:], in_=b[:]),
                            wait=[("sv", sch.n("sv")), ("ss", sch.n("ss")),
                                  ("pe", sch.n("pe"))], inc=[("st", 16)])
                if TAP("Wg0", [NEXP, GRP]) is not None:
                    sch.op("sy", lambda sy: sy.dma_start(
                        out=tap_h["Wg0"][:], in_=wselT_s[:]),
                        wait=[("sv", sch.n("sv"))], inc=[("st", 16)])
                if tap_h:
                    sch.barrier("v", [("st", sch.n("st"))])
            for dt in range(NT):
                sch.op("v", lambda v, dt=dt, co=co: v.scalar_tensor_tensor(
                    out=oin_bf[:, dt * GRP:(dt + 1) * GRP],
                    in0=xn1b[:, dt * GRP:(dt + 1) * GRP],
                    scalar=dvec_s[:, dt:dt + 1],
                    in1=Yf[:, dt * GRP:(dt + 1) * GRP], op0=ALU.mult, op1=ALU.add),
                    inc=([("sv", 1)] if dt == NT - 1 else []))
            SV_OIN = sch.n("sv")
            for et in range(NT):
                sch.op("g", lambda g, et=et: g.dma_start(
                    out=opslc[0][:].rearrange("p (dt e) -> p dt e", dt=NT),
                    in_=outprojT_h[:, et * 128:(et + 1) * 128].rearrange(
                        "(dt p) e -> p dt e", p=128)),
                    wait=[("pe", op_pe_last)], inc=[("lop", 16)])
                LDW = sch.n("lop")
                for di in range(NT):
                    w = [("lop", LDW), ("sv", SV_OIN), ("sv", sch.n("sv"))] \
                        if di == 0 else ()
                    sch.op("t", lambda t, di=di: t.matmul(
                        psOP[:], opslc[0][:, di * 128:(di + 1) * 128],
                        oin_bf[:, di * GRP:(di + 1) * GRP],
                        start=(di == 0), stop=(di == NT - 1)),
                        wait=w, inc=([("pe", 1)] if di == NT - 1 else []))
                op_pe_last = sch.n("pe")
                sch.op("v", lambda v, et=et, co=co: v.tensor_add(
                    xT[:, et * S + co: et * S + co + GRP],
                    xT[:, et * S + co: et * S + co + GRP], psOP[:]),
                    wait=["pe"], inc=[("sv", 1)])
            if phases < 12 + gi or phases in (21, 22, 23, 24, 26, 27, 28, 29, 30, 31):
                break
        flush()
        l1.close()
        l1c.close()

        emit_output()

    return nc, h_in, tap_h


def _layout_inputs(inputs):
    f = np.ascontiguousarray
    def pp(v):
        return f(v.reshape(NT, 128).T)
    return {
        "wqT": f(inputs["L0_attn_Wq"].T), "wkT": f(inputs["L0_attn_Wk"].T),
        "wvT": f(inputs["L0_attn_Wv"].T), "woT": f(inputs["L0_attn_Wo"].T),
        "bq": pp(inputs["L0_attn_bq"]), "bk": pp(inputs["L0_attn_bk"]),
        "bv": pp(inputs["L0_attn_bv"]), "bo": pp(inputs["L0_attn_bo"]),
        "wrT": f(inputs["L0_ffn_W_r"].T), "wkkT": f(inputs["L0_ffn_W_k"].T),
        "wvvT": f(inputs["L0_ffn_W_v"].T),
        "mur": pp(inputs["L0_ffn_mu_r"]), "muk": pp(inputs["L0_ffn_mu_k"]),
        "normw": pp(inputs["L1_attn_norm_w"]),
        "xprojT": f(inputs["L1_attn_x_proj_w"].T),
        "dtprojT": f(inputs["L1_attn_dt_proj_w"].T),
        "dtprojb": pp(inputs["L1_attn_dt_proj_b"]),
        "alog": f(inputs["L1_attn_A_log"].reshape(NT, 128, STATE)
                  .transpose(1, 0, 2).reshape(128, NT * STATE)),
        "dvec": pp(inputs["L1_attn_D"]),
        "outprojT": f(inputs["L1_attn_out_proj_w"].T),
        "wgate": f(inputs["L1_ffn_w_gate"].reshape(NT, 128, NEXP)
                   .transpose(1, 0, 2).reshape(128, NT * NEXP)),
        "bgate": f(inputs["L1_ffn_b_gate"].reshape(1, NEXP)),
        "w1": f(inputs["L1_ffn_W1"]),
        "b1": f(inputs["L1_ffn_b1"].reshape(NEXP, NMT, 128)
                .transpose(2, 0, 1).reshape(128, NEXP * NMT)),
        "w2": f(inputs["L1_ffn_W2"]),
        "b2": f(inputs["L1_ffn_b2"]),
    }


def kernel(**inputs):
    nc, h_in, _ = build()
    shared = _layout_inputs(inputs)
    x = inputs["L0_attn_x"]
    in_maps = []
    for b in range(8):
        im = dict(shared)
        im["x"] = np.ascontiguousarray(x[b])
        in_maps.append(im)
    res = run_bass_kernel_spmd(nc, in_maps, core_ids=list(range(8)))
    return np.stack([np.asarray(res.results[i]["y"]) for i in range(8)], axis=0)
